# version 26
# baseline (speedup 1.0000x reference)
import numpy as np
import ml_dtypes
from contextlib import ExitStack

import concourse.bass as bass
import concourse.mybir as mybir
from concourse.bass_utils import run_bass_kernel_spmd

F32 = mybir.dt.float32
BF16 = mybir.dt.bfloat16
I32 = mybir.dt.int32
I16 = mybir.dt.int16
ALU = mybir.AluOpType
AF = mybir.ActivationFunctionType
AX = mybir.AxisListType.X

B, T, E = 2, 4096, 2048
TT = B * T
EC = E // 128
NS = 2048
GELU_A = 0.044715
GELU_S = 2.0 * 0.7978845608028654
INV_SQRT_E = float(E) ** -0.5
NEG_BIG = -1e30
NP_BF16 = ml_dtypes.bfloat16
KB = 1024


class Sem:
    __slots__ = ("h", "n")

    def __init__(self, h):
        self.h = h
        self.n = 0


class Prog:
    def __init__(self, nc, ctx):
        self.nc = nc
        self.ctx = ctx
        self.sched = []

    def sem(self, name):
        return Sem(self.ctx.enter_context(self.nc.semaphore(name)))

    def op(self, eng, fn, inc=None, k=1):
        if inc is not None:
            inc.n += k
            def wrapped(engine, fn=fn, s=inc.h, k=k):
                fn(engine).then_inc(s, k)
            self.sched.append((eng, wrapped))
        else:
            self.sched.append((eng, fn))

    def wait(self, eng, sem, val=None):
        v = sem.n if val is None else val
        assert 0 <= v <= sem.n, f"wait {v} > scheduled {sem.n}"
        if v > 0:
            self.sched.append((eng, lambda engine, s=sem.h, v=v: engine.wait_ge(s, v)))

    def drain(self, eng, inc=None, k=1):
        if inc is not None:
            inc.n += k
            self.sched.append((eng, lambda engine, s=inc.h, k=k:
                               engine.drain(fusable=False).then_inc(s, k)))
        else:
            self.sched.append((eng, lambda engine: engine.drain()))

    def replay(self):
        with self.nc.Block() as blk:
            for name in ("sync", "scalar", "tensor", "vector", "gpsimd"):
                ops = [fn for e, fn in self.sched if e == name]
                def body(engine, ops=ops):
                    for fn in ops:
                        fn(engine)
                getattr(blk, name)(body)


def build_program(stage="full"):
    nc = bass.Bass("TRN2", target_bir_lowering=False)

    xg = nc.dram_tensor("xg", [TT, E], F32, kind="ExternalInput").ap()
    xo = nc.dram_tensor("xo", [16, 64, E], F32, kind="ExternalInput").ap()
    wq_in = nc.dram_tensor("wq_sl", [E, 256], F32, kind="ExternalInput").ap()
    wk_in = nc.dram_tensor("wk_sl", [E, 256], F32, kind="ExternalInput").ap()
    wv_in = nc.dram_tensor("wv_sl", [E, 256], F32, kind="ExternalInput").ap()
    wp_in = nc.dram_tensor("wp_sl", [256, E], F32, kind="ExternalInput").ap()
    bqk_in = nc.dram_tensor("bqk", [128, 4], F32, kind="ExternalInput").ap()
    bvb_in = nc.dram_tensor("bvb", [128, 256], F32, kind="ExternalInput").ap()
    bpb_in = nc.dram_tensor("bpb", [128, E], F32, kind="ExternalInput").ap()
    rope_in = nc.dram_tensor("rope_R", [2, 128, 128], F32, kind="ExternalInput").ap()
    gpk_in = nc.dram_tensor("gate_pk", [NS, 1024], I32, kind="ExternalInput").ap()
    gsc_in = nc.dram_tensor("gate_sc", [NS, 64], F32, kind="ExternalInput").ap()
    upk_in = nc.dram_tensor("up_pk", [NS, 1024], I32, kind="ExternalInput").ap()
    usc_in = nc.dram_tensor("up_sc", [NS, 64], F32, kind="ExternalInput").ap()
    dpk_in = nc.dram_tensor("down_pk", [E, 1024], I32, kind="ExternalInput").ap()
    dsc_in = nc.dram_tensor("down_sc", [E, 64], F32, kind="ExternalInput").ap()
    out = nc.dram_tensor("out_rows", [16, 64, E], F32, kind="ExternalOutput").ap()

    eye_np = np.eye(128, dtype=np.float32).astype(NP_BF16)
    eye_d = nc.inline_tensor(eye_np, name="eye_c").ap()
    ones_d = nc.inline_tensor(np.ones((128, 128), np.float32).astype(NP_BF16), name="ones_c").ap()
    c128_d = nc.inline_tensor(np.full((1, 128), 1 / 128, np.float32).astype(NP_BF16), name="c128_c").ap()
    ones1_d = nc.inline_tensor(np.ones((1, 128), np.float32).astype(NP_BF16), name="ones1_c").ap()
    mv = np.zeros((4, 128, 512), np.float32)
    pp = np.arange(128)[:, None]
    ff = np.arange(512)[None, :]
    for m in range(4):
        mv[m] = np.where(m * 128 + pp <= ff, 0.0, NEG_BIG)
    mask_d = nc.inline_tensor(mv.astype(NP_BF16), name="mask_c").ap()

    xnf = nc.dram_tensor("xnf", [EC, 128, TT], BF16, kind="Internal").ap()
    contrib = nc.dram_tensor("contrib", [TT, E], F32, kind="Internal").ap()
    wgT_d = nc.dram_tensor("wgT_d", [2, EC, 128, 1024], BF16, kind="Internal").ap()
    wuT_d = nc.dram_tensor("wuT_d", [2, EC, 128, 1024], BF16, kind="Internal").ap()
    wdT_d = nc.dram_tensor("wdT_d", [2, 8, 128, E], BF16, kind="Internal").ap()
    red = [nc.dram_tensor(f"red{j}", [16, 4 * E], F32, kind="Internal").ap() for j in range(16)]

    dbg = {}
    if stage in ("ln",):
        dbg["xnf_out"] = nc.dram_tensor("xnf_out", [EC, 128, TT], BF16, kind="ExternalOutput").ap()
    if stage in ("attn",):
        dbg["contrib_out"] = nc.dram_tensor("contrib_out", [TT, E], F32, kind="ExternalOutput").ap()
        for nm in ("q_fm_out", "k_fm_out", "v_tm_out", "o0_out", "o1_out"):
            dbg[nm] = nc.dram_tensor(nm, [128, T], BF16, kind="ExternalOutput").ap()
        dbg["xq_out"] = nc.dram_tensor("xq_out", [128, EC * 512], BF16, kind="ExternalOutput").ap()
    if stage in ("deq",):
        dbg["wgT_out"] = nc.dram_tensor("wgT_out", [2, EC, 128, 1024], BF16, kind="ExternalOutput").ap()
        dbg["wdT_out"] = nc.dram_tensor("wdT_out", [2, 8, 128, E], BF16, kind="ExternalOutput").ap()
    if stage in ("ffn",):
        dbg["contrib_out"] = nc.dram_tensor("contrib_out", [TT, E], F32, kind="ExternalOutput").ap()

    alloc = nc.alloc_sbuf_tensor_at
    BASE = 16512
    TOP = 229376
    _cur = {"A": BASE, "C": BASE + 16 * KB, "E": BASE + 48 * KB,
            "B": BASE + 103 * KB, "D": BASE + 103 * KB,
            "F": BASE + 16 * KB, "G": BASE + 144 * KB,
            "B2": BASE + 119 * KB, "B3": BASE + 103 * KB}
    _lim = {"A": BASE + 16 * KB, "C": BASE + 48 * KB, "E": BASE + 103 * KB,
            "B": BASE + 168 * KB, "D": TOP, "F": BASE + 144 * KB, "G": TOP,
            "B2": BASE + 151 * KB, "B3": BASE + 119 * KB}

    def sb(name, shape, dtype, region, cur=None):
        cur = _cur
        nbytes = int(np.prod(shape[1:])) * mybir.dt.size(dtype)
        o = (cur[region] + 31) & ~31
        cur[region] = o + nbytes
        assert cur[region] <= _lim[region], \
            f"{name}: region {region} overflow {cur[region]} > {_lim[region]}"
        return alloc(name, shape, dtype, offset=o)

    eye_sb = sb("eye_sb", [128, 128], BF16, "A")
    ones_sb = sb("ones_sb", [128, 128], BF16, "A")
    c128_sb = sb("c128_sb", [1, 128], BF16, "A")
    ones1_sb = sb("ones1_sb", [1, 128], BF16, "A")
    mask_sb = sb("mask_sb", [128, 4 * 512], BF16, "A")
    rope_sb = sb("rope_sb", [128, 2 * 128], BF16, "A")
    bqk_sb = sb("bqk_sb", [128, 4], F32, "A")
    bvb_sb = sb("bvb_sb", [128, 256], BF16, "A")
    bpb_sb = sb("bpb_sb", [128, E], F32, "A")

    wq_sb = sb("wq_sb", [128, EC * 256], BF16, "C")
    wk_sb = sb("wk_sb", [128, EC * 256], BF16, "C")
    wv_sb = sb("wv_sb", [128, EC * 256], BF16, "C")
    wp_sb = sb("wp_sb", [128, 2 * E], BF16, "C")

    pk_sb = [sb(f"pk_sb{i}", [128, 1024], I32, "E") for i in range(2)]
    sc_sb = [sb(f"sc_sb{i}", [128, 64], F32, "E") for i in range(2)]
    wT_st = [sb(f"wT_st{i}", [128, 2048], BF16, "E") for i in range(2)]
    dq_b16 = sb("dq_b16", [128, 1024], I16, "E")
    dq_c16 = sb("dq_c16", [128, 2048], I16, "E")
    dq_k6 = sb("dq_k6", [128, 2048], I16, "E")
    dq_t12 = sb("dq_t12", [128, 2048], I16, "E")
    dq_pat = sb("dq_pat", [128, 2048], I16, "E")
    dq_mbf = sb("dq_mbf", [128, 2048], BF16, "E")
    dq_q16 = sb("dq_q16", [128, 2048], I16, "E")
    dq_corr = sb("dq_corr", [128, 2048], BF16, "E")
    dq_w = [sb(f"dq_w{i}", [128, 2048], BF16, "E") for i in range(2)]

    stg_f = sb("stg_f", [128, EC * 256], F32, "B")
    x_sb = [sb(f"x_sb{i}", [128, 4 * E], BF16, "B") for i in range(2)]
    xn_tm = [sb(f"xn_tm{i}", [128, E], BF16, "B") for i in range(2)]
    xnT_st = [sb(f"xnT_st{i}", [128, E], BF16, "B") for i in range(2)]
    stats_sb = sb("stats_sb", [128, 96], F32, "B")
    ssum_sb = sb("ssum_sb", [128, 4], F32, "B")
    ssq_sb = sb("ssq_sb", [128, 4], F32, "B")
    sqdump_sb = sb("sqdump_sb", [128, E], F32, "B3")
    mean_sb = sb("mean_sb", [128, 4], F32, "B")
    mv_sb = sb("mv_sb", [128, 8], F32, "B")
    rstd_sb = sb("rstd_sb", [128, 4], F32, "B")
    tmp1_sb = sb("tmp1_sb", [128, 4], F32, "B")

    xq_buf = [sb(f"xq_buf{i}", [128, EC * 512], BF16, "D") for i in range(2)]
    q_fm = sb("q_fm", [128, T], BF16, "D")
    k_fm = sb("k_fm", [128, T], BF16, "D")
    v_tm = sb("v_tm", [128, T], BF16, "D")
    o_fm = [sb(f"o_fm{i}", [128, T], BF16, "D") for i in range(2)]
    p_slot = sb("p_slot", [128, 4 * 512], BF16, "D")
    qraw_sb = p_slot
    negmax_sb = sb("negmax_sb", [128, 32], BF16, "D")
    negc_sb = sb("negc_sb", [1, 512], BF16, "D")
    cb_all = sb("cb_all", [128, 8 * 512], BF16, "D")
    cbm_sb = [sb(f"cbm_sb{i}", [128, 4 * 512], BF16, "D") for i in range(2)]
    recip_sb = sb("recip_sb", [128, 512], F32, "D")
    pstage = sb("pstage", [128, E], F32, "D")

    wgT_sb = sb("wgT_sb", [128, EC * 1024], BF16, "F")
    wuT_sb = sb("wuT_sb", [128, EC * 1024], BF16, "F")
    wdT_sb = sb("wdT_sb", [128, 8 * E], BF16, "F")
    xf_buf = [sb(f"xf_buf{i}", [128, EC * 512], BF16, "F") for i in range(2)]
    inter_sb = sb("inter_sb", [128, 8 * 512], BF16, "G")
    g_sq = [sb(f"g_sq{i}", [128, 512], F32, "G") for i in range(2)]
    g_d = [sb(f"g_d{i}", [128, 512], F32, "G") for i in range(2)]
    g_e = [sb(f"g_e{i}", [128, 512], F32, "G") for i in range(2)]
    g_sg = [sb(f"g_sg{i}", [128, 512], F32, "G") for i in range(2)]
    g_gs = [sb(f"g_gs{i}", [128, 512], F32, "G") for i in range(2)]
    dstage = [sb(f"dstage{i}", [128, E], F32, "G") for i in range(2)]

    fin_sb = sb("fin_sb", [64, E], F32, "G")
    xo_sb = sb("xo_sb", [64, E], F32, "G")

    ctx = ExitStack()
    with ctx:
        ps = [ctx.enter_context(nc.psum_tensor(f"ps{i}", [128, 512], F32)) for i in range(8)]
        P = Prog(nc, ctx)

        bankmark = {i: (None, 0) for i in range(8)}

        def wait_bank(eng, pb):
            sem, val = bankmark[pb]
            if sem is not None:
                P.wait(eng, sem, val)

        def set_bank(pb, sem):
            bankmark[pb] = (sem, sem.n)

        s_ld = P.sem("s_ld")
        s_x = [P.sem("s_x0"), P.sem("s_x1")]
        s_xw2 = [P.sem("s_xwA"), P.sem("s_xwB")]
        s_xw = P.sem("s_xw")
        s_pk = P.sem("s_pk")
        s_wT = P.sem("s_wT")
        s_pe = P.sem("s_pe")
        s_v = P.sem("s_v")
        s_s = P.sem("s_s")
        s_g = P.sem("s_g")
        s_cc = P.sem("s_cc")

        s_stg = P.sem("s_stg")

        def syncdma(fn):
            P.op("sync", fn, inc=s_ld, k=16)

        syncdma(lambda e: e.dma_start(out=eye_sb[:], in_=eye_d))
        syncdma(lambda e: e.dma_start(out=ones_sb[:], in_=ones_d))
        syncdma(lambda e: e.dma_start(out=c128_sb[:], in_=c128_d))
        syncdma(lambda e: e.dma_start(out=ones1_sb[:], in_=ones1_d))
        syncdma(lambda e: e.dma_start(out=mask_sb.ap().rearrange("p (m f) -> p m f", m=4),
                                      in_=mask_d.rearrange("m p f -> p m f")))
        syncdma(lambda e: e.dma_start(out=bqk_sb[:], in_=bqk_in))
        syncdma(lambda e: e.dma_start(out=bpb_sb[:], in_=bpb_in))
        ld_all = s_ld.n

        P.op("sync", lambda e: e.dma_start(out=stg_f[:, 0:256], in_=bvb_in),
             inc=s_stg, k=16)
        P.op("sync", lambda e: e.dma_start(out=stg_f[:, 256:384], in_=rope_in[0]),
             inc=s_stg, k=16)
        P.op("sync", lambda e: e.dma_start(out=stg_f[:, 384:512], in_=rope_in[1]),
             inc=s_stg, k=16)
        P.wait("vector", s_stg)
        P.op("vector", lambda v: v.tensor_scalar(bvb_sb[:], stg_f[:, 0:256],
                                                 1.0 / 128.0, None, ALU.mult))
        P.op("vector", lambda v: v.tensor_copy(rope_sb[:], stg_f[:, 256:512]), inc=s_v)
        P.wait("sync", s_v)
        for win, wsb in ((wq_in, wq_sb), (wk_in, wk_sb), (wv_in, wv_sb)):
            P.op("sync", lambda e, win=win: e.dma_start(
                out=stg_f.ap().rearrange("p (c d) -> p c d", d=256),
                in_=win.rearrange("(c p) d -> p c d", p=128)), inc=s_stg, k=16)
            P.wait("vector", s_stg)
            P.op("vector", lambda v, wsb=wsb: v.tensor_copy(wsb[:], stg_f[:]), inc=s_v)
            P.wait("sync", s_v)
        P.op("sync", lambda e: e.dma_start(
            out=stg_f.ap()[:, 0:2 * E].rearrange("p (c u) -> p c u", c=2),
            in_=wp_in.rearrange("(c p) u -> p c u", p=128)), inc=s_stg, k=16)
        P.wait("vector", s_stg)
        P.op("vector", lambda v: v.tensor_copy(wp_sb[:], stg_f[:, 0:2 * E]), inc=s_v)
        P.wait("sync", s_v)

        s_pk2 = [P.sem("s_pkA"), P.sem("s_pkB")]
        s_wst = [P.sem("s_wstA"), P.sem("s_wstB")]
        s_p = P.sem("s_p")
        ps7_bf = ps[7].ap().bitcast(BF16)
        deq_rel = {}
        wst_rel = {}
        deq_pe_mark = {}

        def deq_params(widx, half, gi):
            pkin, scin = [(gpk_in, gsc_in), (upk_in, usc_in), (dpk_in, dsc_in)][widx]
            nb = 1024 if widx < 2 else 512
            return pkin, scin, nb

        def emit_deq_load(gidx):
            widx, half, gi = deq_queue[gidx]
            pkin, scin, nb = deq_params(widx, half, gi)
            slot = gidx % 2
            if gidx >= 2:
                se_, sv_ = deq_rel[gidx - 2]
                P.wait("gpsimd", se_, sv_)
            if widx < 2:
                r0 = half * 1024 + gi * 128
                P.op("gpsimd", lambda g, pkin=pkin, r0=r0, slot=slot: g.dma_start(
                    out=pk_sb[slot][:], in_=pkin[r0:r0 + 128, :]), inc=s_pk2[slot], k=16)
                P.op("gpsimd", lambda g, scin=scin, r0=r0, slot=slot: g.dma_start(
                    out=sc_sb[slot][:], in_=scin[r0:r0 + 128, :]), inc=s_pk2[slot], k=16)
            else:
                r0 = gi * 128
                c0 = half * 512
                P.op("gpsimd", lambda g, r0=r0, c0=c0, slot=slot: g.dma_start(
                    out=pk_sb[slot][:, 0:512], in_=dpk_in[r0:r0 + 128, c0:c0 + 512]),
                    inc=s_pk2[slot], k=16)
                P.op("gpsimd", lambda g, r0=r0, c0=c0, slot=slot: g.dma_start(
                    out=sc_sb[slot][:, 0:32], in_=dsc_in[r0:r0 + 128, c0 // 16: c0 // 16 + 32]),
                    inc=s_pk2[slot], k=16)

        def emit_deq_decode(gidx):
            widx, half, gi = deq_queue[gidx]
            _, _, nb = deq_params(widx, half, gi)
            eng = "vector"
            s_e = s_v
            slot = gidx % 2
            ncd = 2 * nb
            nsc = ncd // 32
            P.wait(eng, s_pk2[slot])
            P.drain(eng)
            if gidx >= 2:
                pse_, pv_ = deq_w_free[gidx - 2]
                P.wait(eng, pse_, pv_)
            P.op(eng, lambda v, slot=slot, nb=nb: v.tensor_copy(
                dq_b16[:, 0:nb], pk_sb[slot][:, 0:nb]))
            P.drain(eng)
            P.op(eng, lambda v, nb=nb: v.tensor_scalar(
                dq_c16[:, 0:nb], dq_b16[:, 0:nb], 15, None, ALU.bitwise_and))
            P.op(eng, lambda v, nb=nb: v.tensor_scalar(
                dq_c16[:, nb:2 * nb], dq_b16[:, 0:nb], 4, 15,
                ALU.logical_shift_right, ALU.bitwise_and))
            P.drain(eng)
            P.op(eng, lambda v, ncd=ncd: v.tensor_scalar(
                dq_k6[:, 0:ncd], dq_c16[:, 0:ncd], 7, 6,
                ALU.bitwise_and, ALU.logical_shift_left))
            P.op(eng, lambda v, ncd=ncd: v.tensor_scalar(
                dq_t12[:, 0:ncd], dq_c16[:, 0:ncd], 8, 12,
                ALU.bitwise_and, ALU.logical_shift_left))
            P.drain(eng)
            P.op(eng, lambda v, ncd=ncd: v.tensor_add(
                dq_pat[:, 0:ncd], dq_k6[:, 0:ncd], dq_t12[:, 0:ncd]))
            P.drain(eng)
            P.op(eng, lambda v, ncd=ncd: v.tensor_scalar(
                dq_pat[:, 0:ncd], dq_pat[:, 0:ncd], 0x3F00, None, ALU.add))
            P.drain(eng)
            P.op(eng, lambda v, ncd=ncd: v.tensor_scalar(
                dq_c16[:, 0:ncd], dq_k6[:, 0:ncd], -1, 0x0080, ALU.mult, ALU.add))
            P.drain(eng)
            P.op(eng, lambda v, ncd=ncd: v.tensor_scalar_max(
                dq_mbf[:, 0:ncd], dq_c16[:, 0:ncd], 0))
            P.op(eng, lambda v, ncd=ncd: v.tensor_scalar(
                dq_q16[:, 0:ncd], dq_t12[:, 0:ncd], 0x3B80, None, ALU.add))
            P.drain(eng)
            P.op(eng, lambda v, ncd=ncd: v.tensor_mul(
                dq_corr[:, 0:ncd], dq_mbf[:, 0:ncd],
                dq_q16.ap()[:, 0:ncd].bitcast(BF16)))
            P.drain(eng)
            P.op(eng, lambda v, ncd=ncd: v.tensor_sub(
                dq_mbf[:, 0:ncd], dq_pat.ap()[:, 0:ncd].bitcast(BF16),
                dq_corr[:, 0:ncd]))
            P.drain(eng)
            P.op(eng, lambda v, ncd=ncd, nb=nb, nsc=nsc, slot=slot: v.tensor_tensor(
                dq_w[slot].ap()[:, 0:ncd].rearrange("p (s x h) -> p h s x", h=2, x=16),
                dq_mbf.ap()[:, 0:ncd].rearrange("p (h s x) -> p h s x", h=2, x=16),
                sc_sb[slot].ap()[:, 0:nsc].rearrange("p (o s x) -> p o s x", o=1, x=1)
                    .broadcast_to([128, 2, nsc, 16]),
                ALU.mult), inc=s_e)
            deq_rel[gidx] = (s_e, s_e.n)
            deq_dec[gidx] = (s_e, s_e.n)

        def emit_deq_transpose(gidx):
            widx, half, gi = deq_queue[gidx]
            _, _, nb = deq_params(widx, half, gi)
            slot = gidx % 2
            nt = (2 * nb) // 128
            de_, dv_ = deq_dec[gidx]
            if gidx >= 2:
                P.wait("vector", s_wst[slot], wst_rel[gidx - 2])
            for rnd in range((nt + 7) // 8):
                k8 = min(8, nt - rnd * 8)
                P.wait("tensor", de_, dv_)
                wait_bank("tensor", 7)
                for i in range(k8):
                    ecx = rnd * 8 + i
                    P.op("tensor", lambda t, ecx=ecx, i=i, slot=slot: t.transpose(
                        ps7_bf[:, i * 128:(i + 1) * 128],
                        dq_w[slot].ap()[:, ecx * 128:(ecx + 1) * 128], eye_sb[:]),
                        inc=(s_pe if i == k8 - 1 else None))
                P.wait("vector", s_pe)
                P.op("vector", lambda v, rnd=rnd, k8=k8, slot=slot: v.tensor_copy(
                    wT_st[slot][:, rnd * 1024: rnd * 1024 + k8 * 128],
                    ps7_bf[:, 0:k8 * 128]), inc=s_v)
                set_bank(7, s_v)
            deq_w_free[gidx] = (s_pe, s_pe.n)
            P.wait("gpsimd", s_v, s_v.n)
            wout = [wgT_d, wuT_d, wdT_d][widx]
            P.op("gpsimd", lambda e, wout=wout, half=half, gi=gi, nt=nt, slot=slot:
                 e.dma_start(
                     out=wout[half, 0:nt, :, gi * 128:(gi + 1) * 128]
                         .rearrange("c p f -> p c f"),
                     in_=wT_st[slot].ap()[:, 0:nt * 128]
                         .rearrange("p (c f) -> p c f", f=128)),
                 inc=s_wst[slot], k=16)
            wst_rel[gidx] = s_wst[slot].n

        deq_queue = []
        for half in range(2):
            for widx in range(3):
                for gi in range(8 if widx < 2 else 16):
                    deq_queue.append((widx, half, gi))
        deq_dec = {}
        deq_w_free = {}
        deq_state = {"ld": 0, "dc": 0, "tr": 0}

        def pump_deq(k):
            nq = len(deq_queue)
            for _ in range(k):
                while deq_state["ld"] < min(nq, deq_state["dc"] + 2):
                    emit_deq_load(deq_state["ld"])
                    deq_state["ld"] += 1
                if deq_state["tr"] < deq_state["dc"] - 1:
                    emit_deq_transpose(deq_state["tr"])
                    deq_state["tr"] += 1
                if deq_state["dc"] < deq_state["ld"]:
                    emit_deq_decode(deq_state["dc"])
                    deq_state["dc"] += 1

        def pump_deq_all2():
            nq = len(deq_queue)
            while deq_state["dc"] < nq:
                pump_deq(1)
            while deq_state["tr"] < nq:
                emit_deq_transpose(deq_state["tr"])
                deq_state["tr"] += 1

        P.wait("tensor", s_ld, ld_all)
        P.wait("scalar", s_v)
        sv_ln = {}
        for ch in range(16):
            buf = ch % 2
            t0 = ch * 512
            if ch >= 2:
                P.wait("gpsimd", s_v, sv_ln[ch - 2])
            P.op("gpsimd", lambda e, t0=t0, buf=buf: e.dma_start(
                out=x_sb[buf].ap().rearrange("p (s e) -> p s e", s=4),
                in_=xg[t0:t0 + 512, :].rearrange("(s p) e -> p s e", p=128)),
                inc=s_x[buf], k=16)
            P.wait("vector", s_x[buf])
            P.drain("vector")
            for st_ in range(4):
                xs = x_sb[buf].ap()[:, st_ * E:(st_ + 1) * E]
                for i in range(4):
                    P.op("vector", lambda v, xs=xs, i=i, st_=st_: v.bn_stats(
                        stats_sb[:, st_ * 24 + i * 6: st_ * 24 + (i + 1) * 6],
                        xs[:, i * 512:(i + 1) * 512]))
            P.drain("vector")
            for st_ in range(4):
                P.op("vector", lambda v, st_=st_: v.bn_aggr(
                    mean_sb.ap().rearrange("p (s x) -> p s x", x=1)[:, st_:st_ + 1, 0]
                    if False else
                    mv_sb[:, st_ * 2: st_ * 2 + 2], stats_sb[:, st_ * 24: st_ * 24 + 24]))
            P.drain("vector")
            P.op("vector", lambda v: v.tensor_scalar(
                tmp1_sb[:], mv_sb.ap().rearrange("p (s x) -> p s x", x=2)[:, :, 1:2],
                1e-5, None, ALU.add), inc=s_v)
            P.wait("scalar", s_v)
            P.drain("scalar")
            P.op("scalar", lambda s: s.activation(tmp1_sb[:], tmp1_sb[:], AF.Sqrt),
                 inc=s_s)
            P.wait("vector", s_s)
            P.op("vector", lambda v: v.reciprocal(rstd_sb[:], tmp1_sb[:]))
            P.drain("vector")
            for st_ in range(4):
                xs = x_sb[buf].ap()[:, st_ * E:(st_ + 1) * E]
                tmb = xn_tm[st_ % 2]
                P.wait("vector", s_pe)
                P.op("vector", lambda v, xs=xs, tmb=tmb, st_=st_: v.tensor_scalar(
                    tmb[:], xs, mv_sb.ap()[:, st_ * 2: st_ * 2 + 1],
                    rstd_sb.ap()[:, st_:st_ + 1], ALU.subtract, ALU.mult),
                    inc=s_v)
                if st_ == 3:
                    sv_ln[ch] = s_v.n
                nm_mark = s_v.n
                for rnd in range(2):
                    pv = ps[6].ap().bitcast(BF16)
                    P.wait("tensor", s_v, nm_mark)
                    wait_bank("tensor", 6)
                    for i in range(8):
                        ecx = rnd * 8 + i
                        P.op("tensor", lambda t, tmb=tmb, ecx=ecx, i=i, pv=pv: t.transpose(
                            pv[:, i * 128:(i + 1) * 128],
                            tmb[:, ecx * 128:(ecx + 1) * 128], eye_sb[:]),
                            inc=(s_pe if i == 7 else None))
                    P.wait("scalar", s_pe)
                    if rnd == 0:
                        P.wait("scalar", s_xw2[st_ % 2])
                    P.op("scalar", lambda s, rnd=rnd, st_=st_, pv=pv: s.copy(
                        xnT_st[st_ % 2][:, rnd * 1024:(rnd + 1) * 1024], pv[:]), inc=s_s)
                    set_bank(6, s_s)
                P.op("scalar", lambda e, st_=st_, t0=t0: e.dma_start(
                    out=xnf[0:16, :, t0 + st_ * 128: t0 + st_ * 128 + 128]
                        .rearrange("c p f -> p c f"),
                    in_=xnT_st[st_ % 2].ap().rearrange("p (c f) -> p c f", f=128)),
                    inc=s_xw2[st_ % 2], k=16)
            if ch >= 1:
                pass


        ln_done_xw = (s_xw2[0].n, s_xw2[1].n)

        if stage == "ln":
            P.wait("gpsimd", s_xw2[0], ln_done_xw[0])
            P.wait("gpsimd", s_xw2[1], ln_done_xw[1])
            for ecx in range(16):
                P.op("gpsimd", lambda g, ecx=ecx: g.dma_start(
                    out=dbg["xnf_out"][ecx], in_=xnf[ecx]), inc=s_g, k=16)
            P.wait("gpsimd", s_g)
            P.op("gpsimd", lambda g: g.dma_start(out=out[0], in_=xo[0]), inc=s_g, k=16)
            P.wait("gpsimd", s_g)
            P.replay()
            return nc, dbg

        s_xq = [P.sem("s_xq0"), P.sem("s_xq1")]
        ps5_bf = ps[5].ap().bitcast(BF16)
        ps6_bf = ps[6].ap().bitcast(BF16)
        xq_rel = {}
        P.wait("sync", s_xw2[0], ln_done_xw[0])
        P.wait("sync", s_xw2[1], ln_done_xw[1])

        exp_mark = {0: 0, 1: 0}
        pslot_rel = {}
        s_pst = [P.sem(f"s_pst{u}") for u in range(4)]
        for pair in range(4):
            b_, hl = divmod(pair, 2)
            tb = b_ * T
            for j in range(8):
                slot = j % 2
                use = pair * 8 + j
                if use >= 2:
                    P.wait("sync", s_pe, xq_rel[use - 2])
                for g4 in range(4):
                    P.op("sync", lambda e, g4=g4, slot=slot, tb=tb, j=j: e.dma_start(
                        out=xq_buf[slot].ap().rearrange("p (c f) -> p c f", c=EC)
                            [:, 4 * g4:4 * g4 + 4, :],
                        in_=xnf[4 * g4:4 * g4 + 4, :, tb + j * 512: tb + j * 512 + 512]
                            .rearrange("c p f -> p c f")),
                        inc=s_xq[slot], k=16)
                P.wait("tensor", s_xq[slot])
                xq = xq_buf[slot]
                pq, pk_, pr = (4, 6, 5) if j % 2 == 0 else (2, 3, 1)
                qrw = qraw_sb.ap()[:, 0:512]
                krw = qraw_sb.ap()[:, 512:1024]
                wait_bank("tensor", pq)
                for ec in range(EC):
                    P.op("tensor", lambda t, ec=ec, xq=xq, hl=hl, pq=pq: t.matmul(
                        ps[pq][:], wq_sb.ap()[:, ec * 256 + hl * 128: ec * 256 + hl * 128 + 128],
                        xq.ap()[:, ec * 512:(ec + 1) * 512],
                        start=(ec == 0), stop=(ec == EC - 1)),
                        inc=(s_pe if ec == EC - 1 else None))
                q_done = s_pe.n
                wait_bank("tensor", pk_)
                for ec in range(EC):
                    P.op("tensor", lambda t, ec=ec, xq=xq, hl=hl, pk_=pk_: t.matmul(
                        ps[pk_][:], wk_sb.ap()[:, ec * 256 + hl * 128: ec * 256 + hl * 128 + 128],
                        xq.ap()[:, ec * 512:(ec + 1) * 512],
                        start=(ec == 0), stop=(ec == EC - 1)),
                        inc=(s_pe if ec == EC - 1 else None))
                k_done = s_pe.n
                P.wait("scalar", s_pe, q_done)
                P.op("scalar", lambda s, hl=hl, pq=pq, qrw=qrw: s.add(
                    qrw, ps[pq][:], bqk_sb.ap()[:, hl:hl + 1]), inc=s_s)
                q_evac = s_s.n
                set_bank(pq, s_s)
                P.wait("scalar", s_pe, k_done)
                P.op("scalar", lambda s, hl=hl, pk_=pk_, krw=krw: s.add(
                    krw, ps[pk_][:], bqk_sb.ap()[:, 2 + hl:3 + hl]), inc=s_s)
                k_evac = s_s.n
                set_bank(pk_, s_s)
                wait_bank("tensor", pr)
                P.wait("tensor", s_s, q_evac)
                P.op("tensor", lambda t, hl=hl, pr=pr, qrw=qrw: t.matmul(
                    ps[pr][:], rope_sb.ap()[:, hl * 128:(hl + 1) * 128], qrw,
                    start=True, stop=True), inc=s_pe)
                P.wait("scalar", s_pe)
                P.op("scalar", lambda s, j=j, pr=pr: s.copy(
                    q_fm.ap()[:, j * 512:(j + 1) * 512], ps[pr][:]), inc=s_s)
                set_bank(pr, s_s)
                wait_bank("tensor", pq)
                for ts_ in range(4):
                    for ec in range(EC):
                        P.op("tensor", lambda t, ec=ec, ts_=ts_, xq=xq, hl=hl, pq=pq: t.matmul(
                            ps[pq][:, ts_ * 128:(ts_ + 1) * 128],
                            xq.ap()[:, ec * 512 + ts_ * 128: ec * 512 + ts_ * 128 + 128],
                            wv_sb.ap()[:, ec * 256 + hl * 128: ec * 256 + hl * 128 + 128],
                            start=(ec == 0), stop=False))
                    P.op("tensor", lambda t, ts_=ts_, hl=hl, pq=pq: t.matmul(
                        ps[pq][:, ts_ * 128:(ts_ + 1) * 128], ones_sb[:],
                        bvb_sb.ap()[:, hl * 128:(hl + 1) * 128],
                        start=False, stop=True), inc=(s_pe if ts_ == 3 else None))
                xq_rel[use] = s_pe.n
                v_done = s_pe.n
                wait_bank("tensor", pr)
                P.wait("tensor", s_s, k_evac)
                P.op("tensor", lambda t, hl=hl, pr=pr, krw=krw: t.matmul(
                    ps[pr][:], rope_sb.ap()[:, hl * 128:(hl + 1) * 128], krw,
                    start=True, stop=True), inc=s_pe)
                rk_done = s_pe.n
                P.wait("scalar", s_pe, rk_done)
                P.op("scalar", lambda s, j=j, pr=pr: s.copy(
                    k_fm.ap()[:, j * 512:(j + 1) * 512], ps[pr][:]), inc=s_s)
                set_bank(pr, s_s)
                P.wait("scalar", s_pe, v_done)
                P.op("scalar", lambda s, j=j, pq=pq: s.copy(
                    v_tm.ap()[:, j * 512:(j + 1) * 512], ps[pq][:]), inc=s_s)
                set_bank(pq, s_s)
                pump_deq(1)
            P.wait("tensor", s_s)
            for r8 in range(8):
                pd_ = 4 if r8 % 2 == 0 else 2
                wait_bank("tensor", pd_)
                for m in range(4):
                    qt = 4 * r8 + m
                    P.op("tensor", lambda t, m=m, qt=qt, pd_=pd_: t.matmul(
                        ps[pd_][:, m * 128:(m + 1) * 128],
                        q_fm.ap()[:, qt * 128:(qt + 1) * 128],
                        k_fm.ap()[:, qt * 128:(qt + 1) * 128],
                        start=True, stop=True), inc=(s_pe if m == 3 else None))
                P.wait("vector", s_pe)
                P.op("vector", lambda v, r8=r8, pd_=pd_: v.reduce_max(
                    negmax_sb[:, r8 * 4:(r8 + 1) * 4],
                    ps[pd_].ap().rearrange("p (m f) -> p m f", m=4),
                    axis=AX, negate=True), inc=s_v)
                set_bank(pd_, s_v)
            nmx_done = s_v.n
            for r8 in range(8):
                wait_bank("tensor", 5)
                if r8 == 0:
                    P.wait("tensor", s_v, nmx_done)
                for m in range(4):
                    P.op("tensor", lambda t, r8=r8, m=m: t.transpose(
                        ps5_bf[0:1, m * 128:(m + 1) * 128],
                        negmax_sb[:, r8 * 4 + m: r8 * 4 + m + 1], eye_sb[:]),
                        inc=(s_pe if m == 3 else None))
                P.wait("vector", s_pe)
                P.op("vector", lambda v: v.tensor_copy(negc_sb[:], ps5_bf[0:1, 0:512]),
                     inc=s_v)
                ngc = s_v.n
                set_bank(5, s_v)
                wait_bank("tensor", 1)
                P.wait("tensor", s_v, ngc)
                P.op("tensor", lambda t: t.matmul(ps[1][:], ones1_sb[:], negc_sb[:],
                                                  start=True, stop=True), inc=s_pe)
                P.wait("vector", s_pe)
                P.op("vector", lambda v, r8=r8: v.tensor_copy(
                    cb_all.ap()[:, r8 * 512:(r8 + 1) * 512], ps[1][:]), inc=s_v)
                set_bank(1, s_v)
            cb_done = s_v.n

            def emit_cbm(C2):
                par3 = C2 % 2
                P.wait("gpsimd", s_v, cb_done)
                if C2 >= 2:
                    P.wait("gpsimd", s_pe, cbm_used[C2 - 2])
                P.drain("gpsimd")
                for m in range(4):
                    P.op("gpsimd", lambda v, C2=C2, m=m, par3=par3: v.tensor_add(
                        cbm_sb[par3].ap()[:, m * 512:(m + 1) * 512],
                        cb_all.ap()[:, C2 * 512:(C2 + 1) * 512],
                        mask_sb.ap()[:, m * 512:(m + 1) * 512]),
                        inc=(s_p if m == 3 else None))
                cbm_mark[C2] = s_p.n

            cbm_mark = {}
            cbm_used = {}
            emit_cbm(0)
            wait_bank("tensor", 0)
            wait_bank("tensor", 1)
            for C_ in range(8):
                nst = 4 * C_ + 4
                par3 = C_ % 2
                po_, psm_ = (2, 3) if C_ % 2 == 0 else (4, 5)
                P.wait("tensor", s_p, cbm_mark[C_])
                wait_bank("tensor", po_)
                wait_bank("tensor", psm_)
                def emit_scores(j):
                    par = j % 2
                    diag = j >= 4 * C_
                    if exp_mark[par]:
                        P.wait("tensor", s_s, exp_mark[par])
                    P.op("tensor", lambda t, j=j, par=par, C_=C_: t.matmul(
                        ps[par][:], k_fm.ap()[:, j * 128:(j + 1) * 128],
                        q_fm.ap()[:, C_ * 512:(C_ + 1) * 512],
                        start=True, stop=False))
                    par3_ = C_ % 2
                    if diag:
                        m = j - 4 * C_
                        P.op("tensor", lambda t, par=par, m=m, par3_=par3_: t.matmul(
                            ps[par][:], eye_sb[:], cbm_sb[par3_].ap()[:, m * 512:(m + 1) * 512],
                            start=False, stop=True), inc=s_pe)
                    else:
                        P.op("tensor", lambda t, par=par, C_=C_: t.matmul(
                            ps[par][:], eye_sb[:],
                            cb_all.ap()[:, C_ * 512:(C_ + 1) * 512],
                            start=False, stop=True), inc=s_pe)
                    fold_mark[j] = s_pe.n

                def emit_exp(j):
                    par = j % 2
                    P.wait("scalar", s_pe, fold_mark[j])
                    if j >= 4:
                        P.wait("scalar", s_pe, pslot_rel[j - 4])
                    P.op("scalar", lambda s, j=j, par=par: s.activation(
                        p_slot.ap()[:, (j % 4) * 512:((j % 4) + 1) * 512],
                        ps[par][:], AF.Exp), inc=s_s)
                    exp_mark[par] = s_s.n

                def emit_av(j):
                    P.wait("tensor", s_s, exp_mark[j % 2])
                    P.op("tensor", lambda t, j=j, nst=nst, po_=po_: t.matmul(
                        ps[po_][:], v_tm.ap()[:, j * 128:(j + 1) * 128],
                        p_slot.ap()[:, (j % 4) * 512:((j % 4) + 1) * 512],
                        start=(j == 0), stop=(j == nst - 1)))
                    P.op("tensor", lambda t, j=j, nst=nst, psm_=psm_: t.matmul(
                        ps[psm_][:], ones_sb[:],
                        p_slot.ap()[:, (j % 4) * 512:((j % 4) + 1) * 512],
                        start=(j == 0), stop=(j == nst - 1)), inc=s_pe)
                    pslot_rel[j] = s_pe.n

                fold_mark = {}
                emit_scores(0)
                emit_exp(0)
                for j in range(nst):
                    if j + 1 < nst:
                        emit_scores(j + 1)
                        emit_exp(j + 1)
                    emit_av(j)
                    if j == 1 and C_ < 7:
                        emit_cbm(C_ + 1)
                cbm_used[C_] = s_pe.n
                P.wait("vector", s_pe)
                P.op("vector", lambda v, psm_=psm_: v.reciprocal(recip_sb[:], ps[psm_][:]))
                P.drain("vector")
                P.op("vector", lambda v, hl=hl, C_=C_, po_=po_: v.scalar_tensor_tensor(
                    o_fm[hl].ap()[:, C_ * 512:(C_ + 1) * 512], ps[po_][:],
                    INV_SQRT_E, recip_sb[:], ALU.mult, ALU.mult), inc=s_v)
                set_bank(po_, s_v)
                set_bank(psm_, s_v)
                pump_deq(1)
            if hl == 1:
                for tt_ in range(32):
                    for u in range(4):
                        pb = [4, 5, 6, 7][u]
                        wait_bank("tensor", pb)
                        P.op("tensor", lambda t, tt_=tt_, u=u, pb=pb: t.matmul(
                            ps[pb][:], o_fm[0].ap()[:, tt_ * 128:(tt_ + 1) * 128],
                            wp_sb.ap()[:, u * 512:u * 512 + 512],
                            start=True, stop=False))
                        P.op("tensor", lambda t, tt_=tt_, u=u, pb=pb: t.matmul(
                            ps[pb][:], o_fm[1].ap()[:, tt_ * 128:(tt_ + 1) * 128],
                            wp_sb.ap()[:, E + u * 512: E + u * 512 + 512],
                            start=False, stop=True), inc=s_pe)
                        P.wait("vector", s_pe)
                        P.wait("vector", s_pst[u])
                        P.op("vector", lambda v, u=u, pb=pb: v.tensor_copy(
                            pstage.ap()[:, u * 512:(u + 1) * 512], ps[pb][:]), inc=s_v)
                        set_bank(pb, s_v)
                        P.wait("gpsimd", s_v)
                        P.op("gpsimd", lambda g, tt_=tt_, u=u, tb=tb: g.dma_start(
                            out=contrib[tb + tt_ * 128: tb + tt_ * 128 + 128,
                                        u * 512:(u + 1) * 512],
                            in_=pstage.ap()[:, u * 512:(u + 1) * 512]), inc=s_pst[u], k=16)

        attn_done_g = s_g.n
        attn_done_pst = [s.n for s in s_pst]

        pump_deq_all2()
        deq_done = (s_wst[0].n, s_wst[1].n)

        if stage == "deq":
            P.wait("gpsimd", s_wst[0], deq_done[0])
            P.wait("gpsimd", s_wst[1], deq_done[1])
            P.op("gpsimd", lambda g: g.dma_start(out=dbg["wgT_out"], in_=wgT_d),
                 inc=s_g, k=16)
            P.op("gpsimd", lambda g: g.dma_start(out=dbg["wdT_out"], in_=wdT_d),
                 inc=s_g, k=16)
            P.wait("gpsimd", s_g)
            P.op("gpsimd", lambda g: g.dma_start(out=out[0], in_=xo[0]), inc=s_g, k=16)
            P.wait("gpsimd", s_g)
            P.replay()
            return nc, dbg


        if stage == "attn":
            P.wait("gpsimd", s_g, attn_done_g)
            P.wait("gpsimd", s_v)
            P.op("gpsimd", lambda g: g.dma_start(out=dbg["contrib_out"], in_=contrib),
                 inc=s_g, k=16)
            P.op("gpsimd", lambda g: g.dma_start(out=dbg["xq_out"], in_=xq_buf[0][:]),
                 inc=s_g, k=16)
            for nm, tsr in (("q_fm_out", q_fm), ("k_fm_out", k_fm), ("v_tm_out", v_tm),
                            ("o0_out", o_fm[0]), ("o1_out", o_fm[1])):
                P.op("gpsimd", lambda g, nm=nm, tsr=tsr: g.dma_start(
                    out=dbg[nm], in_=tsr[:]), inc=s_g, k=16)
            P.wait("gpsimd", s_g)
            P.op("gpsimd", lambda g: g.dma_start(out=out[0], in_=xo[0]), inc=s_g, k=16)
            P.wait("gpsimd", s_g)
            P.replay()
            return nc, dbg

        s_xf = [P.sem("s_xfA"), P.sem("s_xfB")]
        s_wl = P.sem("s_wl")
        s_fin = P.sem("s_fin")
        s_out = P.sem("s_out")
        xf_rel = {}

        s_dst = [P.sem("s_dstA"), P.sem("s_dstB")]

        def emit_final(fc):
            P.wait("sync", s_cc, fc + 1)
            if fc > 0:
                P.wait("sync", s_out, 16 * fc)
            P.op("sync", lambda e, fc=fc: e.dma_start(
                out=fin_sb[:], in_=red[fc].rearrange("p (a c) -> (p a) c", a=4)),
                inc=s_fin, k=16)
            P.op("sync", lambda e, fc=fc: e.dma_start(
                out=xo_sb[:], in_=xo[fc]), inc=s_fin, k=16)
            P.wait("gpsimd", s_fin)
            P.drain("gpsimd")
            P.op("gpsimd", lambda g: g.tensor_add(fin_sb[:], fin_sb[:], xo_sb[:]))
            P.drain("gpsimd")
            P.op("gpsimd", lambda g: g.tensor_add(fin_sb[:], fin_sb[:],
                                                  bpb_sb.ap()[0:64, :]))
            P.drain("gpsimd")
            P.op("gpsimd", lambda g, fc=fc: g.dma_start(
                out=out[fc], in_=fin_sb[:]), inc=s_out, k=16)
        for hp in range(2):
            if hp == 0:
                P.wait("gpsimd", s_g, attn_done_g)
                for u in range(4):
                    P.wait("gpsimd", s_pst[u], attn_done_pst[u])
            else:
                P.wait("gpsimd", s_dst[0])
                P.wait("gpsimd", s_dst[1])
            for qn, (wsb_, wd_) in zip(("sync", "scalar", "gpsimd"),
                                       ((wgT_sb, wgT_d), (wuT_sb, wuT_d), (wdT_sb, wdT_d))):
                if hp == 0:
                    P.wait(qn, s_wst[0], deq_done[0])
                    P.wait(qn, s_wst[1], deq_done[1])
                    P.wait(qn, s_pe)
                    P.wait(qn, s_v)
                    P.wait(qn, s_s)
                else:
                    P.wait(qn, s_pe)
                P.op(qn, lambda e, wsb_=wsb_, wd_=wd_, hp=hp: e.dma_start(
                    out=wsb_.ap().rearrange("p (c f) -> p c f",
                                            c=wd_.shape[1]),
                    in_=wd_[hp].rearrange("c p f -> p c f")), inc=s_wl, k=16)
            wl_mark = s_wl.n
            if hp == 0:
                down_done = 0
            for tc in range(16):
                slot = tc % 2
                use = hp * 16 + tc
                if use >= 2:
                    P.wait("sync", s_pe, xf_rel[use - 2])
                for g4 in range(4):
                    P.op("sync", lambda e, g4=g4, slot=slot, tc=tc: e.dma_start(
                        out=xf_buf[slot].ap().rearrange("p (c f) -> p c f", c=EC)
                            [:, 4 * g4:4 * g4 + 4, :],
                        in_=xnf[4 * g4:4 * g4 + 4, :, tc * 512: tc * 512 + 512]
                            .rearrange("c p f -> p c f")), inc=s_xf[slot], k=16)
                P.wait("tensor", s_xf[slot])
                if tc == 0:
                    P.wait("tensor", s_wl, wl_mark)
                xf = xf_buf[slot]
                for nch in range(8):
                    pg, pu = (0, 1) if nch % 2 == 0 else (4, 5)
                    wait_bank("tensor", pg)
                    wait_bank("tensor", pu)
                    for ec in range(EC):
                        P.op("tensor", lambda t, ec=ec, nch=nch, xf=xf, pg=pg: t.matmul(
                            ps[pg][:],
                            wgT_sb.ap()[:, ec * 1024 + nch * 128: ec * 1024 + nch * 128 + 128],
                            xf.ap()[:, ec * 512:(ec + 1) * 512],
                            start=(ec == 0), stop=(ec == EC - 1)),
                            inc=(s_pe if ec == EC - 1 else None))
                    for ec in range(EC):
                        P.op("tensor", lambda t, ec=ec, nch=nch, xf=xf, pu=pu: t.matmul(
                            ps[pu][:],
                            wuT_sb.ap()[:, ec * 1024 + nch * 128: ec * 1024 + nch * 128 + 128],
                            xf.ap()[:, ec * 512:(ec + 1) * 512],
                            start=(ec == 0), stop=(ec == EC - 1)),
                            inc=(s_pe if ec == EC - 1 else None))
                    par2 = nch % 2
                    up_mark = s_pe.n
                    P.wait("scalar", s_pe, s_pe.n - 1)
                    P.op("scalar", lambda s, pg=pg, par2=par2: s.activation(
                        g_sq[par2][:], ps[pg][:], AF.Square), inc=s_s)
                    P.wait("vector", s_s)
                    P.op("vector", lambda v, par2=par2: v.tensor_scalar(
                        g_d[par2][:], g_sq[par2][:], GELU_A, 1.0, ALU.mult, ALU.add))
                    P.drain("vector")
                    P.op("vector", lambda v, pg=pg, par2=par2: v.tensor_mul(
                        g_e[par2][:], g_d[par2][:], ps[pg][:]), inc=s_v)
                    P.wait("scalar", s_v)
                    P.op("scalar", lambda s, par2=par2: s.activation(
                        g_sg[par2][:], g_e[par2][:], AF.Sigmoid, scale=GELU_S), inc=s_s)
                    P.wait("vector", s_s)
                    P.op("vector", lambda v, pg=pg, par2=par2: v.tensor_mul(
                        g_gs[par2][:], ps[pg][:], g_sg[par2][:]))
                    P.drain("vector")
                    P.wait("vector", s_pe, up_mark)
                    if nch == 0:
                        P.wait("vector", s_pe, down_done)
                    P.op("vector", lambda v, pu=pu, nch=nch, par2=par2: v.tensor_mul(
                        inter_sb.ap()[:, nch * 512:(nch + 1) * 512],
                        g_gs[par2][:], ps[pu][:]), inc=s_v)
                    set_bank(pg, s_v)
                    set_bank(pu, s_v)
                inter_ready = s_v.n
                xf_rel[use] = s_pe.n
                P.wait("tensor", s_v, inter_ready)
                for ts_ in range(4):
                    slot2 = (use * 4 + ts_) % 2
                    P.wait("vector", s_dst[slot2])
                    for hc in range(4):
                        pd = [2, 3, 6, 7][hc]
                        wait_bank("tensor", pd)
                        for nch in range(8):
                            P.op("tensor", lambda t, nch=nch, ts_=ts_, hc=hc, pd=pd: t.matmul(
                                ps[pd][:],
                                inter_sb.ap()[:, nch * 512 + ts_ * 128: nch * 512 + ts_ * 128 + 128],
                                wdT_sb.ap()[:, nch * 2048 + hc * 512: nch * 2048 + hc * 512 + 512],
                                start=(nch == 0), stop=(nch == 7)),
                                inc=(s_pe if nch == 7 else None))
                        P.wait("vector", s_pe)
                        P.op("vector", lambda v, hc=hc, pd=pd, slot2=slot2: v.tensor_copy(
                            dstage[slot2].ap()[:, hc * 512:(hc + 1) * 512], ps[pd][:]),
                            inc=s_v)
                        set_bank(pd, s_v)
                    P.wait("gpsimd", s_v)
                    P.op("gpsimd", lambda g, tc=tc, ts_=ts_, slot2=slot2: g.dma_start(
                        out=contrib[tc * 512 + ts_ * 128: tc * 512 + ts_ * 128 + 128, :],
                        in_=dstage[slot2][:], accum_op=ALU.add), inc=s_dst[slot2], k=16)
                down_done = s_pe.n
                if hp == 1 and stage == "full":
                    P.wait("gpsimd", s_dst[0])
                    P.wait("gpsimd", s_dst[1])
                    P.op("gpsimd", lambda g, tc=tc: g.collective_compute(
                        "ReduceScatter", ALU.add,
                        replica_groups=[list(range(8))],
                        ins=[contrib[tc * 512:(tc + 1) * 512, :]
                             .rearrange("(p a) c -> p (a c)", p=128).opt()],
                        outs=[red[tc].opt()]), inc=s_cc, k=1)
                    if tc >= 1:
                        emit_final(tc - 1)

        if stage == "full":
            emit_final(15)

        if stage == "ffn":
            P.wait("gpsimd", s_g)
            P.op("gpsimd", lambda g: g.dma_start(out=dbg["contrib_out"], in_=contrib),
                 inc=s_g, k=16)
            P.wait("gpsimd", s_g)
            P.op("gpsimd", lambda g: g.dma_start(out=out[0], in_=xo[0]), inc=s_g, k=16)
            P.wait("gpsimd", s_g)
        else:
            P.wait("gpsimd", s_out, 16 * 16)

        P.replay()
    return nc, dbg



def make_rope_R(c):
    hd = 128
    freqs = 10000.0 ** (-(np.arange(0, hd, 2, dtype=np.float64) / hd))
    Rs = np.zeros((2, hd, hd), dtype=np.float32)
    for i, h in enumerate((2 * c, 2 * c + 1)):
        ang = h * freqs
        co, si = np.cos(ang), np.sin(ang)
        for j in range(hd // 2):
            Rs[i, 2 * j, 2 * j] = co[j]
            Rs[i, 2 * j + 1, 2 * j] = -si[j]
            Rs[i, 2 * j, 2 * j + 1] = si[j]
            Rs[i, 2 * j + 1, 2 * j + 1] = co[j]
    return Rs


def make_in_maps(L0_attn_x, L0_attn_w_attn, L0_attn_b_attn, L0_attn_w_proj,
                 L0_attn_b_proj, L0_ffn_gate_W_packed, L0_ffn_gate_scales,
                 L0_ffn_up_W_packed, L0_ffn_up_scales, L0_ffn_down_W_packed,
                 L0_ffn_down_scales):
    x = np.ascontiguousarray(np.asarray(L0_attn_x, dtype=np.float32)).reshape(TT, E)
    wa = np.asarray(L0_attn_w_attn, dtype=np.float32)
    ba = np.asarray(L0_attn_b_attn, dtype=np.float32)
    wp = np.asarray(L0_attn_w_proj, dtype=np.float32)
    bp = np.asarray(L0_attn_b_proj, dtype=np.float32)
    gpk = np.asarray(L0_ffn_gate_W_packed, dtype=np.int32)
    gsc = np.asarray(L0_ffn_gate_scales, dtype=np.float32)
    upk = np.asarray(L0_ffn_up_W_packed, dtype=np.int32)
    usc = np.asarray(L0_ffn_up_scales, dtype=np.float32)
    dpk = np.asarray(L0_ffn_down_W_packed, dtype=np.int32)
    dsc = np.asarray(L0_ffn_down_scales, dtype=np.float32)
    x16 = x.reshape(16, 8, 64, E)
    bpb = np.ascontiguousarray(np.broadcast_to(bp[None, :], (128, E)))
    in_maps = []
    for c in range(8):
        h0 = 2 * c * 128
        bqk = np.stack([ba[h0:h0 + 128], ba[h0 + 128:h0 + 256],
                        ba[E + h0:E + h0 + 128], ba[E + h0 + 128:E + h0 + 256]],
                       axis=1)
        bv = ba[2 * E + h0:2 * E + h0 + 256]
        in_maps.append(dict(
            xg=x,
            xo=np.ascontiguousarray(x16[:, c]),
            wq_sl=np.ascontiguousarray(wa[:, h0:h0 + 256]),
            wk_sl=np.ascontiguousarray(wa[:, E + h0:E + h0 + 256]),
            wv_sl=np.ascontiguousarray(wa[:, 2 * E + h0:2 * E + h0 + 256]),
            wp_sl=np.ascontiguousarray(wp[2 * c * 128:2 * c * 128 + 256, :]),
            bqk=np.ascontiguousarray(bqk),
            bvb=np.ascontiguousarray(np.broadcast_to(bv[None, :], (128, 256))),
            bpb=bpb,
            rope_R=make_rope_R(c),
            gate_pk=np.ascontiguousarray(gpk[NS * c:NS * (c + 1)]),
            gate_sc=np.ascontiguousarray(gsc[NS * c:NS * (c + 1)]),
            up_pk=np.ascontiguousarray(upk[NS * c:NS * (c + 1)]),
            up_sc=np.ascontiguousarray(usc[NS * c:NS * (c + 1)]),
            down_pk=np.ascontiguousarray(dpk[:, 1024 * c:1024 * (c + 1)]),
            down_sc=np.ascontiguousarray(dsc[:, 64 * c:64 * (c + 1)]),
        ))
    return in_maps


def assemble_output(results):
    full = np.empty((16, 8, 64, E), dtype=np.float32)
    for c in range(8):
        full[:, c] = np.asarray(results[c]["out_rows"])
    return full.reshape(B, T, E)


def kernel(**inputs):
    nc, _dbg = build_program("full")
    in_maps = make_in_maps(**inputs)
    res = run_bass_kernel_spmd(nc, in_maps, core_ids=list(range(8)))
    return assemble_output(res.results)


# version 27
# speedup vs baseline: 1.0047x; 1.0047x over previous
import numpy as np
import ml_dtypes
from contextlib import ExitStack

import concourse.bass as bass
import concourse.mybir as mybir
from concourse.bass_utils import run_bass_kernel_spmd

F32 = mybir.dt.float32
BF16 = mybir.dt.bfloat16
I32 = mybir.dt.int32
I16 = mybir.dt.int16
ALU = mybir.AluOpType
AF = mybir.ActivationFunctionType
AX = mybir.AxisListType.X

B, T, E = 2, 4096, 2048
TT = B * T
EC = E // 128
NS = 2048
GELU_A = 0.044715
GELU_S = 2.0 * 0.7978845608028654
INV_SQRT_E = float(E) ** -0.5
NEG_BIG = -1e30
NP_BF16 = ml_dtypes.bfloat16
KB = 1024


class Sem:
    __slots__ = ("h", "n")

    def __init__(self, h):
        self.h = h
        self.n = 0


class Prog:
    def __init__(self, nc, ctx):
        self.nc = nc
        self.ctx = ctx
        self.sched = []

    def sem(self, name):
        return Sem(self.ctx.enter_context(self.nc.semaphore(name)))

    def op(self, eng, fn, inc=None, k=1):
        if inc is not None:
            inc.n += k
            def wrapped(engine, fn=fn, s=inc.h, k=k):
                fn(engine).then_inc(s, k)
            self.sched.append((eng, wrapped))
        else:
            self.sched.append((eng, fn))

    def wait(self, eng, sem, val=None):
        v = sem.n if val is None else val
        assert 0 <= v <= sem.n, f"wait {v} > scheduled {sem.n}"
        if v > 0:
            self.sched.append((eng, lambda engine, s=sem.h, v=v: engine.wait_ge(s, v)))

    def drain(self, eng, inc=None, k=1):
        if inc is not None:
            inc.n += k
            self.sched.append((eng, lambda engine, s=inc.h, k=k:
                               engine.drain(fusable=False).then_inc(s, k)))
        else:
            self.sched.append((eng, lambda engine: engine.drain()))

    def replay(self):
        with self.nc.Block() as blk:
            for name in ("sync", "scalar", "tensor", "vector", "gpsimd"):
                ops = [fn for e, fn in self.sched if e == name]
                def body(engine, ops=ops):
                    for fn in ops:
                        fn(engine)
                getattr(blk, name)(body)


def build_program(stage="full"):
    nc = bass.Bass("TRN2", target_bir_lowering=False)

    xg = nc.dram_tensor("xg", [TT, E], F32, kind="ExternalInput").ap()
    xo = nc.dram_tensor("xo", [16, 64, E], F32, kind="ExternalInput").ap()
    wq_in = nc.dram_tensor("wq_sl", [E, 256], F32, kind="ExternalInput").ap()
    wk_in = nc.dram_tensor("wk_sl", [E, 256], F32, kind="ExternalInput").ap()
    wv_in = nc.dram_tensor("wv_sl", [E, 256], F32, kind="ExternalInput").ap()
    wp_in = nc.dram_tensor("wp_sl", [256, E], F32, kind="ExternalInput").ap()
    bqk_in = nc.dram_tensor("bqk", [128, 4], F32, kind="ExternalInput").ap()
    bvb_in = nc.dram_tensor("bvb", [128, 256], F32, kind="ExternalInput").ap()
    bpb_in = nc.dram_tensor("bpb", [128, E], F32, kind="ExternalInput").ap()
    rope_in = nc.dram_tensor("rope_R", [2, 128, 128], F32, kind="ExternalInput").ap()
    gpk_in = nc.dram_tensor("gate_pk", [NS, 1024], I32, kind="ExternalInput").ap()
    gsc_in = nc.dram_tensor("gate_sc", [NS, 64], F32, kind="ExternalInput").ap()
    upk_in = nc.dram_tensor("up_pk", [NS, 1024], I32, kind="ExternalInput").ap()
    usc_in = nc.dram_tensor("up_sc", [NS, 64], F32, kind="ExternalInput").ap()
    dpk_in = nc.dram_tensor("down_pk", [E, 1024], I32, kind="ExternalInput").ap()
    dsc_in = nc.dram_tensor("down_sc", [E, 64], F32, kind="ExternalInput").ap()
    out = nc.dram_tensor("out_rows", [16, 64, E], F32, kind="ExternalOutput").ap()

    eye_np = np.eye(128, dtype=np.float32).astype(NP_BF16)
    eye_d = nc.inline_tensor(eye_np, name="eye_c").ap()
    ones_d = nc.inline_tensor(np.ones((128, 128), np.float32).astype(NP_BF16), name="ones_c").ap()
    c128_d = nc.inline_tensor(np.full((1, 128), 1 / 128, np.float32).astype(NP_BF16), name="c128_c").ap()
    ones1_d = nc.inline_tensor(np.ones((1, 128), np.float32).astype(NP_BF16), name="ones1_c").ap()
    mv = np.zeros((4, 128, 512), np.float32)
    pp = np.arange(128)[:, None]
    ff = np.arange(512)[None, :]
    for m in range(4):
        mv[m] = np.where(m * 128 + pp <= ff, 0.0, NEG_BIG)
    mask_d = nc.inline_tensor(mv.astype(NP_BF16), name="mask_c").ap()

    xnf = nc.dram_tensor("xnf", [EC, 128, TT], BF16, kind="Internal").ap()
    contrib = nc.dram_tensor("contrib", [TT, E], F32, kind="Internal").ap()
    wgT_d = nc.dram_tensor("wgT_d", [2, EC, 128, 1024], BF16, kind="Internal").ap()
    wuT_d = nc.dram_tensor("wuT_d", [2, EC, 128, 1024], BF16, kind="Internal").ap()
    wdT_d = nc.dram_tensor("wdT_d", [2, 8, 128, E], BF16, kind="Internal").ap()
    red = [nc.dram_tensor(f"red{j}", [16, 4 * E], F32, kind="Internal").ap() for j in range(16)]

    dbg = {}
    if stage in ("ln",):
        dbg["xnf_out"] = nc.dram_tensor("xnf_out", [EC, 128, TT], BF16, kind="ExternalOutput").ap()
    if stage in ("attn",):
        dbg["contrib_out"] = nc.dram_tensor("contrib_out", [TT, E], F32, kind="ExternalOutput").ap()
        for nm in ("q_fm_out", "k_fm_out", "v_tm_out", "o0_out", "o1_out"):
            dbg[nm] = nc.dram_tensor(nm, [128, T], BF16, kind="ExternalOutput").ap()
        dbg["xq_out"] = nc.dram_tensor("xq_out", [128, EC * 512], BF16, kind="ExternalOutput").ap()
    if stage in ("deq",):
        dbg["wgT_out"] = nc.dram_tensor("wgT_out", [2, EC, 128, 1024], BF16, kind="ExternalOutput").ap()
        dbg["wdT_out"] = nc.dram_tensor("wdT_out", [2, 8, 128, E], BF16, kind="ExternalOutput").ap()
    if stage in ("ffn",):
        dbg["contrib_out"] = nc.dram_tensor("contrib_out", [TT, E], F32, kind="ExternalOutput").ap()

    alloc = nc.alloc_sbuf_tensor_at
    BASE = 16512
    TOP = 229376
    _cur = {"A": BASE, "C": BASE + 16 * KB, "E": BASE + 48 * KB,
            "B": BASE + 103 * KB, "D": BASE + 103 * KB,
            "F": BASE + 16 * KB, "G": BASE + 144 * KB,
            "B2": BASE + 119 * KB, "B3": BASE + 103 * KB}
    _lim = {"A": BASE + 16 * KB, "C": BASE + 48 * KB, "E": BASE + 103 * KB,
            "B": BASE + 168 * KB, "D": TOP, "F": BASE + 144 * KB, "G": TOP,
            "B2": BASE + 151 * KB, "B3": BASE + 119 * KB}

    def sb(name, shape, dtype, region, cur=None):
        cur = _cur
        nbytes = int(np.prod(shape[1:])) * mybir.dt.size(dtype)
        o = (cur[region] + 31) & ~31
        cur[region] = o + nbytes
        assert cur[region] <= _lim[region], \
            f"{name}: region {region} overflow {cur[region]} > {_lim[region]}"
        return alloc(name, shape, dtype, offset=o)

    eye_sb = sb("eye_sb", [128, 128], BF16, "A")
    ones_sb = sb("ones_sb", [128, 128], BF16, "A")
    c128_sb = sb("c128_sb", [1, 128], BF16, "A")
    ones1_sb = sb("ones1_sb", [1, 128], BF16, "A")
    mask_sb = sb("mask_sb", [128, 4 * 512], BF16, "A")
    rope_sb = sb("rope_sb", [128, 2 * 128], BF16, "A")
    bqk_sb = sb("bqk_sb", [128, 4], F32, "A")
    bvb_sb = sb("bvb_sb", [128, 256], BF16, "A")
    bpb_sb = sb("bpb_sb", [128, E], F32, "A")

    wq_sb = sb("wq_sb", [128, EC * 256], BF16, "C")
    wk_sb = sb("wk_sb", [128, EC * 256], BF16, "C")
    wv_sb = sb("wv_sb", [128, EC * 256], BF16, "C")
    wp_sb = sb("wp_sb", [128, 2 * E], BF16, "C")

    pk_sb = [sb(f"pk_sb{i}", [128, 1024], I32, "E") for i in range(2)]
    sc_sb = [sb(f"sc_sb{i}", [128, 64], F32, "E") for i in range(2)]
    wT_st = [sb(f"wT_st{i}", [128, 2048], BF16, "E") for i in range(2)]
    dq_b16 = sb("dq_b16", [128, 1024], I16, "E")
    dq_c16 = sb("dq_c16", [128, 2048], I16, "E")
    dq_k6 = sb("dq_k6", [128, 2048], I16, "E")
    dq_t12 = sb("dq_t12", [128, 2048], I16, "E")
    dq_pat = sb("dq_pat", [128, 2048], I16, "E")
    dq_mbf = sb("dq_mbf", [128, 2048], BF16, "E")
    dq_q16 = sb("dq_q16", [128, 2048], I16, "E")
    dq_corr = sb("dq_corr", [128, 2048], BF16, "E")
    dq_w = [sb(f"dq_w{i}", [128, 2048], BF16, "E") for i in range(2)]

    stg_f = sb("stg_f", [128, EC * 256], F32, "B")
    x_sb = [sb(f"x_sb{i}", [128, 4 * E], BF16, "B") for i in range(2)]
    xn_tm = [sb(f"xn_tm{i}", [128, E], BF16, "B") for i in range(2)]
    xnT_st = [sb(f"xnT_st{i}", [128, E], BF16, "B") for i in range(2)]
    stats_sb = sb("stats_sb", [128, 96], F32, "B")
    ssum_sb = sb("ssum_sb", [128, 4], F32, "B")
    ssq_sb = sb("ssq_sb", [128, 4], F32, "B")
    sqdump_sb = sb("sqdump_sb", [128, E], F32, "B3")
    mean_sb = sb("mean_sb", [128, 4], F32, "B")
    mv_sb = sb("mv_sb", [128, 8], F32, "B")
    rstd_sb = sb("rstd_sb", [128, 4], F32, "B")
    tmp1_sb = sb("tmp1_sb", [128, 4], F32, "B")

    xq_buf = [sb(f"xq_buf{i}", [128, EC * 512], BF16, "D") for i in range(2)]
    q_fm = sb("q_fm", [128, T], BF16, "D")
    k_fm = sb("k_fm", [128, T], BF16, "D")
    v_tm = sb("v_tm", [128, T], BF16, "D")
    o_fm = [sb(f"o_fm{i}", [128, T], BF16, "D") for i in range(2)]
    p_slot = sb("p_slot", [128, 4 * 512], BF16, "D")
    qraw_sb = p_slot
    negmax_sb = sb("negmax_sb", [128, 32], BF16, "D")
    negc_sb = sb("negc_sb", [1, 512], BF16, "D")
    cb_all = sb("cb_all", [128, 8 * 512], BF16, "D")
    cbm_sb = [sb(f"cbm_sb{i}", [128, 4 * 512], BF16, "D") for i in range(2)]
    recip_sb = sb("recip_sb", [128, 512], F32, "D")
    pstage = sb("pstage", [128, E], F32, "D")

    wgT_sb = sb("wgT_sb", [128, EC * 1024], BF16, "F")
    wuT_sb = sb("wuT_sb", [128, EC * 1024], BF16, "F")
    wdT_sb = sb("wdT_sb", [128, 8 * E], BF16, "F")
    xf_buf = [sb(f"xf_buf{i}", [128, EC * 512], BF16, "F") for i in range(2)]
    inter_sb = sb("inter_sb", [128, 8 * 512], BF16, "G")
    g_sq = [sb(f"g_sq{i}", [128, 512], F32, "G") for i in range(2)]
    g_d = [sb(f"g_d{i}", [128, 512], F32, "G") for i in range(2)]
    g_e = [sb(f"g_e{i}", [128, 512], F32, "G") for i in range(2)]
    g_sg = [sb(f"g_sg{i}", [128, 512], F32, "G") for i in range(2)]
    g_gs = [sb(f"g_gs{i}", [128, 512], F32, "G") for i in range(2)]
    dstage = [sb(f"dstage{i}", [128, E], F32, "G") for i in range(2)]

    fin_sb = sb("fin_sb", [64, E], F32, "G")
    xo_sb = sb("xo_sb", [64, E], F32, "G")

    ctx = ExitStack()
    with ctx:
        ps = [ctx.enter_context(nc.psum_tensor(f"ps{i}", [128, 512], F32)) for i in range(8)]
        P = Prog(nc, ctx)

        bankmark = {i: (None, 0) for i in range(8)}

        def wait_bank(eng, pb):
            sem, val = bankmark[pb]
            if sem is not None:
                P.wait(eng, sem, val)

        def set_bank(pb, sem):
            bankmark[pb] = (sem, sem.n)

        s_ld = P.sem("s_ld")
        s_x = [P.sem("s_x0"), P.sem("s_x1")]
        s_xw2 = [P.sem("s_xwA"), P.sem("s_xwB")]
        s_xw = P.sem("s_xw")
        s_pk = P.sem("s_pk")
        s_wT = P.sem("s_wT")
        s_pe = P.sem("s_pe")
        s_v = P.sem("s_v")
        s_s = P.sem("s_s")
        s_g = P.sem("s_g")
        s_cc = P.sem("s_cc")

        s_stg = P.sem("s_stg")

        def syncdma(fn):
            P.op("sync", fn, inc=s_ld, k=16)

        syncdma(lambda e: e.dma_start(out=eye_sb[:], in_=eye_d))
        syncdma(lambda e: e.dma_start(out=ones_sb[:], in_=ones_d))
        syncdma(lambda e: e.dma_start(out=c128_sb[:], in_=c128_d))
        syncdma(lambda e: e.dma_start(out=ones1_sb[:], in_=ones1_d))
        syncdma(lambda e: e.dma_start(out=mask_sb.ap().rearrange("p (m f) -> p m f", m=4),
                                      in_=mask_d.rearrange("m p f -> p m f")))
        syncdma(lambda e: e.dma_start(out=bqk_sb[:], in_=bqk_in))
        syncdma(lambda e: e.dma_start(out=bpb_sb[:], in_=bpb_in))
        ld_all = s_ld.n

        P.op("sync", lambda e: e.dma_start(out=stg_f[:, 0:256], in_=bvb_in),
             inc=s_stg, k=16)
        P.op("sync", lambda e: e.dma_start(out=stg_f[:, 256:384], in_=rope_in[0]),
             inc=s_stg, k=16)
        P.op("sync", lambda e: e.dma_start(out=stg_f[:, 384:512], in_=rope_in[1]),
             inc=s_stg, k=16)
        P.wait("vector", s_stg)
        P.op("vector", lambda v: v.tensor_scalar(bvb_sb[:], stg_f[:, 0:256],
                                                 1.0 / 128.0, None, ALU.mult))
        P.op("vector", lambda v: v.tensor_copy(rope_sb[:], stg_f[:, 256:512]), inc=s_v)
        P.wait("sync", s_v)
        for win, wsb in ((wq_in, wq_sb), (wk_in, wk_sb), (wv_in, wv_sb)):
            P.op("sync", lambda e, win=win: e.dma_start(
                out=stg_f.ap().rearrange("p (c d) -> p c d", d=256),
                in_=win.rearrange("(c p) d -> p c d", p=128)), inc=s_stg, k=16)
            P.wait("vector", s_stg)
            P.op("vector", lambda v, wsb=wsb: v.tensor_copy(wsb[:], stg_f[:]), inc=s_v)
            P.wait("sync", s_v)
        P.op("sync", lambda e: e.dma_start(
            out=stg_f.ap()[:, 0:2 * E].rearrange("p (c u) -> p c u", c=2),
            in_=wp_in.rearrange("(c p) u -> p c u", p=128)), inc=s_stg, k=16)
        P.wait("vector", s_stg)
        P.op("vector", lambda v: v.tensor_copy(wp_sb[:], stg_f[:, 0:2 * E]), inc=s_v)
        P.wait("sync", s_v)

        s_pk2 = [P.sem("s_pkA"), P.sem("s_pkB")]
        s_wst = [P.sem("s_wstA"), P.sem("s_wstB")]
        s_p = P.sem("s_p")
        ps7_bf = ps[7].ap().bitcast(BF16)
        deq_rel = {}
        wst_rel = {}
        deq_pe_mark = {}

        def deq_params(widx, half, gi):
            pkin, scin = [(gpk_in, gsc_in), (upk_in, usc_in), (dpk_in, dsc_in)][widx]
            nb = 1024 if widx < 2 else 512
            return pkin, scin, nb

        def emit_deq_load(gidx):
            widx, half, gi = deq_queue[gidx]
            pkin, scin, nb = deq_params(widx, half, gi)
            slot = gidx % 2
            if gidx >= 2:
                se_, sv_ = deq_rel[gidx - 2]
                P.wait("gpsimd", se_, sv_)
            if widx < 2:
                r0 = half * 1024 + gi * 128
                P.op("gpsimd", lambda g, pkin=pkin, r0=r0, slot=slot: g.dma_start(
                    out=pk_sb[slot][:], in_=pkin[r0:r0 + 128, :]), inc=s_pk2[slot], k=16)
                P.op("gpsimd", lambda g, scin=scin, r0=r0, slot=slot: g.dma_start(
                    out=sc_sb[slot][:], in_=scin[r0:r0 + 128, :]), inc=s_pk2[slot], k=16)
            else:
                r0 = gi * 128
                c0 = half * 512
                P.op("gpsimd", lambda g, r0=r0, c0=c0, slot=slot: g.dma_start(
                    out=pk_sb[slot][:, 0:512], in_=dpk_in[r0:r0 + 128, c0:c0 + 512]),
                    inc=s_pk2[slot], k=16)
                P.op("gpsimd", lambda g, r0=r0, c0=c0, slot=slot: g.dma_start(
                    out=sc_sb[slot][:, 0:32], in_=dsc_in[r0:r0 + 128, c0 // 16: c0 // 16 + 32]),
                    inc=s_pk2[slot], k=16)

        def emit_deq_decode(gidx):
            widx, half, gi = deq_queue[gidx]
            _, _, nb = deq_params(widx, half, gi)
            eng = "vector"
            s_e = s_v
            slot = gidx % 2
            ncd = 2 * nb
            nsc = ncd // 32
            P.wait(eng, s_pk2[slot])
            P.drain(eng)
            if gidx >= 2:
                pse_, pv_ = deq_w_free[gidx - 2]
                P.wait(eng, pse_, pv_)
            P.op(eng, lambda v, slot=slot, nb=nb: v.tensor_copy(
                dq_b16[:, 0:nb], pk_sb[slot][:, 0:nb]))
            P.drain(eng)
            P.op(eng, lambda v, nb=nb: v.tensor_scalar(
                dq_c16[:, 0:nb], dq_b16[:, 0:nb], 15, None, ALU.bitwise_and))
            P.op(eng, lambda v, nb=nb: v.tensor_scalar(
                dq_c16[:, nb:2 * nb], dq_b16[:, 0:nb], 4, 15,
                ALU.logical_shift_right, ALU.bitwise_and))
            P.drain(eng)
            P.op(eng, lambda v, ncd=ncd: v.tensor_scalar(
                dq_k6[:, 0:ncd], dq_c16[:, 0:ncd], 7, 6,
                ALU.bitwise_and, ALU.logical_shift_left))
            P.op(eng, lambda v, ncd=ncd: v.tensor_scalar(
                dq_t12[:, 0:ncd], dq_c16[:, 0:ncd], 8, 12,
                ALU.bitwise_and, ALU.logical_shift_left))
            P.drain(eng)
            P.op(eng, lambda v, ncd=ncd: v.tensor_add(
                dq_pat[:, 0:ncd], dq_k6[:, 0:ncd], dq_t12[:, 0:ncd]))
            P.drain(eng)
            P.op(eng, lambda v, ncd=ncd: v.tensor_scalar(
                dq_pat[:, 0:ncd], dq_pat[:, 0:ncd], 0x3F00, None, ALU.add))
            P.drain(eng)
            P.op(eng, lambda v, ncd=ncd: v.tensor_scalar(
                dq_c16[:, 0:ncd], dq_k6[:, 0:ncd], -1, 0x0080, ALU.mult, ALU.add))
            P.drain(eng)
            P.op(eng, lambda v, ncd=ncd: v.tensor_scalar_max(
                dq_mbf[:, 0:ncd], dq_c16[:, 0:ncd], 0))
            P.op(eng, lambda v, ncd=ncd: v.tensor_scalar(
                dq_q16[:, 0:ncd], dq_t12[:, 0:ncd], 0x3B80, None, ALU.add))
            P.drain(eng)
            P.op(eng, lambda v, ncd=ncd: v.tensor_mul(
                dq_corr[:, 0:ncd], dq_mbf[:, 0:ncd],
                dq_q16.ap()[:, 0:ncd].bitcast(BF16)))
            P.drain(eng)
            P.op(eng, lambda v, ncd=ncd: v.tensor_sub(
                dq_mbf[:, 0:ncd], dq_pat.ap()[:, 0:ncd].bitcast(BF16),
                dq_corr[:, 0:ncd]))
            P.drain(eng)
            P.op(eng, lambda v, ncd=ncd, nb=nb, nsc=nsc, slot=slot: v.tensor_tensor(
                dq_w[slot].ap()[:, 0:ncd].rearrange("p (s x h) -> p h s x", h=2, x=16),
                dq_mbf.ap()[:, 0:ncd].rearrange("p (h s x) -> p h s x", h=2, x=16),
                sc_sb[slot].ap()[:, 0:nsc].rearrange("p (o s x) -> p o s x", o=1, x=1)
                    .broadcast_to([128, 2, nsc, 16]),
                ALU.mult), inc=s_e)
            deq_rel[gidx] = (s_e, s_e.n)
            deq_dec[gidx] = (s_e, s_e.n)

        def emit_deq_transpose(gidx):
            widx, half, gi = deq_queue[gidx]
            _, _, nb = deq_params(widx, half, gi)
            slot = gidx % 2
            nt = (2 * nb) // 128
            de_, dv_ = deq_dec[gidx]
            if gidx >= 2:
                P.wait("vector", s_wst[slot], wst_rel[gidx - 2])
            for rnd in range((nt + 7) // 8):
                k8 = min(8, nt - rnd * 8)
                P.wait("tensor", de_, dv_)
                wait_bank("tensor", 7)
                for i in range(k8):
                    ecx = rnd * 8 + i
                    P.op("tensor", lambda t, ecx=ecx, i=i, slot=slot: t.transpose(
                        ps7_bf[:, i * 128:(i + 1) * 128],
                        dq_w[slot].ap()[:, ecx * 128:(ecx + 1) * 128], eye_sb[:]),
                        inc=(s_pe if i == k8 - 1 else None))
                P.wait("vector", s_pe)
                P.op("vector", lambda v, rnd=rnd, k8=k8, slot=slot: v.tensor_copy(
                    wT_st[slot][:, rnd * 1024: rnd * 1024 + k8 * 128],
                    ps7_bf[:, 0:k8 * 128]), inc=s_v)
                set_bank(7, s_v)
            deq_w_free[gidx] = (s_pe, s_pe.n)
            P.wait("gpsimd", s_v, s_v.n)
            wout = [wgT_d, wuT_d, wdT_d][widx]
            P.op("gpsimd", lambda e, wout=wout, half=half, gi=gi, nt=nt, slot=slot:
                 e.dma_start(
                     out=wout[half, 0:nt, :, gi * 128:(gi + 1) * 128]
                         .rearrange("c p f -> p c f"),
                     in_=wT_st[slot].ap()[:, 0:nt * 128]
                         .rearrange("p (c f) -> p c f", f=128)),
                 inc=s_wst[slot], k=16)
            wst_rel[gidx] = s_wst[slot].n

        deq_queue = []
        for half in range(2):
            for widx in range(3):
                for gi in range(8 if widx < 2 else 16):
                    deq_queue.append((widx, half, gi))
        deq_dec = {}
        deq_w_free = {}
        deq_state = {"ld": 0, "dc": 0, "tr": 0}

        def pump_deq(k):
            nq = len(deq_queue)
            for _ in range(k):
                while deq_state["ld"] < min(nq, deq_state["dc"] + 2):
                    emit_deq_load(deq_state["ld"])
                    deq_state["ld"] += 1
                if deq_state["tr"] < deq_state["dc"] - 1:
                    emit_deq_transpose(deq_state["tr"])
                    deq_state["tr"] += 1
                if deq_state["dc"] < deq_state["ld"]:
                    emit_deq_decode(deq_state["dc"])
                    deq_state["dc"] += 1

        def pump_deq_all2():
            nq = len(deq_queue)
            while deq_state["dc"] < nq:
                pump_deq(1)
            while deq_state["tr"] < nq:
                emit_deq_transpose(deq_state["tr"])
                deq_state["tr"] += 1

        P.wait("tensor", s_ld, ld_all)
        P.wait("scalar", s_v)
        sv_ln = {}
        for ch in range(16):
            buf = ch % 2
            t0 = ch * 512
            if ch >= 2:
                P.wait("gpsimd", s_v, sv_ln[ch - 2])
            P.op("gpsimd", lambda e, t0=t0, buf=buf: e.dma_start(
                out=x_sb[buf].ap().rearrange("p (s e) -> p s e", s=4),
                in_=xg[t0:t0 + 512, :].rearrange("(s p) e -> p s e", p=128)),
                inc=s_x[buf], k=16)
            P.wait("vector", s_x[buf])
            P.drain("vector")
            for st_ in range(4):
                xs = x_sb[buf].ap()[:, st_ * E:(st_ + 1) * E]
                for i in range(4):
                    P.op("vector", lambda v, xs=xs, i=i, st_=st_: v.bn_stats(
                        stats_sb[:, st_ * 24 + i * 6: st_ * 24 + (i + 1) * 6],
                        xs[:, i * 512:(i + 1) * 512]))
            P.drain("vector")
            for st_ in range(4):
                P.op("vector", lambda v, st_=st_: v.bn_aggr(
                    mean_sb.ap().rearrange("p (s x) -> p s x", x=1)[:, st_:st_ + 1, 0]
                    if False else
                    mv_sb[:, st_ * 2: st_ * 2 + 2], stats_sb[:, st_ * 24: st_ * 24 + 24]))
            P.drain("vector")
            P.op("vector", lambda v: v.tensor_scalar(
                tmp1_sb[:], mv_sb.ap().rearrange("p (s x) -> p s x", x=2)[:, :, 1:2],
                1e-5, None, ALU.add), inc=s_v)
            P.wait("scalar", s_v)
            P.drain("scalar")
            P.op("scalar", lambda s: s.activation(tmp1_sb[:], tmp1_sb[:], AF.Sqrt),
                 inc=s_s)
            P.wait("vector", s_s)
            P.op("vector", lambda v: v.reciprocal(rstd_sb[:], tmp1_sb[:]))
            P.drain("vector")
            for st_ in range(4):
                xs = x_sb[buf].ap()[:, st_ * E:(st_ + 1) * E]
                tmb = xn_tm[st_ % 2]
                P.wait("vector", s_pe)
                P.op("vector", lambda v, xs=xs, tmb=tmb, st_=st_: v.tensor_scalar(
                    tmb[:], xs, mv_sb.ap()[:, st_ * 2: st_ * 2 + 1],
                    rstd_sb.ap()[:, st_:st_ + 1], ALU.subtract, ALU.mult),
                    inc=s_v)
                if st_ == 3:
                    sv_ln[ch] = s_v.n
                nm_mark = s_v.n
                for rnd in range(2):
                    pv = ps[6].ap().bitcast(BF16)
                    P.wait("tensor", s_v, nm_mark)
                    wait_bank("tensor", 6)
                    for i in range(8):
                        ecx = rnd * 8 + i
                        P.op("tensor", lambda t, tmb=tmb, ecx=ecx, i=i, pv=pv: t.transpose(
                            pv[:, i * 128:(i + 1) * 128],
                            tmb[:, ecx * 128:(ecx + 1) * 128], eye_sb[:]),
                            inc=(s_pe if i == 7 else None))
                    P.wait("scalar", s_pe)
                    if rnd == 0:
                        P.wait("scalar", s_xw2[st_ % 2])
                    P.op("scalar", lambda s, rnd=rnd, st_=st_, pv=pv: s.copy(
                        xnT_st[st_ % 2][:, rnd * 1024:(rnd + 1) * 1024], pv[:]), inc=s_s)
                    set_bank(6, s_s)
                P.op("scalar", lambda e, st_=st_, t0=t0: e.dma_start(
                    out=xnf[0:16, :, t0 + st_ * 128: t0 + st_ * 128 + 128]
                        .rearrange("c p f -> p c f"),
                    in_=xnT_st[st_ % 2].ap().rearrange("p (c f) -> p c f", f=128)),
                    inc=s_xw2[st_ % 2], k=16)
            if ch >= 1:
                pass


        ln_done_xw = (s_xw2[0].n, s_xw2[1].n)

        if stage == "ln":
            P.wait("gpsimd", s_xw2[0], ln_done_xw[0])
            P.wait("gpsimd", s_xw2[1], ln_done_xw[1])
            for ecx in range(16):
                P.op("gpsimd", lambda g, ecx=ecx: g.dma_start(
                    out=dbg["xnf_out"][ecx], in_=xnf[ecx]), inc=s_g, k=16)
            P.wait("gpsimd", s_g)
            P.op("gpsimd", lambda g: g.dma_start(out=out[0], in_=xo[0]), inc=s_g, k=16)
            P.wait("gpsimd", s_g)
            P.replay()
            return nc, dbg

        s_xq = [P.sem("s_xq0"), P.sem("s_xq1")]
        ps5_bf = ps[5].ap().bitcast(BF16)
        ps6_bf = ps[6].ap().bitcast(BF16)
        xq_rel = {}
        P.wait("sync", s_xw2[0], ln_done_xw[0])
        P.wait("sync", s_xw2[1], ln_done_xw[1])

        exp_mark = {0: 0, 1: 0}
        pslot_rel = {}
        s_pst = [P.sem(f"s_pst{u}") for u in range(4)]
        s_wl = P.sem("s_wl")
        for pair in range(4):
            b_, hl = divmod(pair, 2)
            tb = b_ * T
            for j in range(8):
                slot = j % 2
                use = pair * 8 + j
                if use >= 2:
                    P.wait("sync", s_pe, xq_rel[use - 2])
                for g4 in range(4):
                    P.op("sync", lambda e, g4=g4, slot=slot, tb=tb, j=j: e.dma_start(
                        out=xq_buf[slot].ap().rearrange("p (c f) -> p c f", c=EC)
                            [:, 4 * g4:4 * g4 + 4, :],
                        in_=xnf[4 * g4:4 * g4 + 4, :, tb + j * 512: tb + j * 512 + 512]
                            .rearrange("c p f -> p c f")),
                        inc=s_xq[slot], k=16)
                P.wait("tensor", s_xq[slot])
                xq = xq_buf[slot]
                pq, pk_, pr = (4, 6, 5) if j % 2 == 0 else (2, 3, 1)
                qrw = qraw_sb.ap()[:, 0:512]
                krw = qraw_sb.ap()[:, 512:1024]
                wait_bank("tensor", pq)
                for ec in range(EC):
                    P.op("tensor", lambda t, ec=ec, xq=xq, hl=hl, pq=pq: t.matmul(
                        ps[pq][:], wq_sb.ap()[:, ec * 256 + hl * 128: ec * 256 + hl * 128 + 128],
                        xq.ap()[:, ec * 512:(ec + 1) * 512],
                        start=(ec == 0), stop=(ec == EC - 1)),
                        inc=(s_pe if ec == EC - 1 else None))
                q_done = s_pe.n
                wait_bank("tensor", pk_)
                for ec in range(EC):
                    P.op("tensor", lambda t, ec=ec, xq=xq, hl=hl, pk_=pk_: t.matmul(
                        ps[pk_][:], wk_sb.ap()[:, ec * 256 + hl * 128: ec * 256 + hl * 128 + 128],
                        xq.ap()[:, ec * 512:(ec + 1) * 512],
                        start=(ec == 0), stop=(ec == EC - 1)),
                        inc=(s_pe if ec == EC - 1 else None))
                k_done = s_pe.n
                P.wait("scalar", s_pe, q_done)
                P.op("scalar", lambda s, hl=hl, pq=pq, qrw=qrw: s.add(
                    qrw, ps[pq][:], bqk_sb.ap()[:, hl:hl + 1]), inc=s_s)
                q_evac = s_s.n
                set_bank(pq, s_s)
                P.wait("scalar", s_pe, k_done)
                P.op("scalar", lambda s, hl=hl, pk_=pk_, krw=krw: s.add(
                    krw, ps[pk_][:], bqk_sb.ap()[:, 2 + hl:3 + hl]), inc=s_s)
                k_evac = s_s.n
                set_bank(pk_, s_s)
                wait_bank("tensor", pr)
                P.wait("tensor", s_s, q_evac)
                P.op("tensor", lambda t, hl=hl, pr=pr, qrw=qrw: t.matmul(
                    ps[pr][:], rope_sb.ap()[:, hl * 128:(hl + 1) * 128], qrw,
                    start=True, stop=True), inc=s_pe)
                P.wait("scalar", s_pe)
                P.op("scalar", lambda s, j=j, pr=pr: s.copy(
                    q_fm.ap()[:, j * 512:(j + 1) * 512], ps[pr][:]), inc=s_s)
                set_bank(pr, s_s)
                wait_bank("tensor", pq)
                for ts_ in range(4):
                    for ec in range(EC):
                        P.op("tensor", lambda t, ec=ec, ts_=ts_, xq=xq, hl=hl, pq=pq: t.matmul(
                            ps[pq][:, ts_ * 128:(ts_ + 1) * 128],
                            xq.ap()[:, ec * 512 + ts_ * 128: ec * 512 + ts_ * 128 + 128],
                            wv_sb.ap()[:, ec * 256 + hl * 128: ec * 256 + hl * 128 + 128],
                            start=(ec == 0), stop=False))
                    P.op("tensor", lambda t, ts_=ts_, hl=hl, pq=pq: t.matmul(
                        ps[pq][:, ts_ * 128:(ts_ + 1) * 128], ones_sb[:],
                        bvb_sb.ap()[:, hl * 128:(hl + 1) * 128],
                        start=False, stop=True), inc=(s_pe if ts_ == 3 else None))
                xq_rel[use] = s_pe.n
                v_done = s_pe.n
                wait_bank("tensor", pr)
                P.wait("tensor", s_s, k_evac)
                P.op("tensor", lambda t, hl=hl, pr=pr, krw=krw: t.matmul(
                    ps[pr][:], rope_sb.ap()[:, hl * 128:(hl + 1) * 128], krw,
                    start=True, stop=True), inc=s_pe)
                rk_done = s_pe.n
                P.wait("scalar", s_pe, rk_done)
                P.op("scalar", lambda s, j=j, pr=pr: s.copy(
                    k_fm.ap()[:, j * 512:(j + 1) * 512], ps[pr][:]), inc=s_s)
                set_bank(pr, s_s)
                P.wait("scalar", s_pe, v_done)
                P.op("scalar", lambda s, j=j, pq=pq: s.copy(
                    v_tm.ap()[:, j * 512:(j + 1) * 512], ps[pq][:]), inc=s_s)
                set_bank(pq, s_s)
                pump_deq(1)
            if pair == 3:
                P.wait("sync", s_pe, s_pe.n)
                P.wait("sync", s_wst[0], wst_rel[6])
                P.wait("sync", s_wst[1], wst_rel[7])
                P.op("sync", lambda e: e.dma_start(
                    out=wgT_sb.ap()[:, 0:12 * 1024]
                        .rearrange("p (c f) -> p c f", c=12),
                    in_=wgT_d[0, 0:12].rearrange("c p f -> p c f")),
                    inc=s_wl, k=16)
            P.wait("tensor", s_s)
            for r8 in range(8):
                pd_ = 4 if r8 % 2 == 0 else 2
                wait_bank("tensor", pd_)
                for m in range(4):
                    qt = 4 * r8 + m
                    P.op("tensor", lambda t, m=m, qt=qt, pd_=pd_: t.matmul(
                        ps[pd_][:, m * 128:(m + 1) * 128],
                        q_fm.ap()[:, qt * 128:(qt + 1) * 128],
                        k_fm.ap()[:, qt * 128:(qt + 1) * 128],
                        start=True, stop=True), inc=(s_pe if m == 3 else None))
                P.wait("vector", s_pe)
                P.op("vector", lambda v, r8=r8, pd_=pd_: v.reduce_max(
                    negmax_sb[:, r8 * 4:(r8 + 1) * 4],
                    ps[pd_].ap().rearrange("p (m f) -> p m f", m=4),
                    axis=AX, negate=True), inc=s_v)
                set_bank(pd_, s_v)
            nmx_done = s_v.n
            for r8 in range(8):
                wait_bank("tensor", 5)
                if r8 == 0:
                    P.wait("tensor", s_v, nmx_done)
                for m in range(4):
                    P.op("tensor", lambda t, r8=r8, m=m: t.transpose(
                        ps5_bf[0:1, m * 128:(m + 1) * 128],
                        negmax_sb[:, r8 * 4 + m: r8 * 4 + m + 1], eye_sb[:]),
                        inc=(s_pe if m == 3 else None))
                P.wait("vector", s_pe)
                P.op("vector", lambda v: v.tensor_copy(negc_sb[:], ps5_bf[0:1, 0:512]),
                     inc=s_v)
                ngc = s_v.n
                set_bank(5, s_v)
                wait_bank("tensor", 1)
                P.wait("tensor", s_v, ngc)
                P.op("tensor", lambda t: t.matmul(ps[1][:], ones1_sb[:], negc_sb[:],
                                                  start=True, stop=True), inc=s_pe)
                P.wait("vector", s_pe)
                P.op("vector", lambda v, r8=r8: v.tensor_copy(
                    cb_all.ap()[:, r8 * 512:(r8 + 1) * 512], ps[1][:]), inc=s_v)
                set_bank(1, s_v)
            cb_done = s_v.n

            def emit_cbm(C2):
                par3 = C2 % 2
                P.wait("gpsimd", s_v, cb_done)
                if C2 >= 2:
                    P.wait("gpsimd", s_pe, cbm_used[C2 - 2])
                P.drain("gpsimd")
                for m in range(4):
                    P.op("gpsimd", lambda v, C2=C2, m=m, par3=par3: v.tensor_add(
                        cbm_sb[par3].ap()[:, m * 512:(m + 1) * 512],
                        cb_all.ap()[:, C2 * 512:(C2 + 1) * 512],
                        mask_sb.ap()[:, m * 512:(m + 1) * 512]),
                        inc=(s_p if m == 3 else None))
                cbm_mark[C2] = s_p.n

            cbm_mark = {}
            cbm_used = {}
            emit_cbm(0)
            wait_bank("tensor", 0)
            wait_bank("tensor", 1)
            for C_ in range(8):
                nst = 4 * C_ + 4
                par3 = C_ % 2
                po_, psm_ = (2, 3) if C_ % 2 == 0 else (4, 5)
                P.wait("tensor", s_p, cbm_mark[C_])
                wait_bank("tensor", po_)
                wait_bank("tensor", psm_)
                def emit_scores(j):
                    par = j % 2
                    diag = j >= 4 * C_
                    if exp_mark[par]:
                        P.wait("tensor", s_s, exp_mark[par])
                    P.op("tensor", lambda t, j=j, par=par, C_=C_: t.matmul(
                        ps[par][:], k_fm.ap()[:, j * 128:(j + 1) * 128],
                        q_fm.ap()[:, C_ * 512:(C_ + 1) * 512],
                        start=True, stop=False))
                    par3_ = C_ % 2
                    if diag:
                        m = j - 4 * C_
                        P.op("tensor", lambda t, par=par, m=m, par3_=par3_: t.matmul(
                            ps[par][:], eye_sb[:], cbm_sb[par3_].ap()[:, m * 512:(m + 1) * 512],
                            start=False, stop=True), inc=s_pe)
                    else:
                        P.op("tensor", lambda t, par=par, C_=C_: t.matmul(
                            ps[par][:], eye_sb[:],
                            cb_all.ap()[:, C_ * 512:(C_ + 1) * 512],
                            start=False, stop=True), inc=s_pe)
                    fold_mark[j] = s_pe.n

                def emit_exp(j):
                    par = j % 2
                    P.wait("scalar", s_pe, fold_mark[j])
                    if j >= 4:
                        P.wait("scalar", s_pe, pslot_rel[j - 4])
                    P.op("scalar", lambda s, j=j, par=par: s.activation(
                        p_slot.ap()[:, (j % 4) * 512:((j % 4) + 1) * 512],
                        ps[par][:], AF.Exp), inc=s_s)
                    exp_mark[par] = s_s.n

                def emit_av(j):
                    P.wait("tensor", s_s, exp_mark[j % 2])
                    P.op("tensor", lambda t, j=j, nst=nst, po_=po_: t.matmul(
                        ps[po_][:], v_tm.ap()[:, j * 128:(j + 1) * 128],
                        p_slot.ap()[:, (j % 4) * 512:((j % 4) + 1) * 512],
                        start=(j == 0), stop=(j == nst - 1)))
                    P.op("tensor", lambda t, j=j, nst=nst, psm_=psm_: t.matmul(
                        ps[psm_][:], ones_sb[:],
                        p_slot.ap()[:, (j % 4) * 512:((j % 4) + 1) * 512],
                        start=(j == 0), stop=(j == nst - 1)), inc=s_pe)
                    pslot_rel[j] = s_pe.n

                fold_mark = {}
                emit_scores(0)
                emit_exp(0)
                for j in range(nst):
                    if j + 1 < nst:
                        emit_scores(j + 1)
                        emit_exp(j + 1)
                    emit_av(j)
                    if j == 1 and C_ < 7:
                        emit_cbm(C_ + 1)
                cbm_used[C_] = s_pe.n
                P.wait("vector", s_pe)
                P.op("vector", lambda v, psm_=psm_: v.reciprocal(recip_sb[:], ps[psm_][:]))
                P.drain("vector")
                P.op("vector", lambda v, hl=hl, C_=C_, po_=po_: v.scalar_tensor_tensor(
                    o_fm[hl].ap()[:, C_ * 512:(C_ + 1) * 512], ps[po_][:],
                    INV_SQRT_E, recip_sb[:], ALU.mult, ALU.mult), inc=s_v)
                set_bank(po_, s_v)
                set_bank(psm_, s_v)
                pump_deq(1)
            if hl == 1:
                for tt_ in range(32):
                    for u in range(4):
                        pb = [4, 5, 6, 7][u]
                        wait_bank("tensor", pb)
                        P.op("tensor", lambda t, tt_=tt_, u=u, pb=pb: t.matmul(
                            ps[pb][:], o_fm[0].ap()[:, tt_ * 128:(tt_ + 1) * 128],
                            wp_sb.ap()[:, u * 512:u * 512 + 512],
                            start=True, stop=False))
                        P.op("tensor", lambda t, tt_=tt_, u=u, pb=pb: t.matmul(
                            ps[pb][:], o_fm[1].ap()[:, tt_ * 128:(tt_ + 1) * 128],
                            wp_sb.ap()[:, E + u * 512: E + u * 512 + 512],
                            start=False, stop=True), inc=s_pe)
                        P.wait("vector", s_pe)
                        P.wait("vector", s_pst[u])
                        P.op("vector", lambda v, u=u, pb=pb: v.tensor_copy(
                            pstage.ap()[:, u * 512:(u + 1) * 512], ps[pb][:]), inc=s_v)
                        set_bank(pb, s_v)
                        P.wait("gpsimd", s_v)
                        P.op("gpsimd", lambda g, tt_=tt_, u=u, tb=tb: g.dma_start(
                            out=contrib[tb + tt_ * 128: tb + tt_ * 128 + 128,
                                        u * 512:(u + 1) * 512],
                            in_=pstage.ap()[:, u * 512:(u + 1) * 512]), inc=s_pst[u], k=16)

        attn_done_g = s_g.n
        attn_done_pst = [s.n for s in s_pst]

        pump_deq_all2()
        deq_done = (s_wst[0].n, s_wst[1].n)

        if stage == "deq":
            P.wait("gpsimd", s_wst[0], deq_done[0])
            P.wait("gpsimd", s_wst[1], deq_done[1])
            P.op("gpsimd", lambda g: g.dma_start(out=dbg["wgT_out"], in_=wgT_d),
                 inc=s_g, k=16)
            P.op("gpsimd", lambda g: g.dma_start(out=dbg["wdT_out"], in_=wdT_d),
                 inc=s_g, k=16)
            P.wait("gpsimd", s_g)
            P.op("gpsimd", lambda g: g.dma_start(out=out[0], in_=xo[0]), inc=s_g, k=16)
            P.wait("gpsimd", s_g)
            P.replay()
            return nc, dbg


        if stage == "attn":
            P.wait("gpsimd", s_g, attn_done_g)
            P.wait("gpsimd", s_v)
            P.op("gpsimd", lambda g: g.dma_start(out=dbg["contrib_out"], in_=contrib),
                 inc=s_g, k=16)
            P.op("gpsimd", lambda g: g.dma_start(out=dbg["xq_out"], in_=xq_buf[0][:]),
                 inc=s_g, k=16)
            for nm, tsr in (("q_fm_out", q_fm), ("k_fm_out", k_fm), ("v_tm_out", v_tm),
                            ("o0_out", o_fm[0]), ("o1_out", o_fm[1])):
                P.op("gpsimd", lambda g, nm=nm, tsr=tsr: g.dma_start(
                    out=dbg[nm], in_=tsr[:]), inc=s_g, k=16)
            P.wait("gpsimd", s_g)
            P.op("gpsimd", lambda g: g.dma_start(out=out[0], in_=xo[0]), inc=s_g, k=16)
            P.wait("gpsimd", s_g)
            P.replay()
            return nc, dbg

        s_xf = [P.sem("s_xfA"), P.sem("s_xfB")]
        s_fin = P.sem("s_fin")
        s_out = P.sem("s_out")
        xf_rel = {}

        s_dst = [P.sem("s_dstA"), P.sem("s_dstB")]

        def emit_final(fc):
            P.wait("sync", s_cc, fc + 1)
            if fc > 0:
                P.wait("sync", s_out, 16 * fc)
            P.op("sync", lambda e, fc=fc: e.dma_start(
                out=fin_sb[:], in_=red[fc].rearrange("p (a c) -> (p a) c", a=4)),
                inc=s_fin, k=16)
            P.op("sync", lambda e, fc=fc: e.dma_start(
                out=xo_sb[:], in_=xo[fc]), inc=s_fin, k=16)
            P.wait("gpsimd", s_fin)
            P.drain("gpsimd")
            P.op("gpsimd", lambda g: g.tensor_add(fin_sb[:], fin_sb[:], xo_sb[:]))
            P.drain("gpsimd")
            P.op("gpsimd", lambda g: g.tensor_add(fin_sb[:], fin_sb[:],
                                                  bpb_sb.ap()[0:64, :]))
            P.drain("gpsimd")
            P.op("gpsimd", lambda g, fc=fc: g.dma_start(
                out=out[fc], in_=fin_sb[:]), inc=s_out, k=16)
        for hp in range(2):
            if hp == 0:
                P.wait("gpsimd", s_g, attn_done_g)
                for u in range(4):
                    P.wait("gpsimd", s_pst[u], attn_done_pst[u])
            else:
                P.wait("gpsimd", s_dst[0])
                P.wait("gpsimd", s_dst[1])
            for qn, (wsb_, wd_) in zip(("sync", "scalar", "gpsimd"),
                                       ((wgT_sb, wgT_d), (wuT_sb, wuT_d), (wdT_sb, wdT_d))):
                if hp == 0:
                    P.wait(qn, s_wst[0], deq_done[0])
                    P.wait(qn, s_wst[1], deq_done[1])
                    P.wait(qn, s_pe)
                    P.wait(qn, s_v)
                    P.wait(qn, s_s)
                elif wsb_ is wdT_sb:
                    P.wait(qn, s_pe)
                else:
                    P.wait(qn, s_pe, gateup_dead)
                if hp == 0 and wsb_ is wgT_sb:
                    P.op(qn, lambda e: e.dma_start(
                        out=wgT_sb.ap()[:, 12 * 1024:16 * 1024]
                            .rearrange("p (c f) -> p c f", c=4),
                        in_=wgT_d[0, 12:16].rearrange("c p f -> p c f")),
                        inc=s_wl, k=16)
                else:
                    P.op(qn, lambda e, wsb_=wsb_, wd_=wd_, hp=hp: e.dma_start(
                        out=wsb_.ap().rearrange("p (c f) -> p c f",
                                                c=wd_.shape[1]),
                        in_=wd_[hp].rearrange("c p f -> p c f")), inc=s_wl, k=16)
            wl_mark = s_wl.n
            if hp == 0:
                down_done = 0
            for tc in range(16):
                slot = tc % 2
                use = hp * 16 + tc
                if use >= 2:
                    P.wait("sync", s_pe, xf_rel[use - 2])
                for g4 in range(4):
                    P.op("sync", lambda e, g4=g4, slot=slot, tc=tc: e.dma_start(
                        out=xf_buf[slot].ap().rearrange("p (c f) -> p c f", c=EC)
                            [:, 4 * g4:4 * g4 + 4, :],
                        in_=xnf[4 * g4:4 * g4 + 4, :, tc * 512: tc * 512 + 512]
                            .rearrange("c p f -> p c f")), inc=s_xf[slot], k=16)
                P.wait("tensor", s_xf[slot])
                if tc == 0:
                    P.wait("tensor", s_wl, wl_mark)
                xf = xf_buf[slot]
                for nch in range(8):
                    pg, pu = (0, 1) if nch % 2 == 0 else (4, 5)
                    wait_bank("tensor", pg)
                    wait_bank("tensor", pu)
                    for ec in range(EC):
                        P.op("tensor", lambda t, ec=ec, nch=nch, xf=xf, pg=pg: t.matmul(
                            ps[pg][:],
                            wgT_sb.ap()[:, ec * 1024 + nch * 128: ec * 1024 + nch * 128 + 128],
                            xf.ap()[:, ec * 512:(ec + 1) * 512],
                            start=(ec == 0), stop=(ec == EC - 1)),
                            inc=(s_pe if ec == EC - 1 else None))
                    for ec in range(EC):
                        P.op("tensor", lambda t, ec=ec, nch=nch, xf=xf, pu=pu: t.matmul(
                            ps[pu][:],
                            wuT_sb.ap()[:, ec * 1024 + nch * 128: ec * 1024 + nch * 128 + 128],
                            xf.ap()[:, ec * 512:(ec + 1) * 512],
                            start=(ec == 0), stop=(ec == EC - 1)),
                            inc=(s_pe if ec == EC - 1 else None))
                    par2 = nch % 2
                    up_mark = s_pe.n
                    if hp == 0 and tc == 15 and nch == 7:
                        gateup_dead = s_pe.n
                    P.wait("scalar", s_pe, s_pe.n - 1)
                    P.op("scalar", lambda s, pg=pg, par2=par2: s.activation(
                        g_sq[par2][:], ps[pg][:], AF.Square), inc=s_s)
                    P.wait("vector", s_s)
                    P.op("vector", lambda v, par2=par2: v.tensor_scalar(
                        g_d[par2][:], g_sq[par2][:], GELU_A, 1.0, ALU.mult, ALU.add))
                    P.drain("vector")
                    P.op("vector", lambda v, pg=pg, par2=par2: v.tensor_mul(
                        g_e[par2][:], g_d[par2][:], ps[pg][:]), inc=s_v)
                    P.wait("scalar", s_v)
                    P.op("scalar", lambda s, par2=par2: s.activation(
                        g_sg[par2][:], g_e[par2][:], AF.Sigmoid, scale=GELU_S), inc=s_s)
                    P.wait("vector", s_s)
                    P.op("vector", lambda v, pg=pg, par2=par2: v.tensor_mul(
                        g_gs[par2][:], ps[pg][:], g_sg[par2][:]))
                    P.drain("vector")
                    P.wait("vector", s_pe, up_mark)
                    if nch == 0:
                        P.wait("vector", s_pe, down_done)
                    P.op("vector", lambda v, pu=pu, nch=nch, par2=par2: v.tensor_mul(
                        inter_sb.ap()[:, nch * 512:(nch + 1) * 512],
                        g_gs[par2][:], ps[pu][:]), inc=s_v)
                    set_bank(pg, s_v)
                    set_bank(pu, s_v)
                inter_ready = s_v.n
                xf_rel[use] = s_pe.n
                P.wait("tensor", s_v, inter_ready)
                for ts_ in range(4):
                    slot2 = (use * 4 + ts_) % 2
                    P.wait("vector", s_dst[slot2])
                    for hc in range(4):
                        pd = [2, 3, 6, 7][hc]
                        wait_bank("tensor", pd)
                        for nch in range(8):
                            P.op("tensor", lambda t, nch=nch, ts_=ts_, hc=hc, pd=pd: t.matmul(
                                ps[pd][:],
                                inter_sb.ap()[:, nch * 512 + ts_ * 128: nch * 512 + ts_ * 128 + 128],
                                wdT_sb.ap()[:, nch * 2048 + hc * 512: nch * 2048 + hc * 512 + 512],
                                start=(nch == 0), stop=(nch == 7)),
                                inc=(s_pe if nch == 7 else None))
                        P.wait("vector", s_pe)
                        P.op("vector", lambda v, hc=hc, pd=pd, slot2=slot2: v.tensor_copy(
                            dstage[slot2].ap()[:, hc * 512:(hc + 1) * 512], ps[pd][:]),
                            inc=s_v)
                        set_bank(pd, s_v)
                    P.wait("gpsimd", s_v)
                    P.op("gpsimd", lambda g, tc=tc, ts_=ts_, slot2=slot2: g.dma_start(
                        out=contrib[tc * 512 + ts_ * 128: tc * 512 + ts_ * 128 + 128, :],
                        in_=dstage[slot2][:], accum_op=ALU.add), inc=s_dst[slot2], k=16)
                down_done = s_pe.n
                if hp == 1 and stage == "full":
                    P.wait("gpsimd", s_dst[0])
                    P.wait("gpsimd", s_dst[1])
                    P.op("gpsimd", lambda g, tc=tc: g.collective_compute(
                        "ReduceScatter", ALU.add,
                        replica_groups=[list(range(8))],
                        ins=[contrib[tc * 512:(tc + 1) * 512, :]
                             .rearrange("(p a) c -> p (a c)", p=128).opt()],
                        outs=[red[tc].opt()]), inc=s_cc, k=1)
                    if tc >= 1:
                        emit_final(tc - 1)

        if stage == "full":
            emit_final(15)

        if stage == "ffn":
            P.wait("gpsimd", s_g)
            P.op("gpsimd", lambda g: g.dma_start(out=dbg["contrib_out"], in_=contrib),
                 inc=s_g, k=16)
            P.wait("gpsimd", s_g)
            P.op("gpsimd", lambda g: g.dma_start(out=out[0], in_=xo[0]), inc=s_g, k=16)
            P.wait("gpsimd", s_g)
        else:
            P.wait("gpsimd", s_out, 16 * 16)

        P.replay()
    return nc, dbg



def make_rope_R(c):
    hd = 128
    freqs = 10000.0 ** (-(np.arange(0, hd, 2, dtype=np.float64) / hd))
    Rs = np.zeros((2, hd, hd), dtype=np.float32)
    for i, h in enumerate((2 * c, 2 * c + 1)):
        ang = h * freqs
        co, si = np.cos(ang), np.sin(ang)
        for j in range(hd // 2):
            Rs[i, 2 * j, 2 * j] = co[j]
            Rs[i, 2 * j + 1, 2 * j] = -si[j]
            Rs[i, 2 * j, 2 * j + 1] = si[j]
            Rs[i, 2 * j + 1, 2 * j + 1] = co[j]
    return Rs


def make_in_maps(L0_attn_x, L0_attn_w_attn, L0_attn_b_attn, L0_attn_w_proj,
                 L0_attn_b_proj, L0_ffn_gate_W_packed, L0_ffn_gate_scales,
                 L0_ffn_up_W_packed, L0_ffn_up_scales, L0_ffn_down_W_packed,
                 L0_ffn_down_scales):
    x = np.ascontiguousarray(np.asarray(L0_attn_x, dtype=np.float32)).reshape(TT, E)
    wa = np.asarray(L0_attn_w_attn, dtype=np.float32)
    ba = np.asarray(L0_attn_b_attn, dtype=np.float32)
    wp = np.asarray(L0_attn_w_proj, dtype=np.float32)
    bp = np.asarray(L0_attn_b_proj, dtype=np.float32)
    gpk = np.asarray(L0_ffn_gate_W_packed, dtype=np.int32)
    gsc = np.asarray(L0_ffn_gate_scales, dtype=np.float32)
    upk = np.asarray(L0_ffn_up_W_packed, dtype=np.int32)
    usc = np.asarray(L0_ffn_up_scales, dtype=np.float32)
    dpk = np.asarray(L0_ffn_down_W_packed, dtype=np.int32)
    dsc = np.asarray(L0_ffn_down_scales, dtype=np.float32)
    x16 = x.reshape(16, 8, 64, E)
    bpb = np.ascontiguousarray(np.broadcast_to(bp[None, :], (128, E)))
    in_maps = []
    for c in range(8):
        h0 = 2 * c * 128
        bqk = np.stack([ba[h0:h0 + 128], ba[h0 + 128:h0 + 256],
                        ba[E + h0:E + h0 + 128], ba[E + h0 + 128:E + h0 + 256]],
                       axis=1)
        bv = ba[2 * E + h0:2 * E + h0 + 256]
        in_maps.append(dict(
            xg=x,
            xo=np.ascontiguousarray(x16[:, c]),
            wq_sl=np.ascontiguousarray(wa[:, h0:h0 + 256]),
            wk_sl=np.ascontiguousarray(wa[:, E + h0:E + h0 + 256]),
            wv_sl=np.ascontiguousarray(wa[:, 2 * E + h0:2 * E + h0 + 256]),
            wp_sl=np.ascontiguousarray(wp[2 * c * 128:2 * c * 128 + 256, :]),
            bqk=np.ascontiguousarray(bqk),
            bvb=np.ascontiguousarray(np.broadcast_to(bv[None, :], (128, 256))),
            bpb=bpb,
            rope_R=make_rope_R(c),
            gate_pk=np.ascontiguousarray(gpk[NS * c:NS * (c + 1)]),
            gate_sc=np.ascontiguousarray(gsc[NS * c:NS * (c + 1)]),
            up_pk=np.ascontiguousarray(upk[NS * c:NS * (c + 1)]),
            up_sc=np.ascontiguousarray(usc[NS * c:NS * (c + 1)]),
            down_pk=np.ascontiguousarray(dpk[:, 1024 * c:1024 * (c + 1)]),
            down_sc=np.ascontiguousarray(dsc[:, 64 * c:64 * (c + 1)]),
        ))
    return in_maps


def assemble_output(results):
    full = np.empty((16, 8, 64, E), dtype=np.float32)
    for c in range(8):
        full[:, c] = np.asarray(results[c]["out_rows"])
    return full.reshape(B, T, E)


def kernel(**inputs):
    nc, _dbg = build_program("full")
    in_maps = make_in_maps(**inputs)
    res = run_bass_kernel_spmd(nc, in_maps, core_ids=list(range(8)))
    return assemble_output(res.results)


# version 28 (final)
# speedup vs baseline: 1.0118x; 1.0070x over previous
import numpy as np
import ml_dtypes
from contextlib import ExitStack

import concourse.bass as bass
import concourse.mybir as mybir
from concourse.bass_utils import run_bass_kernel_spmd

F32 = mybir.dt.float32
BF16 = mybir.dt.bfloat16
I32 = mybir.dt.int32
I16 = mybir.dt.int16
ALU = mybir.AluOpType
AF = mybir.ActivationFunctionType
AX = mybir.AxisListType.X

B, T, E = 2, 4096, 2048
TT = B * T
EC = E // 128
NS = 2048
GELU_A = 0.044715
GELU_S = 2.0 * 0.7978845608028654
INV_SQRT_E = float(E) ** -0.5
NEG_BIG = -1e30
NP_BF16 = ml_dtypes.bfloat16
KB = 1024


class Sem:
    __slots__ = ("h", "n")

    def __init__(self, h):
        self.h = h
        self.n = 0


class Prog:
    def __init__(self, nc, ctx):
        self.nc = nc
        self.ctx = ctx
        self.sched = []

    def sem(self, name):
        return Sem(self.ctx.enter_context(self.nc.semaphore(name)))

    def op(self, eng, fn, inc=None, k=1):
        if inc is not None:
            inc.n += k
            def wrapped(engine, fn=fn, s=inc.h, k=k):
                fn(engine).then_inc(s, k)
            self.sched.append((eng, wrapped))
        else:
            self.sched.append((eng, fn))

    def wait(self, eng, sem, val=None):
        v = sem.n if val is None else val
        assert 0 <= v <= sem.n, f"wait {v} > scheduled {sem.n}"
        if v > 0:
            self.sched.append((eng, lambda engine, s=sem.h, v=v: engine.wait_ge(s, v)))

    def drain(self, eng, inc=None, k=1):
        if inc is not None:
            inc.n += k
            self.sched.append((eng, lambda engine, s=inc.h, k=k:
                               engine.drain(fusable=False).then_inc(s, k)))
        else:
            self.sched.append((eng, lambda engine: engine.drain()))

    def replay(self):
        with self.nc.Block() as blk:
            for name in ("sync", "scalar", "tensor", "vector", "gpsimd"):
                ops = [fn for e, fn in self.sched if e == name]
                def body(engine, ops=ops):
                    for fn in ops:
                        fn(engine)
                getattr(blk, name)(body)


def build_program(stage="full"):
    nc = bass.Bass("TRN2", target_bir_lowering=False)

    xg = nc.dram_tensor("xg", [TT, E], F32, kind="ExternalInput").ap()
    xo = nc.dram_tensor("xo", [16, 64, E], F32, kind="ExternalInput").ap()
    wq_in = nc.dram_tensor("wq_sl", [E, 256], F32, kind="ExternalInput").ap()
    wk_in = nc.dram_tensor("wk_sl", [E, 256], F32, kind="ExternalInput").ap()
    wv_in = nc.dram_tensor("wv_sl", [E, 256], F32, kind="ExternalInput").ap()
    wp_in = nc.dram_tensor("wp_sl", [256, E], F32, kind="ExternalInput").ap()
    bqk_in = nc.dram_tensor("bqk", [128, 4], F32, kind="ExternalInput").ap()
    bvb_in = nc.dram_tensor("bvb", [128, 256], F32, kind="ExternalInput").ap()
    bpb_in = nc.dram_tensor("bpb", [128, E], F32, kind="ExternalInput").ap()
    rope_in = nc.dram_tensor("rope_R", [2, 128, 128], F32, kind="ExternalInput").ap()
    gpk_in = nc.dram_tensor("gate_pk", [NS, 1024], I32, kind="ExternalInput").ap()
    gsc_in = nc.dram_tensor("gate_sc", [NS, 64], F32, kind="ExternalInput").ap()
    upk_in = nc.dram_tensor("up_pk", [NS, 1024], I32, kind="ExternalInput").ap()
    usc_in = nc.dram_tensor("up_sc", [NS, 64], F32, kind="ExternalInput").ap()
    dpk_in = nc.dram_tensor("down_pk", [E, 1024], I32, kind="ExternalInput").ap()
    dsc_in = nc.dram_tensor("down_sc", [E, 64], F32, kind="ExternalInput").ap()
    out = nc.dram_tensor("out_rows", [16, 64, E], F32, kind="ExternalOutput").ap()

    eye_np = np.eye(128, dtype=np.float32).astype(NP_BF16)
    eye_d = nc.inline_tensor(eye_np, name="eye_c").ap()
    ones_d = nc.inline_tensor(np.ones((128, 128), np.float32).astype(NP_BF16), name="ones_c").ap()
    c128_d = nc.inline_tensor(np.full((1, 128), 1 / 128, np.float32).astype(NP_BF16), name="c128_c").ap()
    ones1_d = nc.inline_tensor(np.ones((1, 128), np.float32).astype(NP_BF16), name="ones1_c").ap()
    mv = np.zeros((4, 128, 512), np.float32)
    pp = np.arange(128)[:, None]
    ff = np.arange(512)[None, :]
    for m in range(4):
        mv[m] = np.where(m * 128 + pp <= ff, 0.0, NEG_BIG)
    mask_d = nc.inline_tensor(mv.astype(NP_BF16), name="mask_c").ap()

    xnf = nc.dram_tensor("xnf", [EC, 128, TT], BF16, kind="Internal").ap()
    contrib = nc.dram_tensor("contrib", [TT, E], F32, kind="Internal").ap()
    wgT_d = nc.dram_tensor("wgT_d", [2, EC, 128, 1024], BF16, kind="Internal").ap()
    wuT_d = nc.dram_tensor("wuT_d", [2, EC, 128, 1024], BF16, kind="Internal").ap()
    wdT_d = nc.dram_tensor("wdT_d", [2, 8, 128, E], BF16, kind="Internal").ap()
    red = [nc.dram_tensor(f"red{j}", [16, 4 * E], F32, kind="Internal").ap() for j in range(16)]

    dbg = {}
    if stage in ("ln",):
        dbg["xnf_out"] = nc.dram_tensor("xnf_out", [EC, 128, TT], BF16, kind="ExternalOutput").ap()
    if stage in ("attn",):
        dbg["contrib_out"] = nc.dram_tensor("contrib_out", [TT, E], F32, kind="ExternalOutput").ap()
        for nm in ("q_fm_out", "k_fm_out", "v_tm_out", "o0_out", "o1_out"):
            dbg[nm] = nc.dram_tensor(nm, [128, T], BF16, kind="ExternalOutput").ap()
        dbg["xq_out"] = nc.dram_tensor("xq_out", [128, EC * 512], BF16, kind="ExternalOutput").ap()
    if stage in ("deq",):
        dbg["wgT_out"] = nc.dram_tensor("wgT_out", [2, EC, 128, 1024], BF16, kind="ExternalOutput").ap()
        dbg["wdT_out"] = nc.dram_tensor("wdT_out", [2, 8, 128, E], BF16, kind="ExternalOutput").ap()
    if stage in ("ffn",):
        dbg["contrib_out"] = nc.dram_tensor("contrib_out", [TT, E], F32, kind="ExternalOutput").ap()

    alloc = nc.alloc_sbuf_tensor_at
    BASE = 16512
    TOP = 229376
    _cur = {"A": BASE, "C": BASE + 16 * KB, "E": BASE + 48 * KB,
            "B": BASE + 103 * KB, "D": BASE + 103 * KB,
            "F": BASE + 16 * KB, "G": BASE + 144 * KB,
            "B2": BASE + 119 * KB, "B3": BASE + 103 * KB}
    _lim = {"A": BASE + 16 * KB, "C": BASE + 48 * KB, "E": BASE + 103 * KB,
            "B": BASE + 168 * KB, "D": TOP, "F": BASE + 144 * KB, "G": TOP,
            "B2": BASE + 151 * KB, "B3": BASE + 119 * KB}

    def sb(name, shape, dtype, region, cur=None):
        cur = _cur
        nbytes = int(np.prod(shape[1:])) * mybir.dt.size(dtype)
        o = (cur[region] + 31) & ~31
        cur[region] = o + nbytes
        assert cur[region] <= _lim[region], \
            f"{name}: region {region} overflow {cur[region]} > {_lim[region]}"
        return alloc(name, shape, dtype, offset=o)

    eye_sb = sb("eye_sb", [128, 128], BF16, "A")
    ones_sb = sb("ones_sb", [128, 128], BF16, "A")
    c128_sb = sb("c128_sb", [1, 128], BF16, "A")
    ones1_sb = sb("ones1_sb", [1, 128], BF16, "A")
    mask_sb = sb("mask_sb", [128, 4 * 512], BF16, "A")
    rope_sb = sb("rope_sb", [128, 2 * 128], BF16, "A")
    bqk_sb = sb("bqk_sb", [128, 4], F32, "A")
    bvb_sb = sb("bvb_sb", [128, 256], BF16, "A")
    bpb_sb = sb("bpb_sb", [128, E], F32, "A")

    wq_sb = sb("wq_sb", [128, EC * 256], BF16, "C")
    wk_sb = sb("wk_sb", [128, EC * 256], BF16, "C")
    wv_sb = sb("wv_sb", [128, EC * 256], BF16, "C")
    wp_sb = sb("wp_sb", [128, 2 * E], BF16, "C")

    pk_sb = [sb(f"pk_sb{i}", [128, 1024], I32, "E") for i in range(2)]
    sc_sb = [sb(f"sc_sb{i}", [128, 64], F32, "E") for i in range(2)]
    wT_st = [sb(f"wT_st{i}", [128, 2048], BF16, "E") for i in range(2)]
    dq_b16 = sb("dq_b16", [128, 1024], I16, "E")
    dq_c16 = sb("dq_c16", [128, 2048], I16, "E")
    dq_k6 = sb("dq_k6", [128, 2048], I16, "E")
    dq_t12 = sb("dq_t12", [128, 2048], I16, "E")
    dq_pat = sb("dq_pat", [128, 2048], I16, "E")
    dq_mbf = sb("dq_mbf", [128, 2048], BF16, "E")
    dq_q16 = sb("dq_q16", [128, 2048], I16, "E")
    dq_corr = sb("dq_corr", [128, 2048], BF16, "E")
    dq_w = [sb(f"dq_w{i}", [128, 2048], BF16, "E") for i in range(2)]

    stg_f = sb("stg_f", [128, EC * 256], F32, "B")
    x_sb = [sb(f"x_sb{i}", [128, 4 * E], BF16, "B") for i in range(2)]
    xn_tm = [sb(f"xn_tm{i}", [128, E], BF16, "B") for i in range(2)]
    xnT_st = [sb(f"xnT_st{i}", [128, E], BF16, "B") for i in range(2)]
    stats_sb = sb("stats_sb", [128, 96], F32, "B")
    ssum_sb = sb("ssum_sb", [128, 4], F32, "B")
    ssq_sb = sb("ssq_sb", [128, 4], F32, "B")
    sqdump_sb = sb("sqdump_sb", [128, E], F32, "B3")
    mean_sb = sb("mean_sb", [128, 4], F32, "B")
    mv_sb = sb("mv_sb", [128, 8], F32, "B")
    rstd_sb = sb("rstd_sb", [128, 4], F32, "B")
    tmp1_sb = sb("tmp1_sb", [128, 4], F32, "B")

    xq_buf = [sb(f"xq_buf{i}", [128, EC * 512], BF16, "D") for i in range(2)]
    q_fm = sb("q_fm", [128, T], BF16, "D")
    k_fm = sb("k_fm", [128, T], BF16, "D")
    v_tm = sb("v_tm", [128, T], BF16, "D")
    o_fm = [sb(f"o_fm{i}", [128, T], BF16, "D") for i in range(2)]
    p_slot = sb("p_slot", [128, 4 * 512], BF16, "D")
    qraw_sb = p_slot
    negmax_sb = sb("negmax_sb", [128, 32], BF16, "D")
    negc_sb = sb("negc_sb", [1, 512], BF16, "D")
    cb_all = sb("cb_all", [128, 8 * 512], BF16, "D")
    cbm_sb = [sb(f"cbm_sb{i}", [128, 4 * 512], BF16, "D") for i in range(2)]
    recip_sb = sb("recip_sb", [128, 512], F32, "D")
    pstage = sb("pstage", [128, E], F32, "D")

    wgT_sb = sb("wgT_sb", [128, EC * 1024], BF16, "F")
    wuT_sb = sb("wuT_sb", [128, EC * 1024], BF16, "F")
    wdT_sb = sb("wdT_sb", [128, 8 * E], BF16, "F")
    xf_buf = [sb(f"xf_buf{i}", [128, EC * 512], BF16, "F") for i in range(2)]
    inter_sb = sb("inter_sb", [128, 8 * 512], BF16, "G")
    g_sq = [sb(f"g_sq{i}", [128, 512], F32, "G") for i in range(2)]
    g_d = [sb(f"g_d{i}", [128, 512], F32, "G") for i in range(2)]
    g_e = [sb(f"g_e{i}", [128, 512], F32, "G") for i in range(2)]
    g_sg = [sb(f"g_sg{i}", [128, 512], F32, "G") for i in range(2)]
    g_gs = [sb(f"g_gs{i}", [128, 512], F32, "G") for i in range(2)]
    dstage = [sb(f"dstage{i}", [128, E], F32, "G") for i in range(2)]

    fin_sb = sb("fin_sb", [64, E], F32, "G")
    xo_sb = sb("xo_sb", [64, E], F32, "G")

    ctx = ExitStack()
    with ctx:
        ps = [ctx.enter_context(nc.psum_tensor(f"ps{i}", [128, 512], F32)) for i in range(8)]
        P = Prog(nc, ctx)

        bankmark = {i: (None, 0) for i in range(8)}

        def wait_bank(eng, pb):
            sem, val = bankmark[pb]
            if sem is not None:
                P.wait(eng, sem, val)

        def set_bank(pb, sem):
            bankmark[pb] = (sem, sem.n)

        s_ld = P.sem("s_ld")
        s_x = [P.sem("s_x0"), P.sem("s_x1")]
        s_xw2 = [P.sem("s_xwA"), P.sem("s_xwB")]
        s_xw = P.sem("s_xw")
        s_pk = P.sem("s_pk")
        s_wT = P.sem("s_wT")
        s_pe = P.sem("s_pe")
        s_v = P.sem("s_v")
        s_s = P.sem("s_s")
        s_g = P.sem("s_g")
        s_cc = P.sem("s_cc")

        s_stg = P.sem("s_stg")

        def syncdma(fn):
            P.op("sync", fn, inc=s_ld, k=16)

        syncdma(lambda e: e.dma_start(out=eye_sb[:], in_=eye_d))
        syncdma(lambda e: e.dma_start(out=ones_sb[:], in_=ones_d))
        syncdma(lambda e: e.dma_start(out=c128_sb[:], in_=c128_d))
        syncdma(lambda e: e.dma_start(out=ones1_sb[:], in_=ones1_d))
        syncdma(lambda e: e.dma_start(out=mask_sb.ap().rearrange("p (m f) -> p m f", m=4),
                                      in_=mask_d.rearrange("m p f -> p m f")))
        syncdma(lambda e: e.dma_start(out=bqk_sb[:], in_=bqk_in))
        syncdma(lambda e: e.dma_start(out=bpb_sb[:], in_=bpb_in))
        ld_all = s_ld.n

        P.op("sync", lambda e: e.dma_start(out=stg_f[:, 0:256], in_=bvb_in),
             inc=s_stg, k=16)
        P.op("sync", lambda e: e.dma_start(out=stg_f[:, 256:384], in_=rope_in[0]),
             inc=s_stg, k=16)
        P.op("sync", lambda e: e.dma_start(out=stg_f[:, 384:512], in_=rope_in[1]),
             inc=s_stg, k=16)
        P.wait("vector", s_stg)
        P.op("vector", lambda v: v.tensor_scalar(bvb_sb[:], stg_f[:, 0:256],
                                                 1.0 / 128.0, None, ALU.mult))
        P.op("vector", lambda v: v.tensor_copy(rope_sb[:], stg_f[:, 256:512]), inc=s_v)
        P.wait("sync", s_v)
        for win, wsb in ((wq_in, wq_sb), (wk_in, wk_sb), (wv_in, wv_sb)):
            P.op("sync", lambda e, win=win: e.dma_start(
                out=stg_f.ap().rearrange("p (c d) -> p c d", d=256),
                in_=win.rearrange("(c p) d -> p c d", p=128)), inc=s_stg, k=16)
            P.wait("vector", s_stg)
            P.op("vector", lambda v, wsb=wsb: v.tensor_copy(wsb[:], stg_f[:]), inc=s_v)
            P.wait("sync", s_v)
        P.op("sync", lambda e: e.dma_start(
            out=stg_f.ap()[:, 0:2 * E].rearrange("p (c u) -> p c u", c=2),
            in_=wp_in.rearrange("(c p) u -> p c u", p=128)), inc=s_stg, k=16)
        P.wait("vector", s_stg)
        P.op("vector", lambda v: v.tensor_copy(wp_sb[:], stg_f[:, 0:2 * E]), inc=s_v)
        P.wait("sync", s_v)

        s_pk2 = [P.sem("s_pkA"), P.sem("s_pkB")]
        s_wst = [P.sem("s_wstA"), P.sem("s_wstB")]
        s_p = P.sem("s_p")
        ps7_bf = ps[7].ap().bitcast(BF16)
        deq_rel = {}
        wst_rel = {}
        deq_pe_mark = {}

        def deq_params(widx, half, gi):
            pkin, scin = [(gpk_in, gsc_in), (upk_in, usc_in), (dpk_in, dsc_in)][widx]
            nb = 1024 if widx < 2 else 512
            return pkin, scin, nb

        def emit_deq_load(gidx):
            widx, half, gi = deq_queue[gidx]
            pkin, scin, nb = deq_params(widx, half, gi)
            slot = gidx % 2
            if gidx >= 2:
                se_, sv_ = deq_rel[gidx - 2]
                P.wait("gpsimd", se_, sv_)
            if widx < 2:
                r0 = half * 1024 + gi * 128
                P.op("gpsimd", lambda g, pkin=pkin, r0=r0, slot=slot: g.dma_start(
                    out=pk_sb[slot][:], in_=pkin[r0:r0 + 128, :]), inc=s_pk2[slot], k=16)
                P.op("gpsimd", lambda g, scin=scin, r0=r0, slot=slot: g.dma_start(
                    out=sc_sb[slot][:], in_=scin[r0:r0 + 128, :]), inc=s_pk2[slot], k=16)
            else:
                r0 = gi * 128
                c0 = half * 512
                P.op("gpsimd", lambda g, r0=r0, c0=c0, slot=slot: g.dma_start(
                    out=pk_sb[slot][:, 0:512], in_=dpk_in[r0:r0 + 128, c0:c0 + 512]),
                    inc=s_pk2[slot], k=16)
                P.op("gpsimd", lambda g, r0=r0, c0=c0, slot=slot: g.dma_start(
                    out=sc_sb[slot][:, 0:32], in_=dsc_in[r0:r0 + 128, c0 // 16: c0 // 16 + 32]),
                    inc=s_pk2[slot], k=16)

        def emit_deq_decode(gidx):
            widx, half, gi = deq_queue[gidx]
            _, _, nb = deq_params(widx, half, gi)
            eng = "vector"
            s_e = s_v
            slot = gidx % 2
            ncd = 2 * nb
            nsc = ncd // 32
            P.wait(eng, s_pk2[slot])
            P.drain(eng)
            if gidx >= 2:
                pse_, pv_ = deq_w_free[gidx - 2]
                P.wait(eng, pse_, pv_)
            P.op(eng, lambda v, slot=slot, nb=nb: v.tensor_copy(
                dq_b16[:, 0:nb], pk_sb[slot][:, 0:nb]))
            P.drain(eng)
            P.op(eng, lambda v, nb=nb: v.tensor_scalar(
                dq_c16[:, 0:nb], dq_b16[:, 0:nb], 15, None, ALU.bitwise_and))
            P.op(eng, lambda v, nb=nb: v.tensor_scalar(
                dq_c16[:, nb:2 * nb], dq_b16[:, 0:nb], 4, 15,
                ALU.logical_shift_right, ALU.bitwise_and))
            P.drain(eng)
            P.op(eng, lambda v, ncd=ncd: v.tensor_scalar(
                dq_k6[:, 0:ncd], dq_c16[:, 0:ncd], 7, 6,
                ALU.bitwise_and, ALU.logical_shift_left))
            P.op(eng, lambda v, ncd=ncd: v.tensor_scalar(
                dq_t12[:, 0:ncd], dq_c16[:, 0:ncd], 8, 12,
                ALU.bitwise_and, ALU.logical_shift_left))
            P.drain(eng)
            P.op(eng, lambda v, ncd=ncd: v.tensor_add(
                dq_pat[:, 0:ncd], dq_k6[:, 0:ncd], dq_t12[:, 0:ncd]))
            P.drain(eng)
            P.op(eng, lambda v, ncd=ncd: v.tensor_scalar(
                dq_pat[:, 0:ncd], dq_pat[:, 0:ncd], 0x3F00, None, ALU.add))
            P.drain(eng)
            P.op(eng, lambda v, ncd=ncd: v.tensor_scalar(
                dq_c16[:, 0:ncd], dq_k6[:, 0:ncd], -1, 0x0080, ALU.mult, ALU.add))
            P.drain(eng)
            P.op(eng, lambda v, ncd=ncd: v.tensor_scalar_max(
                dq_mbf[:, 0:ncd], dq_c16[:, 0:ncd], 0))
            P.op(eng, lambda v, ncd=ncd: v.tensor_scalar(
                dq_q16[:, 0:ncd], dq_t12[:, 0:ncd], 0x3B80, None, ALU.add))
            P.drain(eng)
            P.op(eng, lambda v, ncd=ncd: v.tensor_mul(
                dq_corr[:, 0:ncd], dq_mbf[:, 0:ncd],
                dq_q16.ap()[:, 0:ncd].bitcast(BF16)))
            P.drain(eng)
            P.op(eng, lambda v, ncd=ncd: v.tensor_sub(
                dq_mbf[:, 0:ncd], dq_pat.ap()[:, 0:ncd].bitcast(BF16),
                dq_corr[:, 0:ncd]))
            P.drain(eng)
            P.op(eng, lambda v, ncd=ncd, nb=nb, nsc=nsc, slot=slot: v.tensor_tensor(
                dq_w[slot].ap()[:, 0:ncd].rearrange("p (s x h) -> p h s x", h=2, x=16),
                dq_mbf.ap()[:, 0:ncd].rearrange("p (h s x) -> p h s x", h=2, x=16),
                sc_sb[slot].ap()[:, 0:nsc].rearrange("p (o s x) -> p o s x", o=1, x=1)
                    .broadcast_to([128, 2, nsc, 16]),
                ALU.mult), inc=s_e)
            deq_rel[gidx] = (s_e, s_e.n)
            deq_dec[gidx] = (s_e, s_e.n)

        def emit_deq_transpose(gidx):
            widx, half, gi = deq_queue[gidx]
            _, _, nb = deq_params(widx, half, gi)
            slot = gidx % 2
            nt = (2 * nb) // 128
            de_, dv_ = deq_dec[gidx]
            if gidx >= 2:
                P.wait("vector", s_wst[slot], wst_rel[gidx - 2])
            for rnd in range((nt + 7) // 8):
                k8 = min(8, nt - rnd * 8)
                P.wait("tensor", de_, dv_)
                wait_bank("tensor", 7)
                for i in range(k8):
                    ecx = rnd * 8 + i
                    P.op("tensor", lambda t, ecx=ecx, i=i, slot=slot: t.transpose(
                        ps7_bf[:, i * 128:(i + 1) * 128],
                        dq_w[slot].ap()[:, ecx * 128:(ecx + 1) * 128], eye_sb[:]),
                        inc=(s_pe if i == k8 - 1 else None))
                P.wait("vector", s_pe)
                P.op("vector", lambda v, rnd=rnd, k8=k8, slot=slot: v.tensor_copy(
                    wT_st[slot][:, rnd * 1024: rnd * 1024 + k8 * 128],
                    ps7_bf[:, 0:k8 * 128]), inc=s_v)
                set_bank(7, s_v)
            deq_w_free[gidx] = (s_pe, s_pe.n)
            P.wait("gpsimd", s_v, s_v.n)
            wout = [wgT_d, wuT_d, wdT_d][widx]
            P.op("gpsimd", lambda e, wout=wout, half=half, gi=gi, nt=nt, slot=slot:
                 e.dma_start(
                     out=wout[half, 0:nt, :, gi * 128:(gi + 1) * 128]
                         .rearrange("c p f -> p c f"),
                     in_=wT_st[slot].ap()[:, 0:nt * 128]
                         .rearrange("p (c f) -> p c f", f=128)),
                 inc=s_wst[slot], k=16)
            wst_rel[gidx] = s_wst[slot].n

        deq_queue = []
        for half in range(2):
            for widx in range(3):
                for gi in range(8 if widx < 2 else 16):
                    deq_queue.append((widx, half, gi))
        deq_dec = {}
        deq_w_free = {}
        deq_state = {"ld": 0, "dc": 0, "tr": 0}

        def pump_deq(k):
            nq = len(deq_queue)
            for _ in range(k):
                while deq_state["ld"] < min(nq, deq_state["dc"] + 2):
                    emit_deq_load(deq_state["ld"])
                    deq_state["ld"] += 1
                if deq_state["tr"] < deq_state["dc"] - 1:
                    emit_deq_transpose(deq_state["tr"])
                    deq_state["tr"] += 1
                if deq_state["dc"] < deq_state["ld"]:
                    emit_deq_decode(deq_state["dc"])
                    deq_state["dc"] += 1

        def pump_deq_all2():
            nq = len(deq_queue)
            while deq_state["dc"] < nq:
                pump_deq(1)
            while deq_state["tr"] < nq:
                emit_deq_transpose(deq_state["tr"])
                deq_state["tr"] += 1

        P.wait("tensor", s_ld, ld_all)
        P.wait("scalar", s_v)
        sv_ln = {}
        tpm = {0: 0, 1: 0}
        for ch in range(16):
            buf = ch % 2
            t0 = ch * 512
            if ch >= 2:
                P.wait("gpsimd", s_v, sv_ln[ch - 2])
            P.op("gpsimd", lambda e, t0=t0, buf=buf: e.dma_start(
                out=x_sb[buf].ap().rearrange("p (s e) -> p s e", s=4),
                in_=xg[t0:t0 + 512, :].rearrange("(s p) e -> p s e", p=128)),
                inc=s_x[buf], k=16)
            P.wait("vector", s_x[buf])
            P.drain("vector")
            for st_ in range(4):
                xs = x_sb[buf].ap()[:, st_ * E:(st_ + 1) * E]
                for i in range(4):
                    P.op("vector", lambda v, xs=xs, i=i, st_=st_: v.bn_stats(
                        stats_sb[:, st_ * 24 + i * 6: st_ * 24 + (i + 1) * 6],
                        xs[:, i * 512:(i + 1) * 512]))
            P.drain("vector")
            for st_ in range(4):
                P.op("vector", lambda v, st_=st_: v.bn_aggr(
                    mean_sb.ap().rearrange("p (s x) -> p s x", x=1)[:, st_:st_ + 1, 0]
                    if False else
                    mv_sb[:, st_ * 2: st_ * 2 + 2], stats_sb[:, st_ * 24: st_ * 24 + 24]))
            P.drain("vector")
            P.op("vector", lambda v: v.tensor_scalar(
                tmp1_sb[:], mv_sb.ap().rearrange("p (s x) -> p s x", x=2)[:, :, 1:2],
                1e-5, None, ALU.add), inc=s_v)
            P.wait("scalar", s_v)
            P.drain("scalar")
            P.op("scalar", lambda s: s.activation(tmp1_sb[:], tmp1_sb[:], AF.Sqrt),
                 inc=s_s)
            P.wait("vector", s_s)
            P.op("vector", lambda v: v.reciprocal(rstd_sb[:], tmp1_sb[:]))
            P.drain("vector")
            for st_ in range(4):
                xs = x_sb[buf].ap()[:, st_ * E:(st_ + 1) * E]
                tmb = xn_tm[st_ % 2]
                P.wait("vector", s_pe, tpm[st_ % 2])
                P.op("vector", lambda v, xs=xs, tmb=tmb, st_=st_: v.tensor_scalar(
                    tmb[:], xs, mv_sb.ap()[:, st_ * 2: st_ * 2 + 1],
                    rstd_sb.ap()[:, st_:st_ + 1], ALU.subtract, ALU.mult),
                    inc=s_v)
                if st_ == 3:
                    sv_ln[ch] = s_v.n
                nm_mark = s_v.n
                for rnd in range(2):
                    pb = 6 if rnd == 0 else 5
                    pv = ps[pb].ap().bitcast(BF16)
                    P.wait("tensor", s_v, nm_mark)
                    wait_bank("tensor", pb)
                    for i in range(8):
                        ecx = rnd * 8 + i
                        P.op("tensor", lambda t, tmb=tmb, ecx=ecx, i=i, pv=pv: t.transpose(
                            pv[:, i * 128:(i + 1) * 128],
                            tmb[:, ecx * 128:(ecx + 1) * 128], eye_sb[:]),
                            inc=(s_pe if i == 7 else None))
                    P.wait("scalar", s_pe)
                    if rnd == 0:
                        P.wait("scalar", s_xw2[st_ % 2])
                    P.op("scalar", lambda s, rnd=rnd, st_=st_, pv=pv: s.copy(
                        xnT_st[st_ % 2][:, rnd * 1024:(rnd + 1) * 1024], pv[:]), inc=s_s)
                    set_bank(pb, s_s)
                tpm[st_ % 2] = s_pe.n
                P.wait("sync", s_s, s_s.n)
                P.op("sync", lambda e, st_=st_, t0=t0: e.dma_start(
                    out=xnf[0:16, :, t0 + st_ * 128: t0 + st_ * 128 + 128]
                        .rearrange("c p f -> p c f"),
                    in_=xnT_st[st_ % 2].ap().rearrange("p (c f) -> p c f", f=128)),
                    inc=s_xw2[st_ % 2], k=16)
            if ch >= 1:
                pass


        ln_done_xw = (s_xw2[0].n, s_xw2[1].n)

        if stage == "ln":
            P.wait("gpsimd", s_xw2[0], ln_done_xw[0])
            P.wait("gpsimd", s_xw2[1], ln_done_xw[1])
            for ecx in range(16):
                P.op("gpsimd", lambda g, ecx=ecx: g.dma_start(
                    out=dbg["xnf_out"][ecx], in_=xnf[ecx]), inc=s_g, k=16)
            P.wait("gpsimd", s_g)
            P.op("gpsimd", lambda g: g.dma_start(out=out[0], in_=xo[0]), inc=s_g, k=16)
            P.wait("gpsimd", s_g)
            P.replay()
            return nc, dbg

        s_xq = [P.sem("s_xq0"), P.sem("s_xq1")]
        ps5_bf = ps[5].ap().bitcast(BF16)
        ps6_bf = ps[6].ap().bitcast(BF16)
        xq_rel = {}
        P.wait("sync", s_xw2[0], ln_done_xw[0])
        P.wait("sync", s_xw2[1], ln_done_xw[1])

        exp_mark = {0: 0, 1: 0}
        pslot_rel = {}
        s_pst = [P.sem(f"s_pst{u}") for u in range(4)]
        s_wl = P.sem("s_wl")
        for pair in range(4):
            b_, hl = divmod(pair, 2)
            tb = b_ * T
            for j in range(8):
                slot = j % 2
                use = pair * 8 + j
                if use >= 2:
                    P.wait("sync", s_pe, xq_rel[use - 2])
                for g4 in range(4):
                    P.op("sync", lambda e, g4=g4, slot=slot, tb=tb, j=j: e.dma_start(
                        out=xq_buf[slot].ap().rearrange("p (c f) -> p c f", c=EC)
                            [:, 4 * g4:4 * g4 + 4, :],
                        in_=xnf[4 * g4:4 * g4 + 4, :, tb + j * 512: tb + j * 512 + 512]
                            .rearrange("c p f -> p c f")),
                        inc=s_xq[slot], k=16)
                P.wait("tensor", s_xq[slot])
                xq = xq_buf[slot]
                pq, pk_, pr = (4, 6, 5) if j % 2 == 0 else (2, 3, 1)
                qrw = qraw_sb.ap()[:, 0:512]
                krw = qraw_sb.ap()[:, 512:1024]
                wait_bank("tensor", pq)
                for ec in range(EC):
                    P.op("tensor", lambda t, ec=ec, xq=xq, hl=hl, pq=pq: t.matmul(
                        ps[pq][:], wq_sb.ap()[:, ec * 256 + hl * 128: ec * 256 + hl * 128 + 128],
                        xq.ap()[:, ec * 512:(ec + 1) * 512],
                        start=(ec == 0), stop=(ec == EC - 1)),
                        inc=(s_pe if ec == EC - 1 else None))
                q_done = s_pe.n
                wait_bank("tensor", pk_)
                for ec in range(EC):
                    P.op("tensor", lambda t, ec=ec, xq=xq, hl=hl, pk_=pk_: t.matmul(
                        ps[pk_][:], wk_sb.ap()[:, ec * 256 + hl * 128: ec * 256 + hl * 128 + 128],
                        xq.ap()[:, ec * 512:(ec + 1) * 512],
                        start=(ec == 0), stop=(ec == EC - 1)),
                        inc=(s_pe if ec == EC - 1 else None))
                k_done = s_pe.n
                P.wait("scalar", s_pe, q_done)
                P.op("scalar", lambda s, hl=hl, pq=pq, qrw=qrw: s.add(
                    qrw, ps[pq][:], bqk_sb.ap()[:, hl:hl + 1]), inc=s_s)
                q_evac = s_s.n
                set_bank(pq, s_s)
                P.wait("scalar", s_pe, k_done)
                P.op("scalar", lambda s, hl=hl, pk_=pk_, krw=krw: s.add(
                    krw, ps[pk_][:], bqk_sb.ap()[:, 2 + hl:3 + hl]), inc=s_s)
                k_evac = s_s.n
                set_bank(pk_, s_s)
                wait_bank("tensor", pr)
                P.wait("tensor", s_s, q_evac)
                P.op("tensor", lambda t, hl=hl, pr=pr, qrw=qrw: t.matmul(
                    ps[pr][:], rope_sb.ap()[:, hl * 128:(hl + 1) * 128], qrw,
                    start=True, stop=True), inc=s_pe)
                P.wait("scalar", s_pe)
                P.op("scalar", lambda s, j=j, pr=pr: s.copy(
                    q_fm.ap()[:, j * 512:(j + 1) * 512], ps[pr][:]), inc=s_s)
                set_bank(pr, s_s)
                wait_bank("tensor", pq)
                for ts_ in range(4):
                    for ec in range(EC):
                        P.op("tensor", lambda t, ec=ec, ts_=ts_, xq=xq, hl=hl, pq=pq: t.matmul(
                            ps[pq][:, ts_ * 128:(ts_ + 1) * 128],
                            xq.ap()[:, ec * 512 + ts_ * 128: ec * 512 + ts_ * 128 + 128],
                            wv_sb.ap()[:, ec * 256 + hl * 128: ec * 256 + hl * 128 + 128],
                            start=(ec == 0), stop=False))
                    P.op("tensor", lambda t, ts_=ts_, hl=hl, pq=pq: t.matmul(
                        ps[pq][:, ts_ * 128:(ts_ + 1) * 128], ones_sb[:],
                        bvb_sb.ap()[:, hl * 128:(hl + 1) * 128],
                        start=False, stop=True), inc=(s_pe if ts_ == 3 else None))
                xq_rel[use] = s_pe.n
                v_done = s_pe.n
                wait_bank("tensor", pr)
                P.wait("tensor", s_s, k_evac)
                P.op("tensor", lambda t, hl=hl, pr=pr, krw=krw: t.matmul(
                    ps[pr][:], rope_sb.ap()[:, hl * 128:(hl + 1) * 128], krw,
                    start=True, stop=True), inc=s_pe)
                rk_done = s_pe.n
                P.wait("scalar", s_pe, rk_done)
                P.op("scalar", lambda s, j=j, pr=pr: s.copy(
                    k_fm.ap()[:, j * 512:(j + 1) * 512], ps[pr][:]), inc=s_s)
                set_bank(pr, s_s)
                P.wait("scalar", s_pe, v_done)
                P.op("scalar", lambda s, j=j, pq=pq: s.copy(
                    v_tm.ap()[:, j * 512:(j + 1) * 512], ps[pq][:]), inc=s_s)
                set_bank(pq, s_s)
                pump_deq(1)
            if pair == 3:
                P.wait("sync", s_pe, s_pe.n)
                P.wait("sync", s_wst[0], wst_rel[6])
                P.wait("sync", s_wst[1], wst_rel[7])
                P.op("sync", lambda e: e.dma_start(
                    out=wgT_sb.ap()[:, 0:12 * 1024]
                        .rearrange("p (c f) -> p c f", c=12),
                    in_=wgT_d[0, 0:12].rearrange("c p f -> p c f")),
                    inc=s_wl, k=16)
            P.wait("tensor", s_s)
            for r8 in range(8):
                pd_ = 4 if r8 % 2 == 0 else 2
                wait_bank("tensor", pd_)
                for m in range(4):
                    qt = 4 * r8 + m
                    P.op("tensor", lambda t, m=m, qt=qt, pd_=pd_: t.matmul(
                        ps[pd_][:, m * 128:(m + 1) * 128],
                        q_fm.ap()[:, qt * 128:(qt + 1) * 128],
                        k_fm.ap()[:, qt * 128:(qt + 1) * 128],
                        start=True, stop=True), inc=(s_pe if m == 3 else None))
                P.wait("vector", s_pe)
                P.op("vector", lambda v, r8=r8, pd_=pd_: v.reduce_max(
                    negmax_sb[:, r8 * 4:(r8 + 1) * 4],
                    ps[pd_].ap().rearrange("p (m f) -> p m f", m=4),
                    axis=AX, negate=True), inc=s_v)
                set_bank(pd_, s_v)
            nmx_done = s_v.n
            for r8 in range(8):
                wait_bank("tensor", 5)
                if r8 == 0:
                    P.wait("tensor", s_v, nmx_done)
                for m in range(4):
                    P.op("tensor", lambda t, r8=r8, m=m: t.transpose(
                        ps5_bf[0:1, m * 128:(m + 1) * 128],
                        negmax_sb[:, r8 * 4 + m: r8 * 4 + m + 1], eye_sb[:]),
                        inc=(s_pe if m == 3 else None))
                P.wait("vector", s_pe)
                P.op("vector", lambda v: v.tensor_copy(negc_sb[:], ps5_bf[0:1, 0:512]),
                     inc=s_v)
                ngc = s_v.n
                set_bank(5, s_v)
                wait_bank("tensor", 1)
                P.wait("tensor", s_v, ngc)
                P.op("tensor", lambda t: t.matmul(ps[1][:], ones1_sb[:], negc_sb[:],
                                                  start=True, stop=True), inc=s_pe)
                P.wait("vector", s_pe)
                P.op("vector", lambda v, r8=r8: v.tensor_copy(
                    cb_all.ap()[:, r8 * 512:(r8 + 1) * 512], ps[1][:]), inc=s_v)
                set_bank(1, s_v)
            cb_done = s_v.n

            def emit_cbm(C2):
                par3 = C2 % 2
                P.wait("gpsimd", s_v, cb_done)
                if C2 >= 2:
                    P.wait("gpsimd", s_pe, cbm_used[C2 - 2])
                P.drain("gpsimd")
                for m in range(4):
                    P.op("gpsimd", lambda v, C2=C2, m=m, par3=par3: v.tensor_add(
                        cbm_sb[par3].ap()[:, m * 512:(m + 1) * 512],
                        cb_all.ap()[:, C2 * 512:(C2 + 1) * 512],
                        mask_sb.ap()[:, m * 512:(m + 1) * 512]),
                        inc=(s_p if m == 3 else None))
                cbm_mark[C2] = s_p.n

            cbm_mark = {}
            cbm_used = {}
            emit_cbm(0)
            wait_bank("tensor", 0)
            wait_bank("tensor", 1)
            for C_ in range(8):
                nst = 4 * C_ + 4
                par3 = C_ % 2
                po_, psm_ = (2, 3) if C_ % 2 == 0 else (4, 5)
                P.wait("tensor", s_p, cbm_mark[C_])
                wait_bank("tensor", po_)
                wait_bank("tensor", psm_)
                def emit_scores(j):
                    par = j % 2
                    diag = j >= 4 * C_
                    if exp_mark[par]:
                        P.wait("tensor", s_s, exp_mark[par])
                    P.op("tensor", lambda t, j=j, par=par, C_=C_: t.matmul(
                        ps[par][:], k_fm.ap()[:, j * 128:(j + 1) * 128],
                        q_fm.ap()[:, C_ * 512:(C_ + 1) * 512],
                        start=True, stop=False))
                    par3_ = C_ % 2
                    if diag:
                        m = j - 4 * C_
                        P.op("tensor", lambda t, par=par, m=m, par3_=par3_: t.matmul(
                            ps[par][:], eye_sb[:], cbm_sb[par3_].ap()[:, m * 512:(m + 1) * 512],
                            start=False, stop=True), inc=s_pe)
                    else:
                        P.op("tensor", lambda t, par=par, C_=C_: t.matmul(
                            ps[par][:], eye_sb[:],
                            cb_all.ap()[:, C_ * 512:(C_ + 1) * 512],
                            start=False, stop=True), inc=s_pe)
                    fold_mark[j] = s_pe.n

                def emit_exp(j):
                    par = j % 2
                    P.wait("scalar", s_pe, fold_mark[j])
                    if j >= 4:
                        P.wait("scalar", s_pe, pslot_rel[j - 4])
                    P.op("scalar", lambda s, j=j, par=par: s.activation(
                        p_slot.ap()[:, (j % 4) * 512:((j % 4) + 1) * 512],
                        ps[par][:], AF.Exp), inc=s_s)
                    exp_mark[par] = s_s.n

                def emit_av(j):
                    P.wait("tensor", s_s, exp_mark[j % 2])
                    P.op("tensor", lambda t, j=j, nst=nst, po_=po_: t.matmul(
                        ps[po_][:], v_tm.ap()[:, j * 128:(j + 1) * 128],
                        p_slot.ap()[:, (j % 4) * 512:((j % 4) + 1) * 512],
                        start=(j == 0), stop=(j == nst - 1)))
                    P.op("tensor", lambda t, j=j, nst=nst, psm_=psm_: t.matmul(
                        ps[psm_][:], ones_sb[:],
                        p_slot.ap()[:, (j % 4) * 512:((j % 4) + 1) * 512],
                        start=(j == 0), stop=(j == nst - 1)), inc=s_pe)
                    pslot_rel[j] = s_pe.n

                fold_mark = {}
                emit_scores(0)
                emit_exp(0)
                for j in range(nst):
                    if j + 1 < nst:
                        emit_scores(j + 1)
                        emit_exp(j + 1)
                    emit_av(j)
                    if j == 1 and C_ < 7:
                        emit_cbm(C_ + 1)
                cbm_used[C_] = s_pe.n
                P.wait("vector", s_pe)
                P.op("vector", lambda v, psm_=psm_: v.reciprocal(recip_sb[:], ps[psm_][:]))
                P.drain("vector")
                P.op("vector", lambda v, hl=hl, C_=C_, po_=po_: v.scalar_tensor_tensor(
                    o_fm[hl].ap()[:, C_ * 512:(C_ + 1) * 512], ps[po_][:],
                    INV_SQRT_E, recip_sb[:], ALU.mult, ALU.mult), inc=s_v)
                set_bank(po_, s_v)
                set_bank(psm_, s_v)
                pump_deq(1)
            if hl == 1:
                for tt_ in range(32):
                    for u in range(4):
                        pb = [4, 5, 6, 7][u]
                        wait_bank("tensor", pb)
                        P.op("tensor", lambda t, tt_=tt_, u=u, pb=pb: t.matmul(
                            ps[pb][:], o_fm[0].ap()[:, tt_ * 128:(tt_ + 1) * 128],
                            wp_sb.ap()[:, u * 512:u * 512 + 512],
                            start=True, stop=False))
                        P.op("tensor", lambda t, tt_=tt_, u=u, pb=pb: t.matmul(
                            ps[pb][:], o_fm[1].ap()[:, tt_ * 128:(tt_ + 1) * 128],
                            wp_sb.ap()[:, E + u * 512: E + u * 512 + 512],
                            start=False, stop=True), inc=s_pe)
                        P.wait("vector", s_pe)
                        P.wait("vector", s_pst[u])
                        P.op("vector", lambda v, u=u, pb=pb: v.tensor_copy(
                            pstage.ap()[:, u * 512:(u + 1) * 512], ps[pb][:]), inc=s_v)
                        set_bank(pb, s_v)
                        P.wait("gpsimd", s_v)
                        P.op("gpsimd", lambda g, tt_=tt_, u=u, tb=tb: g.dma_start(
                            out=contrib[tb + tt_ * 128: tb + tt_ * 128 + 128,
                                        u * 512:(u + 1) * 512],
                            in_=pstage.ap()[:, u * 512:(u + 1) * 512]), inc=s_pst[u], k=16)

        attn_done_g = s_g.n
        attn_done_pst = [s.n for s in s_pst]

        pump_deq_all2()
        deq_done = (s_wst[0].n, s_wst[1].n)

        if stage == "deq":
            P.wait("gpsimd", s_wst[0], deq_done[0])
            P.wait("gpsimd", s_wst[1], deq_done[1])
            P.op("gpsimd", lambda g: g.dma_start(out=dbg["wgT_out"], in_=wgT_d),
                 inc=s_g, k=16)
            P.op("gpsimd", lambda g: g.dma_start(out=dbg["wdT_out"], in_=wdT_d),
                 inc=s_g, k=16)
            P.wait("gpsimd", s_g)
            P.op("gpsimd", lambda g: g.dma_start(out=out[0], in_=xo[0]), inc=s_g, k=16)
            P.wait("gpsimd", s_g)
            P.replay()
            return nc, dbg


        if stage == "attn":
            P.wait("gpsimd", s_g, attn_done_g)
            P.wait("gpsimd", s_v)
            P.op("gpsimd", lambda g: g.dma_start(out=dbg["contrib_out"], in_=contrib),
                 inc=s_g, k=16)
            P.op("gpsimd", lambda g: g.dma_start(out=dbg["xq_out"], in_=xq_buf[0][:]),
                 inc=s_g, k=16)
            for nm, tsr in (("q_fm_out", q_fm), ("k_fm_out", k_fm), ("v_tm_out", v_tm),
                            ("o0_out", o_fm[0]), ("o1_out", o_fm[1])):
                P.op("gpsimd", lambda g, nm=nm, tsr=tsr: g.dma_start(
                    out=dbg[nm], in_=tsr[:]), inc=s_g, k=16)
            P.wait("gpsimd", s_g)
            P.op("gpsimd", lambda g: g.dma_start(out=out[0], in_=xo[0]), inc=s_g, k=16)
            P.wait("gpsimd", s_g)
            P.replay()
            return nc, dbg

        s_xf = [P.sem("s_xfA"), P.sem("s_xfB")]
        s_fin = P.sem("s_fin")
        s_out = P.sem("s_out")
        xf_rel = {}

        s_dst = [P.sem("s_dstA"), P.sem("s_dstB")]

        def emit_final(fc):
            P.wait("sync", s_cc, fc + 1)
            if fc > 0:
                P.wait("sync", s_out, 16 * fc)
            P.op("sync", lambda e, fc=fc: e.dma_start(
                out=fin_sb[:], in_=red[fc].rearrange("p (a c) -> (p a) c", a=4)),
                inc=s_fin, k=16)
            P.op("sync", lambda e, fc=fc: e.dma_start(
                out=xo_sb[:], in_=xo[fc]), inc=s_fin, k=16)
            P.wait("gpsimd", s_fin)
            P.drain("gpsimd")
            P.op("gpsimd", lambda g: g.tensor_add(fin_sb[:], fin_sb[:], xo_sb[:]))
            P.drain("gpsimd")
            P.op("gpsimd", lambda g: g.tensor_add(fin_sb[:], fin_sb[:],
                                                  bpb_sb.ap()[0:64, :]))
            P.drain("gpsimd")
            P.op("gpsimd", lambda g, fc=fc: g.dma_start(
                out=out[fc], in_=fin_sb[:]), inc=s_out, k=16)
        for hp in range(2):
            if hp == 0:
                P.wait("gpsimd", s_g, attn_done_g)
                for u in range(4):
                    P.wait("gpsimd", s_pst[u], attn_done_pst[u])
            else:
                P.wait("gpsimd", s_dst[0])
                P.wait("gpsimd", s_dst[1])
            for qn, (wsb_, wd_) in zip(("sync", "scalar", "gpsimd"),
                                       ((wgT_sb, wgT_d), (wuT_sb, wuT_d), (wdT_sb, wdT_d))):
                if hp == 0:
                    P.wait(qn, s_wst[0], deq_done[0])
                    P.wait(qn, s_wst[1], deq_done[1])
                    P.wait(qn, s_pe)
                    P.wait(qn, s_v)
                    P.wait(qn, s_s)
                elif wsb_ is wdT_sb:
                    P.wait(qn, s_pe)
                else:
                    P.wait(qn, s_pe, gateup_dead)
                if hp == 0 and wsb_ is wgT_sb:
                    P.op(qn, lambda e: e.dma_start(
                        out=wgT_sb.ap()[:, 12 * 1024:16 * 1024]
                            .rearrange("p (c f) -> p c f", c=4),
                        in_=wgT_d[0, 12:16].rearrange("c p f -> p c f")),
                        inc=s_wl, k=16)
                else:
                    P.op(qn, lambda e, wsb_=wsb_, wd_=wd_, hp=hp: e.dma_start(
                        out=wsb_.ap().rearrange("p (c f) -> p c f",
                                                c=wd_.shape[1]),
                        in_=wd_[hp].rearrange("c p f -> p c f")), inc=s_wl, k=16)
            wl_mark = s_wl.n
            if hp == 0:
                down_done = 0
            for tc in range(16):
                slot = tc % 2
                use = hp * 16 + tc
                if use >= 2:
                    P.wait("sync", s_pe, xf_rel[use - 2])
                for g4 in range(4):
                    P.op("sync", lambda e, g4=g4, slot=slot, tc=tc: e.dma_start(
                        out=xf_buf[slot].ap().rearrange("p (c f) -> p c f", c=EC)
                            [:, 4 * g4:4 * g4 + 4, :],
                        in_=xnf[4 * g4:4 * g4 + 4, :, tc * 512: tc * 512 + 512]
                            .rearrange("c p f -> p c f")), inc=s_xf[slot], k=16)
                P.wait("tensor", s_xf[slot])
                if tc == 0:
                    P.wait("tensor", s_wl, wl_mark)
                xf = xf_buf[slot]
                for nch in range(8):
                    pg, pu = (0, 1) if nch % 2 == 0 else (4, 5)
                    wait_bank("tensor", pg)
                    wait_bank("tensor", pu)
                    for ec in range(EC):
                        P.op("tensor", lambda t, ec=ec, nch=nch, xf=xf, pg=pg: t.matmul(
                            ps[pg][:],
                            wgT_sb.ap()[:, ec * 1024 + nch * 128: ec * 1024 + nch * 128 + 128],
                            xf.ap()[:, ec * 512:(ec + 1) * 512],
                            start=(ec == 0), stop=(ec == EC - 1)),
                            inc=(s_pe if ec == EC - 1 else None))
                    for ec in range(EC):
                        P.op("tensor", lambda t, ec=ec, nch=nch, xf=xf, pu=pu: t.matmul(
                            ps[pu][:],
                            wuT_sb.ap()[:, ec * 1024 + nch * 128: ec * 1024 + nch * 128 + 128],
                            xf.ap()[:, ec * 512:(ec + 1) * 512],
                            start=(ec == 0), stop=(ec == EC - 1)),
                            inc=(s_pe if ec == EC - 1 else None))
                    par2 = nch % 2
                    up_mark = s_pe.n
                    if hp == 0 and tc == 15 and nch == 7:
                        gateup_dead = s_pe.n
                    P.wait("scalar", s_pe, s_pe.n - 1)
                    P.op("scalar", lambda s, pg=pg, par2=par2: s.activation(
                        g_sq[par2][:], ps[pg][:], AF.Square), inc=s_s)
                    P.wait("vector", s_s)
                    P.op("vector", lambda v, par2=par2: v.tensor_scalar(
                        g_d[par2][:], g_sq[par2][:], GELU_A, 1.0, ALU.mult, ALU.add))
                    P.drain("vector")
                    P.op("vector", lambda v, pg=pg, par2=par2: v.tensor_mul(
                        g_e[par2][:], g_d[par2][:], ps[pg][:]), inc=s_v)
                    P.wait("scalar", s_v)
                    P.op("scalar", lambda s, par2=par2: s.activation(
                        g_sg[par2][:], g_e[par2][:], AF.Sigmoid, scale=GELU_S), inc=s_s)
                    P.wait("vector", s_s)
                    P.op("vector", lambda v, pg=pg, par2=par2: v.tensor_mul(
                        g_gs[par2][:], ps[pg][:], g_sg[par2][:]))
                    P.drain("vector")
                    P.wait("vector", s_pe, up_mark)
                    if nch == 0:
                        P.wait("vector", s_pe, down_done)
                    P.op("vector", lambda v, pu=pu, nch=nch, par2=par2: v.tensor_mul(
                        inter_sb.ap()[:, nch * 512:(nch + 1) * 512],
                        g_gs[par2][:], ps[pu][:]), inc=s_v)
                    set_bank(pg, s_v)
                    set_bank(pu, s_v)
                inter_ready = s_v.n
                xf_rel[use] = s_pe.n
                P.wait("tensor", s_v, inter_ready)
                for ts_ in range(4):
                    slot2 = (use * 4 + ts_) % 2
                    P.wait("vector", s_dst[slot2])
                    for hc in range(4):
                        pd = [2, 3, 6, 7][hc]
                        wait_bank("tensor", pd)
                        for nch in range(8):
                            P.op("tensor", lambda t, nch=nch, ts_=ts_, hc=hc, pd=pd: t.matmul(
                                ps[pd][:],
                                inter_sb.ap()[:, nch * 512 + ts_ * 128: nch * 512 + ts_ * 128 + 128],
                                wdT_sb.ap()[:, nch * 2048 + hc * 512: nch * 2048 + hc * 512 + 512],
                                start=(nch == 0), stop=(nch == 7)),
                                inc=(s_pe if nch == 7 else None))
                        P.wait("vector", s_pe)
                        P.op("vector", lambda v, hc=hc, pd=pd, slot2=slot2: v.tensor_copy(
                            dstage[slot2].ap()[:, hc * 512:(hc + 1) * 512], ps[pd][:]),
                            inc=s_v)
                        set_bank(pd, s_v)
                    P.wait("gpsimd", s_v)
                    P.op("gpsimd", lambda g, tc=tc, ts_=ts_, slot2=slot2: g.dma_start(
                        out=contrib[tc * 512 + ts_ * 128: tc * 512 + ts_ * 128 + 128, :],
                        in_=dstage[slot2][:], accum_op=ALU.add), inc=s_dst[slot2], k=16)
                down_done = s_pe.n
                if hp == 1 and stage == "full":
                    P.wait("gpsimd", s_dst[0])
                    P.wait("gpsimd", s_dst[1])
                    P.op("gpsimd", lambda g, tc=tc: g.collective_compute(
                        "ReduceScatter", ALU.add,
                        replica_groups=[list(range(8))],
                        ins=[contrib[tc * 512:(tc + 1) * 512, :]
                             .rearrange("(p a) c -> p (a c)", p=128).opt()],
                        outs=[red[tc].opt()]), inc=s_cc, k=1)
                    if tc >= 1:
                        emit_final(tc - 1)

        if stage == "full":
            emit_final(15)

        if stage == "ffn":
            P.wait("gpsimd", s_g)
            P.op("gpsimd", lambda g: g.dma_start(out=dbg["contrib_out"], in_=contrib),
                 inc=s_g, k=16)
            P.wait("gpsimd", s_g)
            P.op("gpsimd", lambda g: g.dma_start(out=out[0], in_=xo[0]), inc=s_g, k=16)
            P.wait("gpsimd", s_g)
        else:
            P.wait("gpsimd", s_out, 16 * 16)

        P.replay()
    return nc, dbg



def make_rope_R(c):
    hd = 128
    freqs = 10000.0 ** (-(np.arange(0, hd, 2, dtype=np.float64) / hd))
    Rs = np.zeros((2, hd, hd), dtype=np.float32)
    for i, h in enumerate((2 * c, 2 * c + 1)):
        ang = h * freqs
        co, si = np.cos(ang), np.sin(ang)
        for j in range(hd // 2):
            Rs[i, 2 * j, 2 * j] = co[j]
            Rs[i, 2 * j + 1, 2 * j] = -si[j]
            Rs[i, 2 * j, 2 * j + 1] = si[j]
            Rs[i, 2 * j + 1, 2 * j + 1] = co[j]
    return Rs


def make_in_maps(L0_attn_x, L0_attn_w_attn, L0_attn_b_attn, L0_attn_w_proj,
                 L0_attn_b_proj, L0_ffn_gate_W_packed, L0_ffn_gate_scales,
                 L0_ffn_up_W_packed, L0_ffn_up_scales, L0_ffn_down_W_packed,
                 L0_ffn_down_scales):
    x = np.ascontiguousarray(np.asarray(L0_attn_x, dtype=np.float32)).reshape(TT, E)
    wa = np.asarray(L0_attn_w_attn, dtype=np.float32)
    ba = np.asarray(L0_attn_b_attn, dtype=np.float32)
    wp = np.asarray(L0_attn_w_proj, dtype=np.float32)
    bp = np.asarray(L0_attn_b_proj, dtype=np.float32)
    gpk = np.asarray(L0_ffn_gate_W_packed, dtype=np.int32)
    gsc = np.asarray(L0_ffn_gate_scales, dtype=np.float32)
    upk = np.asarray(L0_ffn_up_W_packed, dtype=np.int32)
    usc = np.asarray(L0_ffn_up_scales, dtype=np.float32)
    dpk = np.asarray(L0_ffn_down_W_packed, dtype=np.int32)
    dsc = np.asarray(L0_ffn_down_scales, dtype=np.float32)
    x16 = x.reshape(16, 8, 64, E)
    bpb = np.ascontiguousarray(np.broadcast_to(bp[None, :], (128, E)))
    in_maps = []
    for c in range(8):
        h0 = 2 * c * 128
        bqk = np.stack([ba[h0:h0 + 128], ba[h0 + 128:h0 + 256],
                        ba[E + h0:E + h0 + 128], ba[E + h0 + 128:E + h0 + 256]],
                       axis=1)
        bv = ba[2 * E + h0:2 * E + h0 + 256]
        in_maps.append(dict(
            xg=x,
            xo=np.ascontiguousarray(x16[:, c]),
            wq_sl=np.ascontiguousarray(wa[:, h0:h0 + 256]),
            wk_sl=np.ascontiguousarray(wa[:, E + h0:E + h0 + 256]),
            wv_sl=np.ascontiguousarray(wa[:, 2 * E + h0:2 * E + h0 + 256]),
            wp_sl=np.ascontiguousarray(wp[2 * c * 128:2 * c * 128 + 256, :]),
            bqk=np.ascontiguousarray(bqk),
            bvb=np.ascontiguousarray(np.broadcast_to(bv[None, :], (128, 256))),
            bpb=bpb,
            rope_R=make_rope_R(c),
            gate_pk=np.ascontiguousarray(gpk[NS * c:NS * (c + 1)]),
            gate_sc=np.ascontiguousarray(gsc[NS * c:NS * (c + 1)]),
            up_pk=np.ascontiguousarray(upk[NS * c:NS * (c + 1)]),
            up_sc=np.ascontiguousarray(usc[NS * c:NS * (c + 1)]),
            down_pk=np.ascontiguousarray(dpk[:, 1024 * c:1024 * (c + 1)]),
            down_sc=np.ascontiguousarray(dsc[:, 64 * c:64 * (c + 1)]),
        ))
    return in_maps


def assemble_output(results):
    full = np.empty((16, 8, 64, E), dtype=np.float32)
    for c in range(8):
        full[:, c] = np.asarray(results[c]["out_rows"])
    return full.reshape(B, T, E)


def kernel(**inputs):
    nc, _dbg = build_program("full")
    in_maps = make_in_maps(**inputs)
    res = run_bass_kernel_spmd(nc, in_maps, core_ids=list(range(8)))
    return assemble_output(res.results)


# version 29
# speedup vs baseline: 1.0200x; 1.0082x over previous
import numpy as np
import ml_dtypes
from contextlib import ExitStack

import concourse.bass as bass
import concourse.mybir as mybir
from concourse.bass_utils import run_bass_kernel_spmd

F32 = mybir.dt.float32
BF16 = mybir.dt.bfloat16
I32 = mybir.dt.int32
I16 = mybir.dt.int16
ALU = mybir.AluOpType
AF = mybir.ActivationFunctionType
AX = mybir.AxisListType.X

B, T, E = 2, 4096, 2048
TT = B * T
EC = E // 128
NS = 2048
GELU_A = 0.044715
GELU_S = 2.0 * 0.7978845608028654
INV_SQRT_E = float(E) ** -0.5
NEG_BIG = -1e30
NP_BF16 = ml_dtypes.bfloat16
KB = 1024


class Sem:
    __slots__ = ("h", "n")

    def __init__(self, h):
        self.h = h
        self.n = 0


class Prog:
    def __init__(self, nc, ctx):
        self.nc = nc
        self.ctx = ctx
        self.sched = []

    def sem(self, name):
        return Sem(self.ctx.enter_context(self.nc.semaphore(name)))

    def op(self, eng, fn, inc=None, k=1):
        if inc is not None:
            inc.n += k
            def wrapped(engine, fn=fn, s=inc.h, k=k):
                fn(engine).then_inc(s, k)
            self.sched.append((eng, wrapped))
        else:
            self.sched.append((eng, fn))

    def wait(self, eng, sem, val=None):
        v = sem.n if val is None else val
        assert 0 <= v <= sem.n, f"wait {v} > scheduled {sem.n}"
        if v > 0:
            self.sched.append((eng, lambda engine, s=sem.h, v=v: engine.wait_ge(s, v)))

    def drain(self, eng, inc=None, k=1):
        if inc is not None:
            inc.n += k
            self.sched.append((eng, lambda engine, s=inc.h, k=k:
                               engine.drain(fusable=False).then_inc(s, k)))
        else:
            self.sched.append((eng, lambda engine: engine.drain()))

    def replay(self):
        with self.nc.Block() as blk:
            for name in ("sync", "scalar", "tensor", "vector", "gpsimd"):
                ops = [fn for e, fn in self.sched if e == name]
                def body(engine, ops=ops):
                    for fn in ops:
                        fn(engine)
                getattr(blk, name)(body)


def build_program(stage="full"):
    nc = bass.Bass("TRN2", target_bir_lowering=False)

    xg = nc.dram_tensor("xg", [TT, E], F32, kind="ExternalInput").ap()
    xo = nc.dram_tensor("xo", [16, 64, E], F32, kind="ExternalInput").ap()
    wq_in = nc.dram_tensor("wq_sl", [E, 256], F32, kind="ExternalInput").ap()
    wk_in = nc.dram_tensor("wk_sl", [E, 256], F32, kind="ExternalInput").ap()
    wv_in = nc.dram_tensor("wv_sl", [E, 256], F32, kind="ExternalInput").ap()
    wp_in = nc.dram_tensor("wp_sl", [256, E], F32, kind="ExternalInput").ap()
    bqk_in = nc.dram_tensor("bqk", [128, 4], F32, kind="ExternalInput").ap()
    bvb_in = nc.dram_tensor("bvb", [128, 256], F32, kind="ExternalInput").ap()
    bpb_in = nc.dram_tensor("bpb", [128, E], F32, kind="ExternalInput").ap()
    rope_in = nc.dram_tensor("rope_R", [2, 128, 128], F32, kind="ExternalInput").ap()
    gpk_in = nc.dram_tensor("gate_pk", [NS, 1024], I32, kind="ExternalInput").ap()
    gsc_in = nc.dram_tensor("gate_sc", [NS, 64], F32, kind="ExternalInput").ap()
    upk_in = nc.dram_tensor("up_pk", [NS, 1024], I32, kind="ExternalInput").ap()
    usc_in = nc.dram_tensor("up_sc", [NS, 64], F32, kind="ExternalInput").ap()
    dpk_in = nc.dram_tensor("down_pk", [E, 1024], I32, kind="ExternalInput").ap()
    dsc_in = nc.dram_tensor("down_sc", [E, 64], F32, kind="ExternalInput").ap()
    out = nc.dram_tensor("out_rows", [16, 64, E], F32, kind="ExternalOutput").ap()

    eye_np = np.eye(128, dtype=np.float32).astype(NP_BF16)
    eye_d = nc.inline_tensor(eye_np, name="eye_c").ap()
    ones_d = nc.inline_tensor(np.ones((128, 128), np.float32).astype(NP_BF16), name="ones_c").ap()
    c128_d = nc.inline_tensor(np.full((1, 128), 1 / 128, np.float32).astype(NP_BF16), name="c128_c").ap()
    ones1_d = nc.inline_tensor(np.ones((1, 128), np.float32).astype(NP_BF16), name="ones1_c").ap()
    mv = np.zeros((4, 128, 512), np.float32)
    pp = np.arange(128)[:, None]
    ff = np.arange(512)[None, :]
    for m in range(4):
        mv[m] = np.where(m * 128 + pp <= ff, 0.0, NEG_BIG)
    mask_d = nc.inline_tensor(mv.astype(NP_BF16), name="mask_c").ap()

    xnf = nc.dram_tensor("xnf", [EC, 128, TT], BF16, kind="Internal").ap()
    contrib = nc.dram_tensor("contrib", [TT, E], F32, kind="Internal").ap()
    wgT_d = nc.dram_tensor("wgT_d", [2, EC, 128, 1024], BF16, kind="Internal").ap()
    wuT_d = nc.dram_tensor("wuT_d", [2, EC, 128, 1024], BF16, kind="Internal").ap()
    wdT_d = nc.dram_tensor("wdT_d", [2, 8, 128, E], BF16, kind="Internal").ap()
    red = [nc.dram_tensor(f"red{j}", [16, 4 * E], F32, kind="Internal").ap() for j in range(16)]

    dbg = {}
    if stage in ("ln",):
        dbg["xnf_out"] = nc.dram_tensor("xnf_out", [EC, 128, TT], BF16, kind="ExternalOutput").ap()
    if stage in ("attn",):
        dbg["contrib_out"] = nc.dram_tensor("contrib_out", [TT, E], F32, kind="ExternalOutput").ap()
        for nm in ("q_fm_out", "k_fm_out", "v_tm_out", "o0_out", "o1_out"):
            dbg[nm] = nc.dram_tensor(nm, [128, T], BF16, kind="ExternalOutput").ap()
        dbg["xq_out"] = nc.dram_tensor("xq_out", [128, EC * 512], BF16, kind="ExternalOutput").ap()
    if stage in ("deq",):
        dbg["wgT_out"] = nc.dram_tensor("wgT_out", [2, EC, 128, 1024], BF16, kind="ExternalOutput").ap()
        dbg["wdT_out"] = nc.dram_tensor("wdT_out", [2, 8, 128, E], BF16, kind="ExternalOutput").ap()
    if stage in ("ffn",):
        dbg["contrib_out"] = nc.dram_tensor("contrib_out", [TT, E], F32, kind="ExternalOutput").ap()

    alloc = nc.alloc_sbuf_tensor_at
    BASE = 16512
    TOP = 229376
    _cur = {"A": BASE, "C": BASE + 16 * KB, "E": BASE + 48 * KB,
            "B": BASE + 103 * KB, "D": BASE + 103 * KB,
            "F": BASE + 16 * KB, "G": BASE + 144 * KB,
            "B2": BASE + 119 * KB, "B3": BASE + 103 * KB}
    _lim = {"A": BASE + 16 * KB, "C": BASE + 48 * KB, "E": BASE + 103 * KB,
            "B": BASE + 168 * KB, "D": TOP, "F": BASE + 144 * KB, "G": TOP,
            "B2": BASE + 151 * KB, "B3": BASE + 119 * KB}

    def sb(name, shape, dtype, region, cur=None):
        cur = _cur
        nbytes = int(np.prod(shape[1:])) * mybir.dt.size(dtype)
        o = (cur[region] + 31) & ~31
        cur[region] = o + nbytes
        assert cur[region] <= _lim[region], \
            f"{name}: region {region} overflow {cur[region]} > {_lim[region]}"
        return alloc(name, shape, dtype, offset=o)

    eye_sb = sb("eye_sb", [128, 128], BF16, "A")
    ones_sb = sb("ones_sb", [128, 128], BF16, "A")
    c128_sb = sb("c128_sb", [1, 128], BF16, "A")
    ones1_sb = sb("ones1_sb", [1, 128], BF16, "A")
    mask_sb = sb("mask_sb", [128, 4 * 512], BF16, "A")
    rope_sb = sb("rope_sb", [128, 2 * 128], BF16, "A")
    bqk_sb = sb("bqk_sb", [128, 4], F32, "A")
    bvb_sb = sb("bvb_sb", [128, 256], BF16, "A")
    bpb_sb = sb("bpb_sb", [128, E], F32, "A")

    wq_sb = sb("wq_sb", [128, EC * 256], BF16, "C")
    wk_sb = sb("wk_sb", [128, EC * 256], BF16, "C")
    wv_sb = sb("wv_sb", [128, EC * 256], BF16, "C")
    wp_sb = sb("wp_sb", [128, 2 * E], BF16, "C")

    pk_sb = [sb(f"pk_sb{i}", [128, 1024], I32, "E") for i in range(2)]
    sc_sb = [sb(f"sc_sb{i}", [128, 64], F32, "E") for i in range(2)]
    wT_st = [sb(f"wT_st{i}", [128, 2048], BF16, "E") for i in range(2)]
    dq_b16 = sb("dq_b16", [128, 1024], I16, "E")
    dq_c16 = sb("dq_c16", [128, 2048], I16, "E")
    dq_k6 = sb("dq_k6", [128, 2048], I16, "E")
    dq_t12 = sb("dq_t12", [128, 2048], I16, "E")
    dq_pat = sb("dq_pat", [128, 2048], I16, "E")
    dq_mbf = sb("dq_mbf", [128, 2048], BF16, "E")
    dq_q16 = sb("dq_q16", [128, 2048], I16, "E")
    dq_corr = sb("dq_corr", [128, 2048], BF16, "E")
    dq_w = [sb(f"dq_w{i}", [128, 2048], BF16, "E") for i in range(2)]

    stg_f = sb("stg_f", [128, EC * 256], F32, "B")
    x_sb = [sb(f"x_sb{i}", [128, 4 * E], BF16, "B") for i in range(2)]
    xn_tm = [sb(f"xn_tm{i}", [128, E], BF16, "B") for i in range(2)]
    xnT_st = ([sb(f"xnT_st{i}", [128, E], BF16, "B") for i in range(2)]
              + [sb(f"xnT_st{i + 2}", [128, E], BF16, "B3") for i in range(2)])
    stats_sb = sb("stats_sb", [128, 96], F32, "B")
    ssum_sb = sb("ssum_sb", [128, 4], F32, "B")
    ssq_sb = sb("ssq_sb", [128, 4], F32, "B")
    sqdump_sb = sb("sqdump_sb", [128, E], F32, "B3")
    mean_sb = sb("mean_sb", [128, 4], F32, "B")
    mv_sb = sb("mv_sb", [128, 8], F32, "B")
    rstd_sb = sb("rstd_sb", [128, 4], F32, "B")
    tmp1_sb = sb("tmp1_sb", [128, 4], F32, "B")

    xq_buf = [sb(f"xq_buf{i}", [128, EC * 512], BF16, "D") for i in range(2)]
    q_fm = sb("q_fm", [128, T], BF16, "D")
    k_fm = sb("k_fm", [128, T], BF16, "D")
    v_tm = sb("v_tm", [128, T], BF16, "D")
    o_fm = [sb(f"o_fm{i}", [128, T], BF16, "D") for i in range(2)]
    p_slot = sb("p_slot", [128, 4 * 512], BF16, "D")
    qraw_sb = p_slot
    negmax_sb = sb("negmax_sb", [128, 32], BF16, "D")
    negc_sb = sb("negc_sb", [1, 512], BF16, "D")
    cb_all = sb("cb_all", [128, 8 * 512], BF16, "D")
    cbm_sb = [sb(f"cbm_sb{i}", [128, 4 * 512], BF16, "D") for i in range(2)]
    recip_sb = sb("recip_sb", [128, 512], F32, "D")
    pstage = sb("pstage", [128, E], F32, "D")

    wgT_sb = sb("wgT_sb", [128, EC * 1024], BF16, "F")
    wuT_sb = sb("wuT_sb", [128, EC * 1024], BF16, "F")
    wdT_sb = sb("wdT_sb", [128, 8 * E], BF16, "F")
    xf_buf = [sb(f"xf_buf{i}", [128, EC * 512], BF16, "F") for i in range(2)]
    inter_sb = sb("inter_sb", [128, 8 * 512], BF16, "G")
    g_sq = [sb(f"g_sq{i}", [128, 512], F32, "G") for i in range(2)]
    g_d = [sb(f"g_d{i}", [128, 512], F32, "G") for i in range(2)]
    g_e = [sb(f"g_e{i}", [128, 512], F32, "G") for i in range(2)]
    g_sg = [sb(f"g_sg{i}", [128, 512], F32, "G") for i in range(2)]
    g_gs = [sb(f"g_gs{i}", [128, 512], F32, "G") for i in range(2)]
    dstage = [sb(f"dstage{i}", [128, E], F32, "G") for i in range(2)]

    fin_sb = sb("fin_sb", [64, E], F32, "G")
    xo_sb = sb("xo_sb", [64, E], F32, "G")

    ctx = ExitStack()
    with ctx:
        ps = [ctx.enter_context(nc.psum_tensor(f"ps{i}", [128, 512], F32)) for i in range(8)]
        P = Prog(nc, ctx)

        bankmark = {i: (None, 0) for i in range(8)}

        def wait_bank(eng, pb):
            sem, val = bankmark[pb]
            if sem is not None:
                P.wait(eng, sem, val)

        def set_bank(pb, sem):
            bankmark[pb] = (sem, sem.n)

        s_ld = P.sem("s_ld")
        s_x = [P.sem("s_x0"), P.sem("s_x1")]
        s_xw2 = [P.sem(f"s_xw{i}") for i in range(4)]
        s_xw = P.sem("s_xw")
        s_pk = P.sem("s_pk")
        s_wT = P.sem("s_wT")
        s_pe = P.sem("s_pe")
        s_v = P.sem("s_v")
        s_s = P.sem("s_s")
        s_g = P.sem("s_g")
        s_cc = P.sem("s_cc")

        s_stg = P.sem("s_stg")

        def syncdma(fn):
            P.op("sync", fn, inc=s_ld, k=16)

        syncdma(lambda e: e.dma_start(out=eye_sb[:], in_=eye_d))
        syncdma(lambda e: e.dma_start(out=ones_sb[:], in_=ones_d))
        syncdma(lambda e: e.dma_start(out=c128_sb[:], in_=c128_d))
        syncdma(lambda e: e.dma_start(out=ones1_sb[:], in_=ones1_d))
        syncdma(lambda e: e.dma_start(out=mask_sb.ap().rearrange("p (m f) -> p m f", m=4),
                                      in_=mask_d.rearrange("m p f -> p m f")))
        syncdma(lambda e: e.dma_start(out=bqk_sb[:], in_=bqk_in))
        syncdma(lambda e: e.dma_start(out=bpb_sb[:], in_=bpb_in))
        ld_all = s_ld.n

        P.op("sync", lambda e: e.dma_start(out=stg_f[:, 0:256], in_=bvb_in),
             inc=s_stg, k=16)
        P.op("sync", lambda e: e.dma_start(out=stg_f[:, 256:384], in_=rope_in[0]),
             inc=s_stg, k=16)
        P.op("sync", lambda e: e.dma_start(out=stg_f[:, 384:512], in_=rope_in[1]),
             inc=s_stg, k=16)
        P.wait("vector", s_stg)
        P.op("vector", lambda v: v.tensor_scalar(bvb_sb[:], stg_f[:, 0:256],
                                                 1.0 / 128.0, None, ALU.mult))
        P.op("vector", lambda v: v.tensor_copy(rope_sb[:], stg_f[:, 256:512]), inc=s_v)
        P.wait("sync", s_v)
        for win, wsb in ((wq_in, wq_sb), (wk_in, wk_sb), (wv_in, wv_sb)):
            P.op("sync", lambda e, win=win: e.dma_start(
                out=stg_f.ap().rearrange("p (c d) -> p c d", d=256),
                in_=win.rearrange("(c p) d -> p c d", p=128)), inc=s_stg, k=16)
            P.wait("vector", s_stg)
            P.op("vector", lambda v, wsb=wsb: v.tensor_copy(wsb[:], stg_f[:]), inc=s_v)
            P.wait("sync", s_v)
        P.op("sync", lambda e: e.dma_start(
            out=stg_f.ap()[:, 0:2 * E].rearrange("p (c u) -> p c u", c=2),
            in_=wp_in.rearrange("(c p) u -> p c u", p=128)), inc=s_stg, k=16)
        P.wait("vector", s_stg)
        P.op("vector", lambda v: v.tensor_copy(wp_sb[:], stg_f[:, 0:2 * E]), inc=s_v)
        P.wait("sync", s_v)

        s_pk2 = [P.sem("s_pkA"), P.sem("s_pkB")]
        s_wst = [P.sem("s_wstA"), P.sem("s_wstB")]
        s_p = P.sem("s_p")
        ps7_bf = ps[7].ap().bitcast(BF16)
        deq_rel = {}
        wst_rel = {}
        deq_pe_mark = {}

        def deq_params(widx, half, gi):
            pkin, scin = [(gpk_in, gsc_in), (upk_in, usc_in), (dpk_in, dsc_in)][widx]
            nb = 1024 if widx < 2 else 512
            return pkin, scin, nb

        def emit_deq_load(gidx):
            widx, half, gi = deq_queue[gidx]
            pkin, scin, nb = deq_params(widx, half, gi)
            slot = gidx % 2
            if gidx >= 2:
                se_, sv_ = deq_rel[gidx - 2]
                P.wait("gpsimd", se_, sv_)
            if widx < 2:
                r0 = half * 1024 + gi * 128
                P.op("gpsimd", lambda g, pkin=pkin, r0=r0, slot=slot: g.dma_start(
                    out=pk_sb[slot][:], in_=pkin[r0:r0 + 128, :]), inc=s_pk2[slot], k=16)
                P.op("gpsimd", lambda g, scin=scin, r0=r0, slot=slot: g.dma_start(
                    out=sc_sb[slot][:], in_=scin[r0:r0 + 128, :]), inc=s_pk2[slot], k=16)
            else:
                r0 = gi * 128
                c0 = half * 512
                P.op("gpsimd", lambda g, r0=r0, c0=c0, slot=slot: g.dma_start(
                    out=pk_sb[slot][:, 0:512], in_=dpk_in[r0:r0 + 128, c0:c0 + 512]),
                    inc=s_pk2[slot], k=16)
                P.op("gpsimd", lambda g, r0=r0, c0=c0, slot=slot: g.dma_start(
                    out=sc_sb[slot][:, 0:32], in_=dsc_in[r0:r0 + 128, c0 // 16: c0 // 16 + 32]),
                    inc=s_pk2[slot], k=16)

        def emit_deq_decode(gidx):
            widx, half, gi = deq_queue[gidx]
            _, _, nb = deq_params(widx, half, gi)
            eng = "vector"
            s_e = s_v
            slot = gidx % 2
            ncd = 2 * nb
            nsc = ncd // 32
            P.wait(eng, s_pk2[slot])
            P.drain(eng)
            if gidx >= 2:
                pse_, pv_ = deq_w_free[gidx - 2]
                P.wait(eng, pse_, pv_)
            P.op(eng, lambda v, slot=slot, nb=nb: v.tensor_copy(
                dq_b16[:, 0:nb], pk_sb[slot][:, 0:nb]))
            P.drain(eng)
            P.op(eng, lambda v, nb=nb: v.tensor_scalar(
                dq_c16[:, 0:nb], dq_b16[:, 0:nb], 15, None, ALU.bitwise_and))
            P.op(eng, lambda v, nb=nb: v.tensor_scalar(
                dq_c16[:, nb:2 * nb], dq_b16[:, 0:nb], 4, 15,
                ALU.logical_shift_right, ALU.bitwise_and))
            P.drain(eng)
            P.op(eng, lambda v, ncd=ncd: v.tensor_scalar(
                dq_k6[:, 0:ncd], dq_c16[:, 0:ncd], 7, 6,
                ALU.bitwise_and, ALU.logical_shift_left))
            P.op(eng, lambda v, ncd=ncd: v.tensor_scalar(
                dq_t12[:, 0:ncd], dq_c16[:, 0:ncd], 8, 12,
                ALU.bitwise_and, ALU.logical_shift_left))
            P.drain(eng)
            P.op(eng, lambda v, ncd=ncd: v.tensor_add(
                dq_pat[:, 0:ncd], dq_k6[:, 0:ncd], dq_t12[:, 0:ncd]))
            P.drain(eng)
            P.op(eng, lambda v, ncd=ncd: v.tensor_scalar(
                dq_pat[:, 0:ncd], dq_pat[:, 0:ncd], 0x3F00, None, ALU.add))
            P.drain(eng)
            P.op(eng, lambda v, ncd=ncd: v.tensor_scalar(
                dq_c16[:, 0:ncd], dq_k6[:, 0:ncd], -1, 0x0080, ALU.mult, ALU.add))
            P.drain(eng)
            P.op(eng, lambda v, ncd=ncd: v.tensor_scalar_max(
                dq_mbf[:, 0:ncd], dq_c16[:, 0:ncd], 0))
            P.op(eng, lambda v, ncd=ncd: v.tensor_scalar(
                dq_q16[:, 0:ncd], dq_t12[:, 0:ncd], 0x3B80, None, ALU.add))
            P.drain(eng)
            P.op(eng, lambda v, ncd=ncd: v.tensor_mul(
                dq_corr[:, 0:ncd], dq_mbf[:, 0:ncd],
                dq_q16.ap()[:, 0:ncd].bitcast(BF16)))
            P.drain(eng)
            P.op(eng, lambda v, ncd=ncd: v.tensor_sub(
                dq_mbf[:, 0:ncd], dq_pat.ap()[:, 0:ncd].bitcast(BF16),
                dq_corr[:, 0:ncd]))
            P.drain(eng)
            P.op(eng, lambda v, ncd=ncd, nb=nb, nsc=nsc, slot=slot: v.tensor_tensor(
                dq_w[slot].ap()[:, 0:ncd].rearrange("p (s x h) -> p h s x", h=2, x=16),
                dq_mbf.ap()[:, 0:ncd].rearrange("p (h s x) -> p h s x", h=2, x=16),
                sc_sb[slot].ap()[:, 0:nsc].rearrange("p (o s x) -> p o s x", o=1, x=1)
                    .broadcast_to([128, 2, nsc, 16]),
                ALU.mult), inc=s_e)
            deq_rel[gidx] = (s_e, s_e.n)
            deq_dec[gidx] = (s_e, s_e.n)

        def emit_deq_transpose(gidx):
            widx, half, gi = deq_queue[gidx]
            _, _, nb = deq_params(widx, half, gi)
            slot = gidx % 2
            nt = (2 * nb) // 128
            de_, dv_ = deq_dec[gidx]
            if gidx >= 2:
                P.wait("vector", s_wst[slot], wst_rel[gidx - 2])
            for rnd in range((nt + 7) // 8):
                k8 = min(8, nt - rnd * 8)
                P.wait("tensor", de_, dv_)
                wait_bank("tensor", 7)
                for i in range(k8):
                    ecx = rnd * 8 + i
                    P.op("tensor", lambda t, ecx=ecx, i=i, slot=slot: t.transpose(
                        ps7_bf[:, i * 128:(i + 1) * 128],
                        dq_w[slot].ap()[:, ecx * 128:(ecx + 1) * 128], eye_sb[:]),
                        inc=(s_pe if i == k8 - 1 else None))
                P.wait("vector", s_pe)
                P.op("vector", lambda v, rnd=rnd, k8=k8, slot=slot: v.tensor_copy(
                    wT_st[slot][:, rnd * 1024: rnd * 1024 + k8 * 128],
                    ps7_bf[:, 0:k8 * 128]), inc=s_v)
                set_bank(7, s_v)
            deq_w_free[gidx] = (s_pe, s_pe.n)
            P.wait("gpsimd", s_v, s_v.n)
            wout = [wgT_d, wuT_d, wdT_d][widx]
            P.op("gpsimd", lambda e, wout=wout, half=half, gi=gi, nt=nt, slot=slot:
                 e.dma_start(
                     out=wout[half, 0:nt, :, gi * 128:(gi + 1) * 128]
                         .rearrange("c p f -> p c f"),
                     in_=wT_st[slot].ap()[:, 0:nt * 128]
                         .rearrange("p (c f) -> p c f", f=128)),
                 inc=s_wst[slot], k=16)
            wst_rel[gidx] = s_wst[slot].n

        deq_queue = []
        for half in range(2):
            for widx in range(3):
                for gi in range(8 if widx < 2 else 16):
                    deq_queue.append((widx, half, gi))
        deq_dec = {}
        deq_w_free = {}
        deq_state = {"ld": 0, "dc": 0, "tr": 0}

        def pump_deq(k):
            nq = len(deq_queue)
            for _ in range(k):
                while deq_state["ld"] < min(nq, deq_state["dc"] + 2):
                    emit_deq_load(deq_state["ld"])
                    deq_state["ld"] += 1
                if deq_state["tr"] < deq_state["dc"] - 1:
                    emit_deq_transpose(deq_state["tr"])
                    deq_state["tr"] += 1
                if deq_state["dc"] < deq_state["ld"]:
                    emit_deq_decode(deq_state["dc"])
                    deq_state["dc"] += 1

        def pump_deq_all2():
            nq = len(deq_queue)
            while deq_state["dc"] < nq:
                pump_deq(1)
            while deq_state["tr"] < nq:
                emit_deq_transpose(deq_state["tr"])
                deq_state["tr"] += 1

        P.wait("tensor", s_ld, ld_all)
        P.wait("scalar", s_v)
        sv_ln = {}
        tpm = {0: 0, 1: 0}
        for ch in range(16):
            buf = ch % 2
            t0 = ch * 512
            if ch >= 2:
                P.wait("gpsimd", s_v, sv_ln[ch - 2])
            P.op("gpsimd", lambda e, t0=t0, buf=buf: e.dma_start(
                out=x_sb[buf].ap().rearrange("p (s e) -> p s e", s=4),
                in_=xg[t0:t0 + 512, :].rearrange("(s p) e -> p s e", p=128)),
                inc=s_x[buf], k=16)
            P.wait("vector", s_x[buf])
            P.drain("vector")
            for st_ in range(4):
                xs = x_sb[buf].ap()[:, st_ * E:(st_ + 1) * E]
                for i in range(4):
                    P.op("vector", lambda v, xs=xs, i=i, st_=st_: v.bn_stats(
                        stats_sb[:, st_ * 24 + i * 6: st_ * 24 + (i + 1) * 6],
                        xs[:, i * 512:(i + 1) * 512]))
            P.drain("vector")
            for st_ in range(4):
                P.op("vector", lambda v, st_=st_: v.bn_aggr(
                    mean_sb.ap().rearrange("p (s x) -> p s x", x=1)[:, st_:st_ + 1, 0]
                    if False else
                    mv_sb[:, st_ * 2: st_ * 2 + 2], stats_sb[:, st_ * 24: st_ * 24 + 24]))
            P.drain("vector")
            P.op("vector", lambda v: v.tensor_scalar(
                tmp1_sb[:], mv_sb.ap().rearrange("p (s x) -> p s x", x=2)[:, :, 1:2],
                1e-5, None, ALU.add), inc=s_v)
            P.wait("scalar", s_v)
            P.drain("scalar")
            P.op("scalar", lambda s: s.activation(tmp1_sb[:], tmp1_sb[:], AF.Sqrt),
                 inc=s_s)
            P.wait("vector", s_s)
            P.op("vector", lambda v: v.reciprocal(rstd_sb[:], tmp1_sb[:]))
            P.drain("vector")
            for st_ in range(4):
                xs = x_sb[buf].ap()[:, st_ * E:(st_ + 1) * E]
                tmb = xn_tm[st_ % 2]
                P.wait("vector", s_pe, tpm[st_ % 2])
                P.op("vector", lambda v, xs=xs, tmb=tmb, st_=st_: v.tensor_scalar(
                    tmb[:], xs, mv_sb.ap()[:, st_ * 2: st_ * 2 + 1],
                    rstd_sb.ap()[:, st_:st_ + 1], ALU.subtract, ALU.mult),
                    inc=s_v)
                if st_ == 3:
                    sv_ln[ch] = s_v.n
                nm_mark = s_v.n
                for rnd in range(2):
                    pb = 6 if rnd == 0 else 5
                    pv = ps[pb].ap().bitcast(BF16)
                    P.wait("tensor", s_v, nm_mark)
                    wait_bank("tensor", pb)
                    for i in range(8):
                        ecx = rnd * 8 + i
                        P.op("tensor", lambda t, tmb=tmb, ecx=ecx, i=i, pv=pv: t.transpose(
                            pv[:, i * 128:(i + 1) * 128],
                            tmb[:, ecx * 128:(ecx + 1) * 128], eye_sb[:]),
                            inc=(s_pe if i == 7 else None))
                    P.wait("scalar", s_pe)
                    if rnd == 0:
                        P.wait("scalar", s_xw2[st_ % 4])
                    P.op("scalar", lambda s, rnd=rnd, st_=st_, pv=pv: s.copy(
                        xnT_st[st_ % 4][:, rnd * 1024:(rnd + 1) * 1024], pv[:]), inc=s_s)
                    set_bank(pb, s_s)
                tpm[st_ % 2] = s_pe.n
                P.wait("sync", s_s, s_s.n)
                P.op("sync", lambda e, st_=st_, t0=t0: e.dma_start(
                    out=xnf[0:16, :, t0 + st_ * 128: t0 + st_ * 128 + 128]
                        .rearrange("c p f -> p c f"),
                    in_=xnT_st[st_ % 4].ap().rearrange("p (c f) -> p c f", f=128)),
                    inc=s_xw2[st_ % 4], k=16)
            if ch >= 1:
                pass


        ln_done_xw = tuple(s.n for s in s_xw2)

        if stage == "ln":
            for i_ in range(4):
                P.wait("gpsimd", s_xw2[i_], ln_done_xw[i_])
            for ecx in range(16):
                P.op("gpsimd", lambda g, ecx=ecx: g.dma_start(
                    out=dbg["xnf_out"][ecx], in_=xnf[ecx]), inc=s_g, k=16)
            P.wait("gpsimd", s_g)
            P.op("gpsimd", lambda g: g.dma_start(out=out[0], in_=xo[0]), inc=s_g, k=16)
            P.wait("gpsimd", s_g)
            P.replay()
            return nc, dbg

        s_xq = [P.sem("s_xq0"), P.sem("s_xq1")]
        ps5_bf = ps[5].ap().bitcast(BF16)
        ps6_bf = ps[6].ap().bitcast(BF16)
        xq_rel = {}
        for i_ in range(4):
            P.wait("sync", s_xw2[i_], ln_done_xw[i_])

        exp_mark = {0: 0, 1: 0}
        pslot_rel = {}
        s_pst = [P.sem(f"s_pst{u}") for u in range(4)]
        s_wl = P.sem("s_wl")
        for pair in range(4):
            b_, hl = divmod(pair, 2)
            tb = b_ * T
            for j in range(8):
                slot = j % 2
                use = pair * 8 + j
                if use >= 2:
                    P.wait("sync", s_pe, xq_rel[use - 2])
                for g4 in range(4):
                    P.op("sync", lambda e, g4=g4, slot=slot, tb=tb, j=j: e.dma_start(
                        out=xq_buf[slot].ap().rearrange("p (c f) -> p c f", c=EC)
                            [:, 4 * g4:4 * g4 + 4, :],
                        in_=xnf[4 * g4:4 * g4 + 4, :, tb + j * 512: tb + j * 512 + 512]
                            .rearrange("c p f -> p c f")),
                        inc=s_xq[slot], k=16)
                P.wait("tensor", s_xq[slot])
                xq = xq_buf[slot]
                pq, pk_, pr = (4, 6, 5) if j % 2 == 0 else (2, 3, 1)
                qrw = qraw_sb.ap()[:, 0:512]
                krw = qraw_sb.ap()[:, 512:1024]
                wait_bank("tensor", pq)
                for ec in range(EC):
                    P.op("tensor", lambda t, ec=ec, xq=xq, hl=hl, pq=pq: t.matmul(
                        ps[pq][:], wq_sb.ap()[:, ec * 256 + hl * 128: ec * 256 + hl * 128 + 128],
                        xq.ap()[:, ec * 512:(ec + 1) * 512],
                        start=(ec == 0), stop=(ec == EC - 1)),
                        inc=(s_pe if ec == EC - 1 else None))
                q_done = s_pe.n
                wait_bank("tensor", pk_)
                for ec in range(EC):
                    P.op("tensor", lambda t, ec=ec, xq=xq, hl=hl, pk_=pk_: t.matmul(
                        ps[pk_][:], wk_sb.ap()[:, ec * 256 + hl * 128: ec * 256 + hl * 128 + 128],
                        xq.ap()[:, ec * 512:(ec + 1) * 512],
                        start=(ec == 0), stop=(ec == EC - 1)),
                        inc=(s_pe if ec == EC - 1 else None))
                k_done = s_pe.n
                P.wait("scalar", s_pe, q_done)
                P.op("scalar", lambda s, hl=hl, pq=pq, qrw=qrw: s.add(
                    qrw, ps[pq][:], bqk_sb.ap()[:, hl:hl + 1]), inc=s_s)
                q_evac = s_s.n
                set_bank(pq, s_s)
                P.wait("scalar", s_pe, k_done)
                P.op("scalar", lambda s, hl=hl, pk_=pk_, krw=krw: s.add(
                    krw, ps[pk_][:], bqk_sb.ap()[:, 2 + hl:3 + hl]), inc=s_s)
                k_evac = s_s.n
                set_bank(pk_, s_s)
                wait_bank("tensor", pr)
                P.wait("tensor", s_s, q_evac)
                P.op("tensor", lambda t, hl=hl, pr=pr, qrw=qrw: t.matmul(
                    ps[pr][:], rope_sb.ap()[:, hl * 128:(hl + 1) * 128], qrw,
                    start=True, stop=True), inc=s_pe)
                P.wait("scalar", s_pe)
                P.op("scalar", lambda s, j=j, pr=pr: s.copy(
                    q_fm.ap()[:, j * 512:(j + 1) * 512], ps[pr][:]), inc=s_s)
                set_bank(pr, s_s)
                wait_bank("tensor", pq)
                for ts_ in range(4):
                    for ec in range(EC):
                        P.op("tensor", lambda t, ec=ec, ts_=ts_, xq=xq, hl=hl, pq=pq: t.matmul(
                            ps[pq][:, ts_ * 128:(ts_ + 1) * 128],
                            xq.ap()[:, ec * 512 + ts_ * 128: ec * 512 + ts_ * 128 + 128],
                            wv_sb.ap()[:, ec * 256 + hl * 128: ec * 256 + hl * 128 + 128],
                            start=(ec == 0), stop=False))
                    P.op("tensor", lambda t, ts_=ts_, hl=hl, pq=pq: t.matmul(
                        ps[pq][:, ts_ * 128:(ts_ + 1) * 128], ones_sb[:],
                        bvb_sb.ap()[:, hl * 128:(hl + 1) * 128],
                        start=False, stop=True), inc=(s_pe if ts_ == 3 else None))
                xq_rel[use] = s_pe.n
                v_done = s_pe.n
                wait_bank("tensor", pr)
                P.wait("tensor", s_s, k_evac)
                P.op("tensor", lambda t, hl=hl, pr=pr, krw=krw: t.matmul(
                    ps[pr][:], rope_sb.ap()[:, hl * 128:(hl + 1) * 128], krw,
                    start=True, stop=True), inc=s_pe)
                rk_done = s_pe.n
                P.wait("scalar", s_pe, rk_done)
                P.op("scalar", lambda s, j=j, pr=pr: s.copy(
                    k_fm.ap()[:, j * 512:(j + 1) * 512], ps[pr][:]), inc=s_s)
                set_bank(pr, s_s)
                P.wait("scalar", s_pe, v_done)
                P.op("scalar", lambda s, j=j, pq=pq: s.copy(
                    v_tm.ap()[:, j * 512:(j + 1) * 512], ps[pq][:]), inc=s_s)
                set_bank(pq, s_s)
                pump_deq(1)
            if pair == 3:
                P.wait("sync", s_pe, s_pe.n)
                P.wait("sync", s_wst[0], wst_rel[6])
                P.wait("sync", s_wst[1], wst_rel[7])
                P.op("sync", lambda e: e.dma_start(
                    out=wgT_sb.ap()[:, 0:12 * 1024]
                        .rearrange("p (c f) -> p c f", c=12),
                    in_=wgT_d[0, 0:12].rearrange("c p f -> p c f")),
                    inc=s_wl, k=16)
            P.wait("tensor", s_s)
            for r8 in range(8):
                pd_ = 4 if r8 % 2 == 0 else 2
                wait_bank("tensor", pd_)
                for m in range(4):
                    qt = 4 * r8 + m
                    P.op("tensor", lambda t, m=m, qt=qt, pd_=pd_: t.matmul(
                        ps[pd_][:, m * 128:(m + 1) * 128],
                        q_fm.ap()[:, qt * 128:(qt + 1) * 128],
                        k_fm.ap()[:, qt * 128:(qt + 1) * 128],
                        start=True, stop=True), inc=(s_pe if m == 3 else None))
                P.wait("vector", s_pe)
                P.op("vector", lambda v, r8=r8, pd_=pd_: v.reduce_max(
                    negmax_sb[:, r8 * 4:(r8 + 1) * 4],
                    ps[pd_].ap().rearrange("p (m f) -> p m f", m=4),
                    axis=AX, negate=True), inc=s_v)
                set_bank(pd_, s_v)
            nmx_done = s_v.n
            for r8 in range(8):
                wait_bank("tensor", 5)
                if r8 == 0:
                    P.wait("tensor", s_v, nmx_done)
                for m in range(4):
                    P.op("tensor", lambda t, r8=r8, m=m: t.transpose(
                        ps5_bf[0:1, m * 128:(m + 1) * 128],
                        negmax_sb[:, r8 * 4 + m: r8 * 4 + m + 1], eye_sb[:]),
                        inc=(s_pe if m == 3 else None))
                P.wait("vector", s_pe)
                P.op("vector", lambda v: v.tensor_copy(negc_sb[:], ps5_bf[0:1, 0:512]),
                     inc=s_v)
                ngc = s_v.n
                set_bank(5, s_v)
                wait_bank("tensor", 1)
                P.wait("tensor", s_v, ngc)
                P.op("tensor", lambda t: t.matmul(ps[1][:], ones1_sb[:], negc_sb[:],
                                                  start=True, stop=True), inc=s_pe)
                P.wait("vector", s_pe)
                P.op("vector", lambda v, r8=r8: v.tensor_copy(
                    cb_all.ap()[:, r8 * 512:(r8 + 1) * 512], ps[1][:]), inc=s_v)
                set_bank(1, s_v)
            cb_done = s_v.n

            def emit_cbm(C2):
                par3 = C2 % 2
                P.wait("gpsimd", s_v, cb_done)
                if C2 >= 2:
                    P.wait("gpsimd", s_pe, cbm_used[C2 - 2])
                P.drain("gpsimd")
                for m in range(4):
                    P.op("gpsimd", lambda v, C2=C2, m=m, par3=par3: v.tensor_add(
                        cbm_sb[par3].ap()[:, m * 512:(m + 1) * 512],
                        cb_all.ap()[:, C2 * 512:(C2 + 1) * 512],
                        mask_sb.ap()[:, m * 512:(m + 1) * 512]),
                        inc=(s_p if m == 3 else None))
                cbm_mark[C2] = s_p.n

            cbm_mark = {}
            cbm_used = {}
            emit_cbm(0)
            wait_bank("tensor", 0)
            wait_bank("tensor", 1)
            for C_ in range(8):
                nst = 4 * C_ + 4
                par3 = C_ % 2
                po_, psm_ = (2, 3) if C_ % 2 == 0 else (4, 5)
                P.wait("tensor", s_p, cbm_mark[C_])
                wait_bank("tensor", po_)
                wait_bank("tensor", psm_)
                def emit_scores(j):
                    par = j % 2
                    diag = j >= 4 * C_
                    if exp_mark[par]:
                        P.wait("tensor", s_s, exp_mark[par])
                    P.op("tensor", lambda t, j=j, par=par, C_=C_: t.matmul(
                        ps[par][:], k_fm.ap()[:, j * 128:(j + 1) * 128],
                        q_fm.ap()[:, C_ * 512:(C_ + 1) * 512],
                        start=True, stop=False))
                    par3_ = C_ % 2
                    if diag:
                        m = j - 4 * C_
                        P.op("tensor", lambda t, par=par, m=m, par3_=par3_: t.matmul(
                            ps[par][:], eye_sb[:], cbm_sb[par3_].ap()[:, m * 512:(m + 1) * 512],
                            start=False, stop=True), inc=s_pe)
                    else:
                        P.op("tensor", lambda t, par=par, C_=C_: t.matmul(
                            ps[par][:], eye_sb[:],
                            cb_all.ap()[:, C_ * 512:(C_ + 1) * 512],
                            start=False, stop=True), inc=s_pe)
                    fold_mark[j] = s_pe.n

                def emit_exp(j):
                    par = j % 2
                    P.wait("scalar", s_pe, fold_mark[j])
                    if j >= 4:
                        P.wait("scalar", s_pe, pslot_rel[j - 4])
                    P.op("scalar", lambda s, j=j, par=par: s.activation(
                        p_slot.ap()[:, (j % 4) * 512:((j % 4) + 1) * 512],
                        ps[par][:], AF.Exp), inc=s_s)
                    exp_mark[par] = s_s.n

                def emit_av(j):
                    P.wait("tensor", s_s, exp_mark[j % 2])
                    P.op("tensor", lambda t, j=j, nst=nst, po_=po_: t.matmul(
                        ps[po_][:], v_tm.ap()[:, j * 128:(j + 1) * 128],
                        p_slot.ap()[:, (j % 4) * 512:((j % 4) + 1) * 512],
                        start=(j == 0), stop=(j == nst - 1)))
                    P.op("tensor", lambda t, j=j, nst=nst, psm_=psm_: t.matmul(
                        ps[psm_][:], ones_sb[:],
                        p_slot.ap()[:, (j % 4) * 512:((j % 4) + 1) * 512],
                        start=(j == 0), stop=(j == nst - 1)), inc=s_pe)
                    pslot_rel[j] = s_pe.n

                fold_mark = {}
                emit_scores(0)
                emit_exp(0)
                for j in range(nst):
                    if j + 1 < nst:
                        emit_scores(j + 1)
                        emit_exp(j + 1)
                    emit_av(j)
                    if j == 1 and C_ < 7:
                        emit_cbm(C_ + 1)
                cbm_used[C_] = s_pe.n
                P.wait("vector", s_pe)
                P.op("vector", lambda v, psm_=psm_: v.reciprocal(recip_sb[:], ps[psm_][:]))
                P.drain("vector")
                P.op("vector", lambda v, hl=hl, C_=C_, po_=po_: v.scalar_tensor_tensor(
                    o_fm[hl].ap()[:, C_ * 512:(C_ + 1) * 512], ps[po_][:],
                    INV_SQRT_E, recip_sb[:], ALU.mult, ALU.mult), inc=s_v)
                set_bank(po_, s_v)
                set_bank(psm_, s_v)
                pump_deq(1)
            if hl == 1:
                for tt_ in range(32):
                    for u in range(4):
                        pb = [4, 5, 6, 7][u]
                        wait_bank("tensor", pb)
                        P.op("tensor", lambda t, tt_=tt_, u=u, pb=pb: t.matmul(
                            ps[pb][:], o_fm[0].ap()[:, tt_ * 128:(tt_ + 1) * 128],
                            wp_sb.ap()[:, u * 512:u * 512 + 512],
                            start=True, stop=False))
                        P.op("tensor", lambda t, tt_=tt_, u=u, pb=pb: t.matmul(
                            ps[pb][:], o_fm[1].ap()[:, tt_ * 128:(tt_ + 1) * 128],
                            wp_sb.ap()[:, E + u * 512: E + u * 512 + 512],
                            start=False, stop=True), inc=s_pe)
                        P.wait("vector", s_pe)
                        P.wait("vector", s_pst[u])
                        P.op("vector", lambda v, u=u, pb=pb: v.tensor_copy(
                            pstage.ap()[:, u * 512:(u + 1) * 512], ps[pb][:]), inc=s_v)
                        set_bank(pb, s_v)
                        P.wait("gpsimd", s_v)
                        P.op("gpsimd", lambda g, tt_=tt_, u=u, tb=tb: g.dma_start(
                            out=contrib[tb + tt_ * 128: tb + tt_ * 128 + 128,
                                        u * 512:(u + 1) * 512],
                            in_=pstage.ap()[:, u * 512:(u + 1) * 512]), inc=s_pst[u], k=16)

        attn_done_g = s_g.n
        attn_done_pst = [s.n for s in s_pst]

        pump_deq_all2()
        deq_done = (s_wst[0].n, s_wst[1].n)

        if stage == "deq":
            P.wait("gpsimd", s_wst[0], deq_done[0])
            P.wait("gpsimd", s_wst[1], deq_done[1])
            P.op("gpsimd", lambda g: g.dma_start(out=dbg["wgT_out"], in_=wgT_d),
                 inc=s_g, k=16)
            P.op("gpsimd", lambda g: g.dma_start(out=dbg["wdT_out"], in_=wdT_d),
                 inc=s_g, k=16)
            P.wait("gpsimd", s_g)
            P.op("gpsimd", lambda g: g.dma_start(out=out[0], in_=xo[0]), inc=s_g, k=16)
            P.wait("gpsimd", s_g)
            P.replay()
            return nc, dbg


        if stage == "attn":
            P.wait("gpsimd", s_g, attn_done_g)
            P.wait("gpsimd", s_v)
            P.op("gpsimd", lambda g: g.dma_start(out=dbg["contrib_out"], in_=contrib),
                 inc=s_g, k=16)
            P.op("gpsimd", lambda g: g.dma_start(out=dbg["xq_out"], in_=xq_buf[0][:]),
                 inc=s_g, k=16)
            for nm, tsr in (("q_fm_out", q_fm), ("k_fm_out", k_fm), ("v_tm_out", v_tm),
                            ("o0_out", o_fm[0]), ("o1_out", o_fm[1])):
                P.op("gpsimd", lambda g, nm=nm, tsr=tsr: g.dma_start(
                    out=dbg[nm], in_=tsr[:]), inc=s_g, k=16)
            P.wait("gpsimd", s_g)
            P.op("gpsimd", lambda g: g.dma_start(out=out[0], in_=xo[0]), inc=s_g, k=16)
            P.wait("gpsimd", s_g)
            P.replay()
            return nc, dbg

        s_xf = [P.sem("s_xfA"), P.sem("s_xfB")]
        s_fin = P.sem("s_fin")
        s_out = P.sem("s_out")
        xf_rel = {}

        s_dst = [P.sem("s_dstA"), P.sem("s_dstB")]

        def emit_final(fc):
            P.wait("sync", s_cc, fc + 1)
            if fc > 0:
                P.wait("sync", s_out, 16 * fc)
            P.op("sync", lambda e, fc=fc: e.dma_start(
                out=fin_sb[:], in_=red[fc].rearrange("p (a c) -> (p a) c", a=4)),
                inc=s_fin, k=16)
            P.op("sync", lambda e, fc=fc: e.dma_start(
                out=xo_sb[:], in_=xo[fc]), inc=s_fin, k=16)
            P.wait("gpsimd", s_fin)
            P.drain("gpsimd")
            P.op("gpsimd", lambda g: g.tensor_add(fin_sb[:], fin_sb[:], xo_sb[:]))
            P.drain("gpsimd")
            P.op("gpsimd", lambda g: g.tensor_add(fin_sb[:], fin_sb[:],
                                                  bpb_sb.ap()[0:64, :]))
            P.drain("gpsimd")
            P.op("gpsimd", lambda g, fc=fc: g.dma_start(
                out=out[fc], in_=fin_sb[:]), inc=s_out, k=16)
        for hp in range(2):
            if hp == 0:
                P.wait("gpsimd", s_g, attn_done_g)
                for u in range(4):
                    P.wait("gpsimd", s_pst[u], attn_done_pst[u])
            else:
                P.wait("gpsimd", s_dst[0])
                P.wait("gpsimd", s_dst[1])
            for qn, (wsb_, wd_) in zip(("sync", "scalar", "gpsimd"),
                                       ((wgT_sb, wgT_d), (wuT_sb, wuT_d), (wdT_sb, wdT_d))):
                if hp == 0:
                    P.wait(qn, s_wst[0], deq_done[0])
                    P.wait(qn, s_wst[1], deq_done[1])
                    P.wait(qn, s_pe)
                    P.wait(qn, s_v)
                    P.wait(qn, s_s)
                elif wsb_ is wdT_sb:
                    P.wait(qn, s_pe)
                else:
                    P.wait(qn, s_pe, gateup_dead)
                if hp == 0 and wsb_ is wgT_sb:
                    P.op(qn, lambda e: e.dma_start(
                        out=wgT_sb.ap()[:, 12 * 1024:16 * 1024]
                            .rearrange("p (c f) -> p c f", c=4),
                        in_=wgT_d[0, 12:16].rearrange("c p f -> p c f")),
                        inc=s_wl, k=16)
                else:
                    P.op(qn, lambda e, wsb_=wsb_, wd_=wd_, hp=hp: e.dma_start(
                        out=wsb_.ap().rearrange("p (c f) -> p c f",
                                                c=wd_.shape[1]),
                        in_=wd_[hp].rearrange("c p f -> p c f")), inc=s_wl, k=16)
            wl_mark = s_wl.n
            if hp == 0:
                down_done = 0
            for tc in range(16):
                slot = tc % 2
                use = hp * 16 + tc
                if use >= 2:
                    P.wait("sync", s_pe, xf_rel[use - 2])
                for g4 in range(4):
                    P.op("sync", lambda e, g4=g4, slot=slot, tc=tc: e.dma_start(
                        out=xf_buf[slot].ap().rearrange("p (c f) -> p c f", c=EC)
                            [:, 4 * g4:4 * g4 + 4, :],
                        in_=xnf[4 * g4:4 * g4 + 4, :, tc * 512: tc * 512 + 512]
                            .rearrange("c p f -> p c f")), inc=s_xf[slot], k=16)
                P.wait("tensor", s_xf[slot])
                if tc == 0:
                    P.wait("tensor", s_wl, wl_mark)
                xf = xf_buf[slot]
                for nch in range(8):
                    pg, pu = (0, 1) if nch % 2 == 0 else (4, 5)
                    wait_bank("tensor", pg)
                    wait_bank("tensor", pu)
                    for ec in range(EC):
                        P.op("tensor", lambda t, ec=ec, nch=nch, xf=xf, pg=pg: t.matmul(
                            ps[pg][:],
                            wgT_sb.ap()[:, ec * 1024 + nch * 128: ec * 1024 + nch * 128 + 128],
                            xf.ap()[:, ec * 512:(ec + 1) * 512],
                            start=(ec == 0), stop=(ec == EC - 1)),
                            inc=(s_pe if ec == EC - 1 else None))
                    for ec in range(EC):
                        P.op("tensor", lambda t, ec=ec, nch=nch, xf=xf, pu=pu: t.matmul(
                            ps[pu][:],
                            wuT_sb.ap()[:, ec * 1024 + nch * 128: ec * 1024 + nch * 128 + 128],
                            xf.ap()[:, ec * 512:(ec + 1) * 512],
                            start=(ec == 0), stop=(ec == EC - 1)),
                            inc=(s_pe if ec == EC - 1 else None))
                    par2 = nch % 2
                    up_mark = s_pe.n
                    if hp == 0 and tc == 15 and nch == 7:
                        gateup_dead = s_pe.n
                    P.wait("scalar", s_pe, s_pe.n - 1)
                    P.op("scalar", lambda s, pg=pg, par2=par2: s.activation(
                        g_sq[par2][:], ps[pg][:], AF.Square), inc=s_s)
                    P.wait("vector", s_s)
                    P.op("vector", lambda v, par2=par2: v.tensor_scalar(
                        g_d[par2][:], g_sq[par2][:], GELU_A, 1.0, ALU.mult, ALU.add))
                    P.drain("vector")
                    P.op("vector", lambda v, pg=pg, par2=par2: v.tensor_mul(
                        g_e[par2][:], g_d[par2][:], ps[pg][:]), inc=s_v)
                    P.wait("scalar", s_v)
                    P.op("scalar", lambda s, par2=par2: s.activation(
                        g_sg[par2][:], g_e[par2][:], AF.Sigmoid, scale=GELU_S), inc=s_s)
                    P.wait("vector", s_s)
                    P.op("vector", lambda v, pg=pg, par2=par2: v.tensor_mul(
                        g_gs[par2][:], ps[pg][:], g_sg[par2][:]))
                    P.drain("vector")
                    P.wait("vector", s_pe, up_mark)
                    if nch == 0:
                        P.wait("vector", s_pe, down_done)
                    P.op("vector", lambda v, pu=pu, nch=nch, par2=par2: v.tensor_mul(
                        inter_sb.ap()[:, nch * 512:(nch + 1) * 512],
                        g_gs[par2][:], ps[pu][:]), inc=s_v)
                    set_bank(pg, s_v)
                    set_bank(pu, s_v)
                inter_ready = s_v.n
                xf_rel[use] = s_pe.n
                P.wait("tensor", s_v, inter_ready)
                for ts_ in range(4):
                    slot2 = (use * 4 + ts_) % 2
                    P.wait("vector", s_dst[slot2])
                    for hc in range(4):
                        pd = [2, 3, 6, 7][hc]
                        wait_bank("tensor", pd)
                        for nch in range(8):
                            P.op("tensor", lambda t, nch=nch, ts_=ts_, hc=hc, pd=pd: t.matmul(
                                ps[pd][:],
                                inter_sb.ap()[:, nch * 512 + ts_ * 128: nch * 512 + ts_ * 128 + 128],
                                wdT_sb.ap()[:, nch * 2048 + hc * 512: nch * 2048 + hc * 512 + 512],
                                start=(nch == 0), stop=(nch == 7)),
                                inc=(s_pe if nch == 7 else None))
                        P.wait("vector", s_pe)
                        P.op("vector", lambda v, hc=hc, pd=pd, slot2=slot2: v.tensor_copy(
                            dstage[slot2].ap()[:, hc * 512:(hc + 1) * 512], ps[pd][:]),
                            inc=s_v)
                        set_bank(pd, s_v)
                    P.wait("gpsimd", s_v)
                    P.op("gpsimd", lambda g, tc=tc, ts_=ts_, slot2=slot2: g.dma_start(
                        out=contrib[tc * 512 + ts_ * 128: tc * 512 + ts_ * 128 + 128, :],
                        in_=dstage[slot2][:], accum_op=ALU.add), inc=s_dst[slot2], k=16)
                down_done = s_pe.n
                if hp == 1 and stage == "full":
                    P.wait("gpsimd", s_dst[0])
                    P.wait("gpsimd", s_dst[1])
                    P.op("gpsimd", lambda g, tc=tc: g.collective_compute(
                        "ReduceScatter", ALU.add,
                        replica_groups=[list(range(8))],
                        ins=[contrib[tc * 512:(tc + 1) * 512, :]
                             .rearrange("(p a) c -> p (a c)", p=128).opt()],
                        outs=[red[tc].opt()]), inc=s_cc, k=1)
                    if tc >= 1:
                        emit_final(tc - 1)

        if stage == "full":
            emit_final(15)

        if stage == "ffn":
            P.wait("gpsimd", s_g)
            P.op("gpsimd", lambda g: g.dma_start(out=dbg["contrib_out"], in_=contrib),
                 inc=s_g, k=16)
            P.wait("gpsimd", s_g)
            P.op("gpsimd", lambda g: g.dma_start(out=out[0], in_=xo[0]), inc=s_g, k=16)
            P.wait("gpsimd", s_g)
        else:
            P.wait("gpsimd", s_out, 16 * 16)

        P.replay()
    return nc, dbg



def make_rope_R(c):
    hd = 128
    freqs = 10000.0 ** (-(np.arange(0, hd, 2, dtype=np.float64) / hd))
    Rs = np.zeros((2, hd, hd), dtype=np.float32)
    for i, h in enumerate((2 * c, 2 * c + 1)):
        ang = h * freqs
        co, si = np.cos(ang), np.sin(ang)
        for j in range(hd // 2):
            Rs[i, 2 * j, 2 * j] = co[j]
            Rs[i, 2 * j + 1, 2 * j] = -si[j]
            Rs[i, 2 * j, 2 * j + 1] = si[j]
            Rs[i, 2 * j + 1, 2 * j + 1] = co[j]
    return Rs


def make_in_maps(L0_attn_x, L0_attn_w_attn, L0_attn_b_attn, L0_attn_w_proj,
                 L0_attn_b_proj, L0_ffn_gate_W_packed, L0_ffn_gate_scales,
                 L0_ffn_up_W_packed, L0_ffn_up_scales, L0_ffn_down_W_packed,
                 L0_ffn_down_scales):
    x = np.ascontiguousarray(np.asarray(L0_attn_x, dtype=np.float32)).reshape(TT, E)
    wa = np.asarray(L0_attn_w_attn, dtype=np.float32)
    ba = np.asarray(L0_attn_b_attn, dtype=np.float32)
    wp = np.asarray(L0_attn_w_proj, dtype=np.float32)
    bp = np.asarray(L0_attn_b_proj, dtype=np.float32)
    gpk = np.asarray(L0_ffn_gate_W_packed, dtype=np.int32)
    gsc = np.asarray(L0_ffn_gate_scales, dtype=np.float32)
    upk = np.asarray(L0_ffn_up_W_packed, dtype=np.int32)
    usc = np.asarray(L0_ffn_up_scales, dtype=np.float32)
    dpk = np.asarray(L0_ffn_down_W_packed, dtype=np.int32)
    dsc = np.asarray(L0_ffn_down_scales, dtype=np.float32)
    x16 = x.reshape(16, 8, 64, E)
    bpb = np.ascontiguousarray(np.broadcast_to(bp[None, :], (128, E)))
    in_maps = []
    for c in range(8):
        h0 = 2 * c * 128
        bqk = np.stack([ba[h0:h0 + 128], ba[h0 + 128:h0 + 256],
                        ba[E + h0:E + h0 + 128], ba[E + h0 + 128:E + h0 + 256]],
                       axis=1)
        bv = ba[2 * E + h0:2 * E + h0 + 256]
        in_maps.append(dict(
            xg=x,
            xo=np.ascontiguousarray(x16[:, c]),
            wq_sl=np.ascontiguousarray(wa[:, h0:h0 + 256]),
            wk_sl=np.ascontiguousarray(wa[:, E + h0:E + h0 + 256]),
            wv_sl=np.ascontiguousarray(wa[:, 2 * E + h0:2 * E + h0 + 256]),
            wp_sl=np.ascontiguousarray(wp[2 * c * 128:2 * c * 128 + 256, :]),
            bqk=np.ascontiguousarray(bqk),
            bvb=np.ascontiguousarray(np.broadcast_to(bv[None, :], (128, 256))),
            bpb=bpb,
            rope_R=make_rope_R(c),
            gate_pk=np.ascontiguousarray(gpk[NS * c:NS * (c + 1)]),
            gate_sc=np.ascontiguousarray(gsc[NS * c:NS * (c + 1)]),
            up_pk=np.ascontiguousarray(upk[NS * c:NS * (c + 1)]),
            up_sc=np.ascontiguousarray(usc[NS * c:NS * (c + 1)]),
            down_pk=np.ascontiguousarray(dpk[:, 1024 * c:1024 * (c + 1)]),
            down_sc=np.ascontiguousarray(dsc[:, 64 * c:64 * (c + 1)]),
        ))
    return in_maps


def assemble_output(results):
    full = np.empty((16, 8, 64, E), dtype=np.float32)
    for c in range(8):
        full[:, c] = np.asarray(results[c]["out_rows"])
    return full.reshape(B, T, E)


def kernel(**inputs):
    nc, _dbg = build_program("full")
    in_maps = make_in_maps(**inputs)
    res = run_bass_kernel_spmd(nc, in_maps, core_ids=list(range(8)))
    return assemble_output(res.results)
